# version 07
# baseline (speedup 1.0000x reference)
import os
import numpy as np
import ml_dtypes
from contextlib import ExitStack

import concourse.bass as bass
import concourse.mybir as mybir
from concourse.bass_utils import run_bass_kernel_spmd

F32 = mybir.dt.float32
BF16 = mybir.dt.bfloat16
F16 = mybir.dt.float16
I32 = mybir.dt.int32
AF = mybir.ActivationFunctionType
ALU = mybir.AluOpType
BF = ml_dtypes.bfloat16
AX = mybir.AxisListType

ENGINES = ("sync", "gpsimd", "tensor", "vector", "scalar")


def cdiv(a, b):
    return -(-a // b)


_UNIQ = [0]


def uname(n):
    _UNIQ[0] += 1
    return f"{n}_{_UNIQ[0]}"


class Ch:
    def __init__(self, nc, st, name):
        self.sem = st.enter_context(nc.semaphore(name))
        self.n = 0
        self.name = name


class Node:

    __slots__ = ("writers", "readers", "name", "_sem")

    def __init__(self, name=""):
        self.writers = {}
        self.readers = {}
        self.name = name
        self._sem = None

    def deps_for_read(self):
        return list(self.writers.items())

    def deps_for_write(self):
        return list(self.writers.items()) + list(self.readers.items())


def nodes(prefix, n):
    return [Node(f"{prefix}{i}") for i in range(n)]


class Prog:

    ENG_CH = {"sync": "cs", "gpsimd": "cg", "tensor": "cte", "vector": "cve", "scalar": "cca"}

    def __init__(self, nc, st):
        self.nc = nc
        self.st = st
        self.q = {e: [] for e in ENGINES}
        self.wm = {e: {} for e in ENGINES}
        self.ch = {e: Ch(nc, st, self.ENG_CH[e]) for e in ENGINES}

    def _waits(self, eng, waits):
        out = []
        wm = self.wm[eng]
        for ch, v in waits:
            if v > 0 and wm.get(ch, 0) < v:
                wm[ch] = v
                out.append((ch.sem, v))
        return out

    def op(self, eng, fn, reads=(), writes=(), waits=(), sigk=1):
        ch = self.ch[eng]
        allw = list(waits)
        for n in reads:
            allw += n.deps_for_read()
        for n in writes:
            allw += n.deps_for_write()
        ws = self._waits(eng, allw)
        ch.n += sigk
        val = ch.n

        def run(e):
            for sem, v in ws:
                e.wait_ge(sem, v)
            inst = fn(e)
            assert inst is not None, "op returned no instruction"
            inst.then_inc(ch.sem, sigk)

        self.q[eng].append(run)
        ev = (ch, val)
        for n in reads:
            n.readers[ch] = val
        for n in writes:
            n.writers[ch] = val
        return ev

    def node_ch(self, node):
        if not hasattr(node, "_sem") or node._sem is None:
            node._sem = Ch(self.nc, self.st, f"nd_{node.name}_{id(node) % 9999}")
        return node._sem

    def dma(self, eng, out, in_, reads=(), writes=(), waits=(), store_node=None, **kw):
        sig = self.node_ch(writes[0] if writes else store_node)
        allw = list(waits)
        for n in reads:
            allw += n.deps_for_read()
        for n in writes:
            allw += n.deps_for_write()
        ws = self._waits(eng, allw)
        sig.n += 16
        val = sig.n

        def run(e):
            for sem, v in ws:
                e.wait_ge(sem, v)
            e.dma_start(out=out, in_=in_, **kw).then_inc(sig.sem, 16)

        self.q[eng].append(run)
        ev = (sig, val)
        for n in reads:
            n.readers[sig] = val
        for n in writes:
            n.writers[sig] = val
        return ev

    def barrier_all(self):
        evs = [(c, c.n) for c in self.ch.values()]
        for eng in ENGINES:
            ws = self._waits(eng, evs)

            def run(e, ws=ws):
                for sem, v in ws:
                    e.wait_ge(sem, v)
            self.q[eng].append(run)

    def fence(self, eng, evs):
        evs = [e for e in evs if e]
        ws = self._waits(eng, evs)
        own = [(c, v) for c, v in evs if c is self.ch[eng]]
        mx = max([v for _, v in own], default=None)

        def run(e):
            for sem, v in ws:
                e.wait_ge(sem, v)
            if mx is not None:
                e.wait_ge(self.ch[eng].sem, mx)

        self.q[eng].append(run)

    def finish(self):
        with self.nc.Block() as block:
            for eng in ENGINES:
                ops = self.q[eng]
                getattr(block, eng)(lambda e, ops=ops: [f(e) for f in ops])



GUP_SH = 1408
DWN_SH = 256
G_SH = 688
G_SH2 = 768
GD_SH = 256


def emit_dequant(p, nc, st, gup_p, gup_s, dwn_p, dwn_s, gidx, uidx, didx, lut, o_gup, o_dwn, o_g, o_u, o_d):
    CB = 2048
    sbuf = lambda name, shape, dt: st.enter_context(nc.sbuf_tensor(uname(name), shape, dt))
    pk = [sbuf(f"pk{i}", [128, CB // 8], I32) for i in range(2)]
    scl = [sbuf(f"scl{i}", [128, CB // 128], F32) for i in range(2)]
    nib = sbuf("nib", [128, CB], I32)
    t_m = sbuf("t_m", [128, CB], I32)
    t_e = sbuf("t_e", [128, CB], I32)
    t_f = sbuf("t_f", [128, CB], F32)
    t_g = sbuf("t_g", [128, CB], F32)
    val = sbuf("val", [128, CB], F32)
    obuf = [sbuf(f"obuf{i}", [128, CB], BF16) for i in range(2)]
    idxb = [sbuf(f"idxb{i}", [128, CB], I32) for i in range(2)]
    idxf = sbuf("idxf", [128, CB], BF16)
    gtmp = sbuf("gtmp", [128, CB], BF16)
    gacc = [sbuf(f"gacc{i}", [128, CB], BF16) for i in range(2)]
    lutr = sbuf("lutr", [1, 16], F32)
    lutb = sbuf("lutb", [128, 16], F32)
    ones1 = sbuf("ones1", [1, 128], F32)
    idbs = sbuf("idbs", [128, 128], BF16)
    tbuf = [sbuf(f"tbuf{i}", [128, CB], BF16) for i in range(2)]
    pslut = st.enter_context(nc.psum_tensor(uname("pslut"), [128, 16], F32))
    ps_tr = st.enter_context(nc.psum_tensor(uname("ps_tr"), [128, 1024], BF16))
    idd = nc.inline_tensor(np.eye(128, dtype=np.float32).astype(BF), name=uname("idd")).ap()

    n_pk = nodes("pk", 2)
    n_scl = nodes("scl", 2)
    n_obuf = nodes("obuf", 2)
    n_idxb = nodes("idxb", 2)
    n_gacc = nodes("gacc", 2)
    n_lutr, n_ones, n_pslut, n_lutb = Node("lutr"), Node("ones"), Node("pslut"), Node("lutb")
    n_id, n_pstr = Node("id"), Node("pstr")
    n_tbuf = nodes("tbuf", 2)
    n_store = Node("stores")
    stores = []
    p.dma("gpsimd", idbs[:], idd, writes=[n_id])

    p.dma("sync", lutr[:], lut, writes=[n_lutr])
    p.op("vector", lambda e: e.memset(ones1[:], 1.0), writes=[n_ones])
    p.op("tensor", lambda e: e.matmul(pslut[:], ones1[:], lutr[:], start=True, stop=True),
         reads=[n_lutr, n_ones], writes=[n_pslut])
    p.op("vector", lambda e: e.tensor_copy(out=lutb[:], in_=pslut[:]),
         reads=[n_pslut], writes=[n_lutb])

    ti = 0

    def tr_store(srcbuf, n_src, dstT, rows, cols, b):
        nblk = cols // 128
        for tb_ in range(nblk):
            p.op("tensor", lambda e, tb_=tb_, srcbuf=srcbuf, rows=rows: e.transpose(
                ps_tr[:, tb_ % 2 * 128: tb_ % 2 * 128 + rows],
                srcbuf[:rows, tb_ * 128:(tb_ + 1) * 128], idbs[:rows, :rows]),
                reads=[n_src, n_id], writes=[n_pstr])
            p.op("vector", lambda e, tb_=tb_, rows=rows, b=b: e.tensor_copy(
                out=tbuf[b][:, tb_ * rows:(tb_ + 1) * rows],
                in_=ps_tr[:, tb_ % 2 * 128: tb_ % 2 * 128 + rows]),
                reads=[n_pstr], writes=[n_tbuf[b]])
        stores.append(p.dma(
            "sync", dstT.rearrange("(nb q) r -> q nb r", q=128),
            tbuf[b][:, :nblk * rows].rearrange("q (nb r) -> q nb r", r=rows),
            reads=[n_tbuf[b]], store_node=n_store))

    def fp4_tile(dst, packed_src, scales_src, rows, wq, cols, sgroups):
        nonlocal ti
        b = ti % 2
        ti += 1
        p.dma("gpsimd", pk[b][:rows, :wq], packed_src, writes=[n_pk[b]])
        p.dma("gpsimd", scl[b][:rows, :sgroups], scales_src, writes=[n_scl[b]])

        def decode(e, b=b, rows=rows, wq=wq, cols=cols, sgroups=sgroups):
            for k in range(8):
                e.tensor_scalar(out=nib[:rows, k:cols:8], in0=pk[b][:rows, :wq],
                                scalar1=4 * k, scalar2=15,
                                op0=ALU.logical_shift_right, op1=ALU.bitwise_and)
            e.tensor_scalar(out=t_m[:rows, :cols], in0=nib[:rows, :cols], scalar1=1,
                            scalar2=None, op0=ALU.bitwise_and, op1=ALU.bypass)
            e.tensor_scalar(out=t_e[:rows, :cols], in0=nib[:rows, :cols], scalar1=7,
                            scalar2=1, op0=ALU.bitwise_and, op1=ALU.logical_shift_right)
            e.tensor_scalar(out=t_f[:rows, :cols].bitcast(I32), in0=t_e[:rows, :cols],
                            scalar1=126, scalar2=1 << 23, op0=ALU.add, op1=ALU.mult)
            e.tensor_copy(out=t_g[:rows, :cols], in_=t_m[:rows, :cols])
            e.tensor_scalar(out=t_g[:rows, :cols], in0=t_g[:rows, :cols], scalar1=0.5,
                            scalar2=1.0, op0=ALU.mult, op1=ALU.add)
            e.tensor_mul(val[:rows, :cols], t_f[:rows, :cols], t_g[:rows, :cols])
            e.tensor_scalar(out=t_g[:rows, :cols], in0=t_g[:rows, :cols], scalar1=1.0,
                            scalar2=None, op0=ALU.subtract, op1=ALU.bypass)
            e.tensor_scalar(out=t_m[:rows, :cols], in0=t_e[:rows, :cols], scalar1=0,
                            scalar2=None, op0=ALU.is_equal, op1=ALU.bypass)
            e.select(out=val[:rows, :cols], mask=t_m[:rows, :cols],
                     on_true=t_g[:rows, :cols], on_false=val[:rows, :cols])
            e.tensor_scalar(out=t_e[:rows, :cols], in0=nib[:rows, :cols], scalar1=3,
                            scalar2=None, op0=ALU.logical_shift_right, op1=ALU.bypass)
            e.tensor_copy(out=t_g[:rows, :cols], in_=t_e[:rows, :cols])
            e.tensor_scalar(out=t_g[:rows, :cols], in0=t_g[:rows, :cols], scalar1=-2.0,
                            scalar2=1.0, op0=ALU.mult, op1=ALU.add)
            e.tensor_mul(val[:rows, :cols], val[:rows, :cols], t_g[:rows, :cols])
            sclv = scl[b][:rows, :sgroups].unsqueeze(2).broadcast_to([rows, sgroups, 128])
            return e.tensor_tensor(
                out=obuf[b][:rows, :cols].rearrange("p (g q) -> p g q", q=128),
                in0=val[:rows, :cols].rearrange("p (g q) -> p g q", q=128),
                in1=sclv, op=ALU.mult)

        p.op("vector", decode, reads=[n_pk[b], n_scl[b]], writes=[n_obuf[b]])
        tr_store(obuf[b], n_obuf[b], dst, rows, cols, b)

    def fp4_mat(dstT, src_p, src_s, R, COLS):
        for i in range(cdiv(R, 128)):
            r0 = i * 128
            rows = min(128, R - r0)
            for c0 in range(0, COLS, CB):
                cols = min(CB, COLS - c0)
                fp4_tile(dstT[c0:c0 + cols, r0:r0 + rows],
                         src_p[r0:r0 + rows, c0 // 8:(c0 + cols) // 8],
                         src_s[r0:r0 + rows, c0 // 128:(c0 + cols) // 128],
                         rows, cols // 8, cols, cols // 128)

    fp4_mat(o_gup, gup_p, gup_s, GUP_SH, 2048)
    fp4_mat(o_dwn, dwn_p, dwn_s, DWN_SH, 5632)

    gi = 0

    def ghost_tile(dst, idx_src, rows, cols):
        nonlocal gi
        b = gi % 2
        gi += 1
        p.dma("gpsimd", idxb[b][:rows, :cols], idx_src, writes=[n_idxb[b]])

        def comp(e, b=b, rows=rows, cols=cols):
            e.tensor_copy(out=idxf[:rows, :cols], in_=idxb[b][:rows, :cols])
            e.memset(gacc[b][:rows, :cols], 0.0)
            last = None
            for j in range(16):
                e.tensor_scalar(out=gtmp[:rows, :cols], in0=idxf[:rows, :cols],
                                scalar1=float(j), scalar2=lutb[:rows, j:j + 1],
                                op0=ALU.is_equal, op1=ALU.mult)
                last = e.tensor_add(gacc[b][:rows, :cols], gacc[b][:rows, :cols],
                                    gtmp[:rows, :cols])
            return last

        p.op("vector", comp, reads=[n_idxb[b], n_lutb], writes=[n_gacc[b]])
        tr_store(gacc[b], n_gacc[b], dst, rows, cols, b)

    def ghost_mat(dstT, src, R, COLS):
        for i in range(cdiv(R, 128)):
            r0 = i * 128
            rows = min(128, R - r0)
            for c0 in range(0, COLS, CB):
                cols = min(CB, COLS - c0)
                ghost_tile(dstT[c0:c0 + cols, r0:r0 + rows],
                           src[r0:r0 + rows, c0:c0 + cols], rows, cols)

    ghost_mat(o_g, gidx, G_SH2, 2048)
    ghost_mat(o_u, uidx, G_SH2, 2048)
    ghost_mat(o_d, didx, GD_SH, 5504)

    return stores


def build_dequant():
    nc = bass.Bass("TRN2", target_bir_lowering=False, debug=False, detect_race_conditions=bool(os.environ.get("RACEDET")))
    C = 2048
    gup_p = nc.dram_tensor("gup_packed", [GUP_SH, C // 8], I32, kind="ExternalInput").ap()
    gup_s = nc.dram_tensor("gup_scales", [GUP_SH, 16], F16, kind="ExternalInput").ap()
    dwn_p = nc.dram_tensor("down_packed", [DWN_SH, 5632 // 8], I32, kind="ExternalInput").ap()
    dwn_s = nc.dram_tensor("down_scales", [DWN_SH, 44], F16, kind="ExternalInput").ap()
    gidx = nc.dram_tensor("gate_idx", [G_SH2, C], I32, kind="ExternalInput").ap()
    uidx = nc.dram_tensor("up_idx", [G_SH2, C], I32, kind="ExternalInput").ap()
    didx = nc.dram_tensor("down_idx", [GD_SH, 5504], I32, kind="ExternalInput").ap()
    lut = nc.dram_tensor("lut", [1, 16], F32, kind="ExternalInput").ap()

    o_gup = nc.dram_tensor("w_guT", [C, GUP_SH], BF16, kind="ExternalOutput").ap()
    o_dwn = nc.dram_tensor("w_downT", [5632, DWN_SH], BF16, kind="ExternalOutput").ap()
    o_g = nc.dram_tensor("w_gateT", [C, G_SH2], BF16, kind="ExternalOutput").ap()
    o_u = nc.dram_tensor("w_upT", [C, G_SH2], BF16, kind="ExternalOutput").ap()
    o_d = nc.dram_tensor("w_downgT", [5504, GD_SH], BF16, kind="ExternalOutput").ap()

    with ExitStack() as st:
        p = Prog(nc, st)
        stores = emit_dequant(p, nc, st, gup_p, gup_s, dwn_p, dwn_s, gidx, uidx, didx, lut, o_gup, o_dwn, o_g, o_u, o_d)
        p.fence("sync", stores)
        p.finish()
    return nc

def dequant_in_maps(inputs):
    gup_sT = np.ascontiguousarray(np.asarray(inputs["L0_ffn_gate_up_scales"]).T)
    dwn_sT = np.ascontiguousarray(np.asarray(inputs["L0_ffn_down_scales"]).T)
    lut = np.ascontiguousarray(np.asarray(inputs["L1_ffn_lut"]).reshape(1, 16))
    gidx = np.zeros((6144, 2048), np.int32)
    gidx[:5504] = inputs["L1_ffn_gate_idx"]
    uidx = np.zeros((6144, 2048), np.int32)
    uidx[:5504] = inputs["L1_ffn_up_idx"]
    maps = []
    for c in range(8):
        maps.append(dict(
            gup_packed=np.ascontiguousarray(inputs["L0_ffn_gate_up_packed"][c * GUP_SH:(c + 1) * GUP_SH]),
            gup_scales=np.ascontiguousarray(gup_sT[c * GUP_SH:(c + 1) * GUP_SH]),
            down_packed=np.ascontiguousarray(inputs["L0_ffn_down_packed"][c * DWN_SH:(c + 1) * DWN_SH]),
            down_scales=np.ascontiguousarray(dwn_sT[c * DWN_SH:(c + 1) * DWN_SH]),
            gate_idx=np.ascontiguousarray(gidx[c * G_SH2:(c + 1) * G_SH2]),
            up_idx=np.ascontiguousarray(uidx[c * G_SH2:(c + 1) * G_SH2]),
            down_idx=np.ascontiguousarray(inputs["L1_ffn_down_idx"][c * GD_SH:(c + 1) * GD_SH]),
            lut=lut,
        ))
    return maps


def run_dequant(inputs):
    nc = build_dequant()
    res = run_bass_kernel_spmd(nc, dequant_in_maps(inputs), core_ids=list(range(8)))
    cat = lambda k: np.ascontiguousarray(np.concatenate([r[k] for r in res.results], axis=0))
    return dict(w_guT_g=cat("w_guT"), w_downT_g=cat("w_downT"), w_gateT_g=cat("w_gateT"),
                w_upT_g=cat("w_upT"), w_downgT_g=cat("w_downgT"))



C = 2048
CK = 16
TCTX = 4096
TOWN = 2048
NHQ = 16
NHKV = 4
D = 128


def _rope_tables():
    inv = 1.0 / (10000.0 ** (np.arange(0, D, 2, dtype=np.float64) / D))
    ang = np.arange(TCTX, dtype=np.float64)[:, None] * inv[None, :]
    cos = np.concatenate([np.cos(ang), np.cos(ang)], axis=1).T
    sin = np.concatenate([np.sin(ang), np.sin(ang)], axis=1).T
    return np.ascontiguousarray(cos.astype(np.float32)), np.ascontiguousarray(sin.astype(np.float32))


def _rmat():
    R = np.zeros((128, 128), np.float32)
    for j in range(64):
        R[64 + j, j] = -1.0
        R[j, 64 + j] = 1.0
    return R


def emit_l0_attn(p, nc, st, ctx, WqT, WkT, WvT, WoT, qb, kb, vbr, lnw, maskv, x1o, tq_lo, tq_hi):
    cosT, sinT = _rope_tables()
    cosd = nc.inline_tensor(cosT, name=uname("cosT")).ap()
    sind = nc.inline_tensor(sinT, name=uname("sinT")).ap()
    Rd = nc.inline_tensor(_rmat(), name=uname("Rm")).ap()

    sb = lambda name, shape, dt=F32: st.enter_context(nc.sbuf_tensor(uname(name), shape, dt))
    psal = lambda name, shape: st.enter_context(nc.psum_tensor(uname(name), shape, F32))

    wv = sb("wv", [128, 16 * 512], BF16)
    ksb = sb("ksb", [128, 4 * 4096], BF16)
    vsb = sb("vsb", [128, 32 * 512], BF16)
    qsb = sb("qsb", [128, 16 * 512], BF16)
    hsb = sb("hsb", [128, 16 * 512], BF16)
    atn = sb("atn", [128, 16 * 512], BF16)
    x1s = sb("x1s", [128, 16 * 512], F32)
    Rsb = sb("Rsb", [128, 128], F32)
    cossb = sb("cossb", [128, 512], F32)
    sinsb = sb("sinsb", [128, 512], F32)
    qbs = sb("qbs", [128, 16], F32)
    kbs = sb("kbs", [128, 4], F32)
    vbs = sb("vbs", [1, 512], BF16)
    lnws = sb("lnws", [128, 16], F32)
    maskvs = sb("maskvs", [128, 32], F32)
    onesc = sb("onesc", [128, 1], BF16)
    onescol = sb("onescol", [128, 1], F32)
    onescf = sb("onescf", [1, 128], F32)
    zerocol = sb("zerocol", [1, 128], BF16)
    zrow = sb("zrow", [1, 128], BF16)
    onesrb = sb("onesrb", [1, 128], BF16)
    k2m = sb("k2m", [1, 4], F32)
    rowbuf = sb("rowbuf", [1, 512], F32)
    rowbuf2 = sb("rowbuf2", [1, 512], F32)
    negM = sb("negM", [1, 512], BF16)
    rbsb = sb("rbsb", [128, 512], F32)
    rbsb2 = rbsb
    ctxt = [sb(f"ctxt{i}", [128, 512]) for i in range(2)]
    sqt = sb("sqt", [128, 512], BF16)
    sqf = [sb(f"sqf{i}", [128, 512]) for i in range(2)]
    kf = sb("kf", [128, 512], F32)
    s1b = sb("s1b", [128, 512], F32)
    t2f = sb("t2f", [128, 512], F32)
    probs = [sb(f"probs{i}", [128, 512], BF16) for i in range(2)]
    wqt = [sb(f"wqt{i}", [128, 16 * 128], BF16) for i in range(2)]
    wot = [sb(f"wot{i}", [128, 16 * 128], BF16) for i in range(2)]

    ps_ss = psal("ps_ss", [1, 512])
    ps_ss2 = psal("ps_ss2", [1, 512])
    ps_b = psal("ps_b", [128, 512])
    ps_b2 = psal("ps_b2", [128, 512])
    ps_k = psal("ps_k", [128, 512])
    ps_r = ps_b
    ps_s = [psal(f"ps_s{i}", [128, 512]) for i in range(2)]
    ps_o = psal("ps_o", [128, 512])

    N = Node
    n_wv = N("wv")
    n_ksb, n_vsb, n_qsb, n_hsb, n_atn, n_x1s = N("ksb"), N("vsb"), N("qsb"), N("hsb"), N("atn"), N("x1s")
    n_R, n_cos, n_sin = N("R"), N("cos"), N("sin")
    n_qbs, n_kbs, n_vbs, n_lnws, n_maskv = N("qbs"), N("kbs"), N("vbs"), N("lnws"), N("maskv")
    n_onesc, n_onescol, n_onescf, n_onesrb = N("onesc"), N("onescol"), N("onescf"), N("onesrb")
    n_k2m, n_row, n_row2, n_negM = N("k2m"), N("row"), N("row2"), N("negM")
    n_rbsb = N("rbsb")
    n_rbsb2 = n_rbsb
    n_ctxt = nodes("ctxt", 2)
    n_sqt = N("sqt")
    n_sqf = nodes("sqf", 2)
    n_kf, n_s1b, n_t2f = N("kf"), N("s1b"), N("t2f")
    n_probs = nodes("probs", 2)
    n_wqt = nodes("wqt", 2)
    n_wot = nodes("wot", 2)
    n_pss, n_pss2, n_psb, n_psb2, n_psk, n_pso = (
        N("pss"), N("pss2"), N("psb"), N("psb2"), N("psk"), N("pso"))
    n_psr = n_psb
    n_pssc = nodes("pssc", 2)
    n_store = Node("stores")
    stores = []

    p.dma("gpsimd", Rsb[:], Rd, writes=[n_R])
    p.dma("gpsimd", qbs[:], qb, writes=[n_qbs])
    p.dma("gpsimd", kbs[:], kb, writes=[n_kbs])
    p.dma("gpsimd", vbs[:], vbr, writes=[n_vbs])
    p.dma("gpsimd", lnws[:], lnw, writes=[n_lnws])
    p.dma("gpsimd", maskvs[:], maskv, writes=[n_maskv])
    for i in range(CK):
        p.dma("gpsimd", wv[:, i * 512:(i + 1) * 512], WvT[i * 128:(i + 1) * 128, :], writes=[n_wv])
    p.op("vector", lambda e: e.memset(onesc[:], 1.0), writes=[n_onesc])
    p.op("vector", lambda e: e.memset(onescol[:], 1.0), writes=[n_onescol])
    p.op("vector", lambda e: e.memset(onescf[:], 1.0), writes=[n_onescf])
    p.op("vector", lambda e: e.memset(onesrb[:], 1.0), writes=[n_onesrb])
    p.op("vector", lambda e: e.memset(k2m[:], 0.0), writes=[n_k2m])

    def rms_h(t0):
        for ck in range(CK):
            b = ck % 2
            sq = ck % 2
            p.dma("gpsimd", ctxt[b][:], ctx[ck * 128:(ck + 1) * 128, t0:t0 + 512],
                  writes=[n_ctxt[b]])
            p.op("scalar", lambda e, b=b, sq=sq: e.activation(out=sqf[sq][:], in_=ctxt[b][:], func=AF.Square),
                 reads=[n_ctxt[b]], writes=[n_sqf[sq]])
            p.op("tensor", lambda e, ck=ck, sq=sq: e.matmul(
                ps_ss[:], onescol[:], sqf[sq][:], start=(ck == 0), stop=(ck == CK - 1)),
                reads=[n_onescol, n_sqf[sq]], writes=[n_pss])
        p.op("vector", lambda e: e.tensor_scalar(
            out=rowbuf[:], in0=ps_ss[:], scalar1=1.0 / 2048.0, scalar2=1e-6,
            op0=ALU.mult, op1=ALU.add), reads=[n_pss], writes=[n_row])
        p.op("scalar", lambda e: e.activation(out=rowbuf[:], in_=rowbuf[:], func=AF.Ln),
             reads=[n_row], writes=[n_row])
        p.op("scalar", lambda e: e.activation(out=rowbuf[:], in_=rowbuf[:], func=AF.Exp,
                                              scale=-0.5), reads=[n_row], writes=[n_row])
        p.op("tensor", lambda e: e.matmul(ps_b[:], onescf[:], rowbuf[:], start=True, stop=True),
             reads=[n_onescf, n_row], writes=[n_psb])
        for ck in range(CK):
            b = ck % 2
            p.dma("gpsimd", ctxt[b][:], ctx[ck * 128:(ck + 1) * 128, t0:t0 + 512],
                  writes=[n_ctxt[b]])
            p.op("vector", lambda e, ck=ck, b=b: e.scalar_tensor_tensor(
                out=hsb[:, ck * 512:(ck + 1) * 512], in0=ctxt[b][:],
                scalar=lnws[:, ck:ck + 1], in1=ps_b[:],
                op0=ALU.mult, op1=ALU.mult),
                reads=[n_ctxt[b], n_lnws, n_psb], writes=[n_hsb])

    def rope_tile(ps_src, n_ps_src, bias_col, n_bias, out_ap, n_out, t0):
        p.dma("gpsimd", cossb[:], cosd[:, t0:t0 + 512], writes=[n_cos])
        p.dma("gpsimd", sinsb[:], sind[:, t0:t0 + 512], writes=[n_sin])
        p.op("vector", lambda e: e.tensor_scalar(
            out=kf[:], in0=ps_src[:], scalar1=bias_col, scalar2=None,
            op0=ALU.add, op1=ALU.bypass), reads=[n_ps_src, n_bias], writes=[n_kf])
        p.op("vector", lambda e: e.tensor_mul(s1b[:], kf[:], sinsb[:]),
             reads=[n_kf, n_sin], writes=[n_s1b])
        p.op("tensor", lambda e: e.matmul(ps_r[:], Rsb[:], s1b[:], start=True, stop=True),
             reads=[n_R, n_s1b], writes=[n_psr])
        p.op("vector", lambda e: e.tensor_mul(t2f[:], kf[:], cossb[:]),
             reads=[n_kf, n_cos], writes=[n_t2f])
        p.op("vector", lambda e: e.tensor_add(out_ap, t2f[:], ps_r[:]),
             reads=[n_t2f, n_psr], writes=[n_out])

    for tb in range(8):
        t0 = tb * 512
        rms_h(t0)
        for ak in range(4):
            wbi = ak % 2
            p.dma("gpsimd", wqt[wbi][:].rearrange("p (k a) -> p k a", k=16),
                  WkT[:, ak * 128:(ak + 1) * 128].rearrange("(k p) a -> p k a", p=128),
                  writes=[n_wqt[wbi]])

            def kproj(e, wbi=wbi):
                last = None
                for ck in range(CK):
                    last = e.matmul(
                        ps_k[:], wqt[wbi][:, ck * 128:(ck + 1) * 128],
                        hsb[:, ck * 512:(ck + 1) * 512],
                        start=(ck == 0), stop=(ck == CK - 1))
                return last
            p.op("tensor", kproj, reads=[n_wqt[wbi], n_hsb], writes=[n_psk])
            rope_tile(ps_k, n_psk, kbs[:, ak:ak + 1], n_kbs,
                      ksb[:, ak * 4096 + t0: ak * 4096 + t0 + 512], n_ksb, t0)
            p.op("scalar", lambda e, ak=ak, t0=t0: e.activation(
                out=sqt[:], in_=ksb[:, ak * 4096 + t0: ak * 4096 + t0 + 512], func=AF.Square),
                reads=[n_ksb], writes=[n_sqt])
            p.op("tensor", lambda e: e.matmul(ps_ss2[:], onesc[:], sqt[:], start=True, stop=True),
                 reads=[n_onesc, n_sqt], writes=[n_pss2])
            p.op("vector", lambda e: e.reduce_max(rowbuf2[:, 0:1], ps_ss2[:], axis=AX.X),
                 reads=[n_pss2], writes=[n_row2])
            p.op("vector", lambda e, ak=ak: e.tensor_tensor(
                out=k2m[:, ak:ak + 1], in0=k2m[:, ak:ak + 1], in1=rowbuf2[:, 0:1],
                op=ALU.max), reads=[n_row2, n_k2m], writes=[n_k2m])
        for tt in range(4):
            tg = tb * 4 + tt

            def vproj(e, tt=tt):
                for ck in range(CK):
                    e.matmul(ps_k[:], hsb[:, ck * 512 + tt * 128: ck * 512 + (tt + 1) * 128],
                             wv[:, ck * 512:(ck + 1) * 512],
                             start=(ck == 0), stop=False)
                return e.matmul(ps_k[:], onesrb[:], vbs[:], start=False, stop=True)
            p.op("tensor", vproj, reads=[n_hsb, n_wv, n_onesrb, n_vbs], writes=[n_psk])
            p.op("vector", lambda e, tg=tg: e.tensor_copy(
                out=vsb[:, tg * 512:(tg + 1) * 512], in_=ps_k[:]),
                reads=[n_psk], writes=[n_vsb])

    SCALE = float(1.0 / np.sqrt(128.0))
    for tqg in range(tq_lo, tq_hi):
        if os.environ.get("TQBAR"):
            p.barrier_all()
        t0 = 2048 + tqg * 512
        rms_h(t0)
        for ak in range(NHQ):
            wbi = ak % 2
            p.dma("gpsimd", wqt[wbi][:].rearrange("p (k a) -> p k a", k=16),
                  WqT[:, ak * 128:(ak + 1) * 128].rearrange("(k p) a -> p k a", p=128),
                  writes=[n_wqt[wbi]])

            def qproj(e, wbi=wbi):
                last = None
                for ck in range(CK):
                    last = e.matmul(ps_k[:], wqt[wbi][:, ck * 128:(ck + 1) * 128],
                                    hsb[:, ck * 512:(ck + 1) * 512],
                                    start=(ck == 0), stop=(ck == CK - 1))
                return last
            p.op("tensor", qproj, reads=[n_wqt[wbi], n_hsb], writes=[n_psk])
            rope_tile(ps_k, n_psk, qbs[:, ak:ak + 1], n_qbs,
                      qsb[:, ak * 512:(ak + 1) * 512], n_qsb, t0)

        n_kv = (2048 + (tqg + 1) * 512) // 128
        for hq in range(NHQ):
            g = hq // 4
            p.op("scalar", lambda e, hq=hq: e.activation(
                out=sqt[:], in_=qsb[:, hq * 512:(hq + 1) * 512], func=AF.Square),
                reads=[n_qsb], writes=[n_sqt])
            p.op("tensor", lambda e: e.matmul(ps_ss[:], onesc[:], sqt[:], start=True, stop=True),
                 reads=[n_onesc, n_sqt], writes=[n_pss])
            p.op("vector", lambda e, g=g: e.tensor_scalar(
                out=negM[:], in0=ps_ss[:], scalar1=k2m[:, g:g + 1], scalar2=-0.5,
                op0=ALU.add, op1=ALU.mult), reads=[n_pss, n_k2m], writes=[n_negM])
            for j in range(n_kv):
                jb = j % 2

                def scores(e, g=g, j=j, jb=jb, hq=hq):
                    e.matmul(ps_s[jb][:],
                             ksb[:, g * 4096 + j * 128: g * 4096 + (j + 1) * 128],
                             qsb[:, hq * 512:(hq + 1) * 512], start=True, stop=False)
                    return e.matmul(ps_s[jb][:], onesrb[:], negM[:], start=False, stop=True)
                p.op("tensor", scores, reads=[n_ksb, n_qsb, n_onesrb, n_negM],
                     writes=[n_pssc[jb]])
                p.op("scalar", lambda e, j=j, jb=jb: e.activation(
                    out=probs[jb][:], in_=ps_s[jb][:], func=AF.Exp,
                    bias=maskvs[:, j:j + 1], scale=SCALE),
                    reads=[n_pssc[jb], n_maskv], writes=[n_probs[jb]])
                if (j + 1) * 128 > t0:
                    p.op("gpsimd", lambda e, j=j, jb=jb, t0=t0: e.affine_select(
                        out=probs[jb][:], in_=probs[jb][:],
                        pattern=[[1, 512]], compare_op=ALU.is_ge, fill=0.0,
                        base=t0 - j * 128, channel_multiplier=-1),
                        reads=[n_probs[jb]], writes=[n_probs[jb]])

                def pv(e, j=j, g=g, jb=jb, n_kv=n_kv):
                    e.matmul(ps_o[:], vsb[:, j * 512 + g * 128: j * 512 + (g + 1) * 128],
                             probs[jb][:], start=(j == 0), stop=(j == n_kv - 1))
                    return e.matmul(ps_ss2[:], onesc[:], probs[jb][:],
                                    start=(j == 0), stop=(j == n_kv - 1))
                p.op("tensor", pv, reads=[n_vsb, n_probs[jb], n_onesc],
                     writes=[n_pso, n_pss2])
            p.op("vector", lambda e: e.tensor_scalar(
                out=rowbuf[:], in0=ps_ss2[:], scalar1=1e-30, scalar2=None,
                op0=ALU.add, op1=ALU.bypass), reads=[n_pss2], writes=[n_row])
            p.op("vector", lambda e: e.reciprocal(out=rowbuf[:], in_=rowbuf[:]),
                 reads=[n_row], writes=[n_row])
            p.op("tensor", lambda e: e.matmul(ps_b2[:], onescf[:], rowbuf[:], start=True, stop=True),
                 reads=[n_onescf, n_row], writes=[n_psb2])
            p.op("vector", lambda e: e.tensor_copy(out=rbsb[:], in_=ps_b2[:]),
                 reads=[n_psb2], writes=[n_rbsb])
            p.op("vector", lambda e, hq=hq: e.tensor_mul(
                atn[:, hq * 512:(hq + 1) * 512], ps_o[:], rbsb[:]),
                reads=[n_pso, n_rbsb], writes=[n_atn])

        for ck in range(CK):
            wbi = ck % 2
            p.dma("gpsimd", wot[wbi][:].rearrange("p (k c) -> p k c", k=16),
                  WoT[:, ck * 128:(ck + 1) * 128].rearrange("(k p) c -> p k c", p=128),
                  writes=[n_wot[wbi]])

            def oproj(e, wbi=wbi):
                last = None
                for ak in range(NHQ):
                    last = e.matmul(ps_k[:], wot[wbi][:, ak * 128:(ak + 1) * 128],
                                    atn[:, ak * 512:(ak + 1) * 512],
                                    start=(ak == 0), stop=(ak == NHQ - 1))
                return last
            p.op("tensor", oproj, reads=[n_wot[wbi], n_atn], writes=[n_psk])
            b = ck % 2
            p.dma("gpsimd", ctxt[b][:], ctx[ck * 128:(ck + 1) * 128, t0:t0 + 512],
                  writes=[n_ctxt[b]])
            p.op("vector", lambda e, ck=ck, b=b: e.tensor_add(
                x1s[:, ck * 512:(ck + 1) * 512], ps_k[:], ctxt[b][:]),
                reads=[n_psk, n_ctxt[b]], writes=[n_x1s])

        for ck in range(CK):
            sq = ck % 2
            p.op("tensor", lambda e, ck=ck: e.matmul(
                ps_ss[:], onescol[:], x1s[:, ck * 512:(ck + 1) * 512],
                start=(ck == 0), stop=(ck == CK - 1)),
                reads=[n_onescol, n_x1s], writes=[n_pss])
            p.op("scalar", lambda e, ck=ck, sq=sq: e.activation(
                out=sqf[sq][:], in_=x1s[:, ck * 512:(ck + 1) * 512], func=AF.Square),
                reads=[n_x1s], writes=[n_sqf[sq]])
            p.op("tensor", lambda e, ck=ck, sq=sq: e.matmul(
                ps_ss2[:], onescol[:], sqf[sq][:],
                start=(ck == 0), stop=(ck == CK - 1)),
                reads=[n_onescol, n_sqf[sq]], writes=[n_pss2])
        p.op("vector", lambda e: e.tensor_scalar(
            out=rowbuf[:], in0=ps_ss[:], scalar1=1.0 / 2048.0, scalar2=None,
            op0=ALU.mult, op1=ALU.bypass), reads=[n_pss], writes=[n_row])
        p.op("vector", lambda e: e.tensor_mul(rowbuf2[:], rowbuf[:], rowbuf[:]),
             reads=[n_row], writes=[n_row2])
        p.op("vector", lambda e: e.tensor_scalar(
            out=rowbuf2[:], in0=rowbuf2[:], scalar1=1e-5, scalar2=None,
            op0=ALU.subtract, op1=ALU.bypass), reads=[n_row2], writes=[n_row2])
        p.op("vector", lambda e: e.scalar_tensor_tensor(
            out=rowbuf2[:], in0=ps_ss2[:], scalar=1.0 / 2048.0, in1=rowbuf2[:],
            op0=ALU.mult, op1=ALU.subtract), reads=[n_pss2, n_row2], writes=[n_row2])
        p.op("scalar", lambda e: e.activation(out=rowbuf2[:], in_=rowbuf2[:], func=AF.Ln),
             reads=[n_row2], writes=[n_row2])
        p.op("scalar", lambda e: e.activation(out=rowbuf2[:], in_=rowbuf2[:], func=AF.Exp,
                                              scale=-0.5), reads=[n_row2], writes=[n_row2])
        p.op("tensor", lambda e: e.matmul(ps_b[:], onescf[:], rowbuf[:], start=True, stop=True),
             reads=[n_onescf, n_row], writes=[n_psb])
        p.op("tensor", lambda e: e.matmul(ps_b2[:], onescf[:], rowbuf2[:], start=True, stop=True),
             reads=[n_onescf, n_row2], writes=[n_psb2])
        p.op("vector", lambda e: e.tensor_copy(out=rbsb2[:], in_=ps_b2[:]),
             reads=[n_psb2], writes=[n_rbsb2])
        for ck in range(CK):
            p.op("vector", lambda e, ck=ck: e.tensor_sub(
                x1s[:, ck * 512:(ck + 1) * 512], x1s[:, ck * 512:(ck + 1) * 512], ps_b[:]),
                reads=[n_psb], writes=[n_x1s])
            p.op("vector", lambda e, ck=ck: e.tensor_mul(
                x1s[:, ck * 512:(ck + 1) * 512], x1s[:, ck * 512:(ck + 1) * 512], rbsb2[:]),
                reads=[n_rbsb2], writes=[n_x1s])
            stores.append(p.dma(
                "sync", x1o[ck * 128:(ck + 1) * 128, tqg * 512:(tqg + 1) * 512],
                x1s[:, ck * 512:(ck + 1) * 512], reads=[n_x1s], store_node=n_store))

    return stores


def build_l0_attn(tq_lo=0, tq_hi=4):
    nc = bass.Bass("TRN2", target_bir_lowering=False, debug=False, detect_race_conditions=bool(os.environ.get("RACEDET")))
    ctx = nc.dram_tensor("ctx", [C, TCTX], F32, kind="ExternalInput").ap()
    WqT = nc.dram_tensor("WqT", [C, 2048], F32, kind="ExternalInput").ap()
    WkT = nc.dram_tensor("WkT", [C, 512], F32, kind="ExternalInput").ap()
    WvT = nc.dram_tensor("WvT", [C, 512], F32, kind="ExternalInput").ap()
    WoT = nc.dram_tensor("WoT", [2048, C], F32, kind="ExternalInput").ap()
    qb = nc.dram_tensor("qb", [128, 16], F32, kind="ExternalInput").ap()
    kb = nc.dram_tensor("kb", [128, 4], F32, kind="ExternalInput").ap()
    vbr = nc.dram_tensor("vbr", [1, 512], F32, kind="ExternalInput").ap()
    lnw = nc.dram_tensor("lnw", [128, 16], F32, kind="ExternalInput").ap()
    maskv = nc.dram_tensor("maskv", [128, 32], F32, kind="ExternalInput").ap()
    x1o = nc.dram_tensor("x1", [C, TOWN], F32, kind="ExternalOutput").ap()

    with ExitStack() as st:
        p = Prog(nc, st)
        stores = emit_l0_attn(p, nc, st, ctx, WqT, WkT, WvT, WoT, qb, kb, vbr, lnw, maskv, x1o, tq_lo, tq_hi)
        p.fence("sync", stores)
        p.finish()
    return nc

def run_l0_attn(inputs, x_parts):
    nc = build_l0_attn()
    qb = np.ascontiguousarray(inputs["L0_attn_q_bias"].reshape(16, 128).T)
    kb = np.ascontiguousarray(inputs["L0_attn_k_bias"].reshape(4, 128).T)
    vb = np.ascontiguousarray(inputs["L0_attn_v_bias"].reshape(1, 512))
    lnw = np.ascontiguousarray(inputs["L0_attn_ln_weight"].reshape(16, 128).T)
    WqT = np.ascontiguousarray(np.asarray(inputs["L0_attn_q_project"]).T)
    WkT = np.ascontiguousarray(np.asarray(inputs["L0_attn_k_project"]).T)
    WvT = np.ascontiguousarray(np.asarray(inputs["L0_attn_v_project"]).T)
    WoT = np.ascontiguousarray(np.asarray(inputs["L0_attn_o_project"]).T)
    mask_pad = np.zeros((4096,), np.float32)
    mask_pad[:2048] = -1e30
    mask_pad = np.ascontiguousarray(mask_pad.reshape(32, 128).T)
    mask_zero = np.zeros((128, 32), np.float32)
    in_maps = []
    for c in range(8):
        half = c % 2
        if half == 0:
            ctxc = np.concatenate([np.zeros_like(x_parts[c]), x_parts[c]], axis=1)
            mv = mask_pad
        else:
            ctxc = np.concatenate([x_parts[c - 1], x_parts[c]], axis=1)
            mv = mask_zero
        in_maps.append(dict(ctx=np.ascontiguousarray(ctxc), WqT=WqT, WkT=WkT, WvT=WvT,
                            WoT=WoT, qb=qb, kb=kb, vbr=vb, lnw=lnw, maskv=mv))
    res = run_bass_kernel_spmd(nc, in_maps, core_ids=list(range(8)))
    return [r["x1"] for r in res.results]


def split_x_parts(x):
    xt = np.ascontiguousarray(np.asarray(x).transpose(0, 2, 1))
    parts = []
    for c in range(8):
        b, half = c // 2, c % 2
        parts.append(np.ascontiguousarray(xt[b, :, half * 2048:(half + 1) * 2048]))
    return parts


def kernel(**inputs):
    raise NotImplementedError("phase A under construction")



def emit_ffn(p, nc, st, x1, wg_g, wu_g, wd_g, x2o, I, rms_ep, SHW):
    IK = I // 128
    nsh = cdiv(IK * 128, SHW)
    def gu_ap(g, i):
        o = i * 128
        r, ol = o // SHW, o % SHW
        return g[r * C:(r + 1) * C, ol:ol + 128].rearrange("(k p) a -> p k a", p=128)

    def dn_ap(ck):
        r, cl = ck // 2, (ck % 2) * 128
        return wd_g[r * I:(r + 1) * I, cl:cl + 128].rearrange("(k p) a -> p k a", p=128)

    sb = lambda name, shape, dt=F32: st.enter_context(nc.sbuf_tensor(uname(name), shape, dt))
    psal = lambda name, shape: st.enter_context(nc.psum_tensor(uname(name), shape, F32))

    xb = sb("xb", [128, 16 * 512], BF16)
    hsb = sb("hsb", [128, IK * 512], BF16)
    gt = sb("gt", [128, 512], BF16)
    ysb = sb("ysb", [128, 16 * 512], F32)
    x2s = sb("x2s", [128, 16 * 512], F32)
    ctxt = [sb(f"ctxt{i}", [128, 512]) for i in range(2)]
    sqf = [sb(f"sqf{i}", [128, 512]) for i in range(2)]
    wgt = [sb(f"wgt{i}", [128, 16 * 128], BF16) for i in range(2)]
    wut = [sb(f"wut{i}", [128, 16 * 128], BF16) for i in range(2)]
    wdt = [sb(f"wdt{i}", [128, IK * 128], BF16) for i in range(2)]
    onescol = sb("onescol", [128, 1], F32)
    onescf = sb("onescf", [1, 128], F32)
    zerocol = sb("zerocol", [1, 128], BF16)
    zrow = sb("zrow", [1, 128], BF16)
    rowbuf = sb("rowbuf", [1, 512], F32)
    rowbuf2 = sb("rowbuf2", [1, 512], F32)
    rbsb = sb("rbsb", [128, 512], F32)

    ps_g = psal("ps_g", [128, 512])
    ps_u = psal("ps_u", [128, 512])
    ps_d = psal("ps_d", [128, 512])
    ps_ss = psal("ps_ss", [1, 512])
    ps_ss2 = psal("ps_ss2", [1, 512])
    ps_b = psal("ps_b", [128, 512])
    ps_b2 = psal("ps_b2", [128, 512])

    N = Node
    n_xb, n_hsb, n_gt, n_ysb, n_x2s = N("xb"), N("hsb"), N("gt"), N("ysb"), N("x2s")
    n_ctxt = nodes("ctxt", 2)
    n_sqf = nodes("sqf", 2)
    n_wgt = nodes("wgt", 2)
    n_wut = nodes("wut", 2)
    n_wdt = nodes("wdt", 2)
    n_ones, n_onescf = N("ones"), N("onescf")
    n_row, n_row2, n_rbsb = N("row"), N("row2"), N("rbsb")
    n_psg, n_psu, n_psd = N("psg"), N("psu"), N("psd")
    n_pss, n_pss2, n_psb, n_psb2 = N("pss"), N("pss2"), N("psb"), N("psb2")
    n_store = Node("store")
    stores = []

    p.op("vector", lambda e: e.memset(onescol[:], 1.0), writes=[n_ones])
    p.op("vector", lambda e: e.memset(onescf[:], 1.0), writes=[n_onescf])

    for tg in range(4):
        t0 = tg * 512
        for ck in range(CK):
            p.dma("gpsimd", xb[:, ck * 512:(ck + 1) * 512],
                  x1[ck * 128:(ck + 1) * 128, t0:t0 + 512], writes=[n_xb])
        for ii in range(IK):
            wb = ii % 2
            p.dma("sync", wgt[wb][:].rearrange("p (k a) -> p k a", k=16),
                  gu_ap(wg_g, ii), writes=[n_wgt[wb]])
            p.dma("sync", wut[wb][:].rearrange("p (k a) -> p k a", k=16),
                  gu_ap(wu_g, ii), writes=[n_wut[wb]])

            def gmm(e, wb=wb, w=wgt, ps=ps_g):
                last = None
                for ck in range(CK):
                    last = e.matmul(ps[:], w[wb][:, ck * 128:(ck + 1) * 128],
                                    xb[:, ck * 512:(ck + 1) * 512],
                                    start=(ck == 0), stop=(ck == CK - 1))
                return last
            p.op("tensor", gmm, reads=[n_wgt[wb], n_xb], writes=[n_psg])
            p.op("scalar", lambda e: e.activation(out=gt[:], in_=ps_g[:], func=AF.Silu),
                 reads=[n_psg], writes=[n_gt])
            p.op("tensor", lambda e, wb=wb: gmm(e, wb=wb, w=wut, ps=ps_u),
                 reads=[n_wut[wb], n_xb], writes=[n_psu])
            p.op("vector", lambda e, ii=ii: e.tensor_mul(
                hsb[:, ii * 512:(ii + 1) * 512], ps_u[:], gt[:]),
                reads=[n_psu, n_gt], writes=[n_hsb])
        for ck in range(CK):
            wb = ck % 2
            p.dma("sync", wdt[wb][:].rearrange("p (k a) -> p k a", k=IK),
                  dn_ap(ck), writes=[n_wdt[wb]])

            def dmm(e, wb=wb):
                last = None
                for ii in range(IK):
                    last = e.matmul(ps_d[:], wdt[wb][:, ii * 128:(ii + 1) * 128],
                                    hsb[:, ii * 512:(ii + 1) * 512],
                                    start=(ii == 0), stop=(ii == IK - 1))
                return last
            p.op("tensor", dmm, reads=[n_wdt[wb], n_hsb], writes=[n_psd])
            p.op("vector", lambda e, ck=ck: e.tensor_copy(
                out=ysb[:, ck * 512:(ck + 1) * 512], in_=ps_d[:]),
                reads=[n_psd], writes=[n_ysb])

        if rms_ep:
            for ck in range(CK):
                sq = ck % 2
                p.op("scalar", lambda e, ck=ck, sq=sq: e.activation(
                    out=sqf[sq][:], in_=ysb[:, ck * 512:(ck + 1) * 512], func=AF.Square),
                    reads=[n_ysb], writes=[n_sqf[sq]])
                p.op("tensor", lambda e, ck=ck, sq=sq: e.matmul(
                    ps_ss[:], onescol[:], sqf[sq][:],
                    start=(ck == 0), stop=(ck == CK - 1)),
                    reads=[n_ones, n_sqf[sq]], writes=[n_pss])
            p.op("vector", lambda e: e.tensor_scalar(
                out=rowbuf[:], in0=ps_ss[:], scalar1=1.0 / 2048.0, scalar2=1e-6,
                op0=ALU.mult, op1=ALU.add), reads=[n_pss], writes=[n_row])
            p.op("scalar", lambda e: e.activation(out=rowbuf[:], in_=rowbuf[:], func=AF.Ln),
                 reads=[n_row], writes=[n_row])
            p.op("scalar", lambda e: e.activation(out=rowbuf[:], in_=rowbuf[:], func=AF.Exp,
                                                  scale=-0.5), reads=[n_row], writes=[n_row])
            p.op("tensor", lambda e: e.matmul(ps_b[:], onescf[:], rowbuf[:], start=True, stop=True),
                 reads=[n_onescf, n_row], writes=[n_psb])
        for ck in range(CK):
            b = ck % 2
            p.dma("gpsimd", ctxt[b][:], x1[ck * 128:(ck + 1) * 128, t0:t0 + 512],
                  writes=[n_ctxt[b]])
            if rms_ep:
                p.op("vector", lambda e, ck=ck, b=b: e.tensor_mul(
                    x2s[:, ck * 512:(ck + 1) * 512], ps_b[:], ysb[:, ck * 512:(ck + 1) * 512]),
                    reads=[n_psb, n_ysb], writes=[n_x2s])
                p.op("vector", lambda e, ck=ck, b=b: e.tensor_add(
                    x2s[:, ck * 512:(ck + 1) * 512], x2s[:, ck * 512:(ck + 1) * 512],
                    ctxt[b][:]), reads=[n_ctxt[b]], writes=[n_x2s])
            else:
                p.op("vector", lambda e, ck=ck, b=b: e.tensor_add(
                    x2s[:, ck * 512:(ck + 1) * 512], ysb[:, ck * 512:(ck + 1) * 512],
                    ctxt[b][:]), reads=[n_ysb, n_ctxt[b]], writes=[n_x2s])
        for ck in range(CK):
            sq = ck % 2
            p.op("tensor", lambda e, ck=ck: e.matmul(
                ps_ss[:], onescol[:], x2s[:, ck * 512:(ck + 1) * 512],
                start=(ck == 0), stop=(ck == CK - 1)),
                reads=[n_ones, n_x2s], writes=[n_pss])
            p.op("scalar", lambda e, ck=ck, sq=sq: e.activation(
                out=sqf[sq][:], in_=x2s[:, ck * 512:(ck + 1) * 512], func=AF.Square),
                reads=[n_x2s], writes=[n_sqf[sq]])
            p.op("tensor", lambda e, ck=ck, sq=sq: e.matmul(
                ps_ss2[:], onescol[:], sqf[sq][:],
                start=(ck == 0), stop=(ck == CK - 1)),
                reads=[n_ones, n_sqf[sq]], writes=[n_pss2])
        p.op("vector", lambda e: e.tensor_scalar(
            out=rowbuf[:], in0=ps_ss[:], scalar1=1.0 / 2048.0, scalar2=None,
            op0=ALU.mult, op1=ALU.bypass), reads=[n_pss], writes=[n_row])
        p.op("vector", lambda e: e.tensor_mul(rowbuf2[:], rowbuf[:], rowbuf[:]),
             reads=[n_row], writes=[n_row2])
        p.op("vector", lambda e: e.tensor_scalar(
            out=rowbuf2[:], in0=rowbuf2[:], scalar1=1e-5, scalar2=None,
            op0=ALU.subtract, op1=ALU.bypass), reads=[n_row2], writes=[n_row2])
        p.op("vector", lambda e: e.scalar_tensor_tensor(
            out=rowbuf2[:], in0=ps_ss2[:], scalar=1.0 / 2048.0, in1=rowbuf2[:],
            op0=ALU.mult, op1=ALU.subtract), reads=[n_pss2, n_row2], writes=[n_row2])
        p.op("scalar", lambda e: e.activation(out=rowbuf2[:], in_=rowbuf2[:], func=AF.Ln),
             reads=[n_row2], writes=[n_row2])
        p.op("scalar", lambda e: e.activation(out=rowbuf2[:], in_=rowbuf2[:], func=AF.Exp,
                                              scale=-0.5), reads=[n_row2], writes=[n_row2])
        p.op("tensor", lambda e: e.matmul(ps_b[:], onescf[:], rowbuf[:], start=True, stop=True),
             reads=[n_onescf, n_row], writes=[n_psb])
        p.op("tensor", lambda e: e.matmul(ps_b2[:], onescf[:], rowbuf2[:], start=True, stop=True),
             reads=[n_onescf, n_row2], writes=[n_psb2])
        p.op("vector", lambda e: e.tensor_copy(out=rbsb[:], in_=ps_b2[:]),
             reads=[n_psb2], writes=[n_rbsb])
        for ck in range(CK):
            p.op("vector", lambda e, ck=ck: e.tensor_sub(
                x2s[:, ck * 512:(ck + 1) * 512], x2s[:, ck * 512:(ck + 1) * 512], ps_b[:]),
                reads=[n_psb], writes=[n_x2s])
            p.op("vector", lambda e, ck=ck: e.tensor_mul(
                x2s[:, ck * 512:(ck + 1) * 512], x2s[:, ck * 512:(ck + 1) * 512], rbsb[:]),
                reads=[n_rbsb], writes=[n_x2s])
            stores.append(p.dma(
                "sync", x2o[ck * 128:(ck + 1) * 128, t0:t0 + 512],
                x2s[:, ck * 512:(ck + 1) * 512], reads=[n_x2s], store_node=n_store))

    return stores


def build_ffn(I, rms_ep, SHW):
    IK = I // 128
    nc = bass.Bass("TRN2", target_bir_lowering=False, debug=False,
                   detect_race_conditions=bool(os.environ.get("RACEDET")))
    nsh = cdiv(IK * 128, SHW)
    x1 = nc.dram_tensor("x1", [C, TOWN], F32, kind="ExternalInput").ap()
    wg_g = nc.dram_tensor("wg_g", [nsh * C, SHW], BF16, kind="ExternalInput").ap()
    wu_g = nc.dram_tensor("wu_g", [nsh * C, SHW], BF16, kind="ExternalInput").ap()
    wd_g = nc.dram_tensor("wd_g", [8 * I, 256], BF16, kind="ExternalInput").ap()
    x2o = nc.dram_tensor("x2", [C, TOWN], F32, kind="ExternalOutput").ap()

    with ExitStack() as st:
        p = Prog(nc, st)
        stores = emit_ffn(p, nc, st, x1, wg_g, wu_g, wd_g, x2o, I, rms_ep, SHW)
        p.fence("sync", stores)
        p.finish()
    return nc

def run_ffn(x_parts, wg_g, wu_g, wd_g, I, rms_ep, SHW):
    nc = build_ffn(I, rms_ep, SHW)
    in_maps = [dict(x1=np.ascontiguousarray(x_parts[c]), wg_g=wg_g, wu_g=wu_g, wd_g=wd_g)
               for c in range(8)]
    res = run_bass_kernel_spmd(nc, in_maps, core_ids=list(range(8)))
    return [r["x2"] for r in res.results]



RT = 4096
TTILE = 256
NTT = RT // TTILE
NCH = TTILE // 32
AC = 8
CLAMP = 4.8


def _masks_lt():
    s = np.arange(128)
    m_loc = ((s[:, None] // 16 == s[None, :] // 16) & (s[None, :] > s[:, None]))
    m_bnd = ((s[:, None] // 32 == s[None, :] // 32) & (s[:, None] % 32 < 16) & (s[None, :] % 32 >= 16))
    return m_loc.astype(np.float32).astype(BF), m_bnd.astype(np.float32).astype(BF)


def emit_l1_attn(p, nc, st, x2ap, maa, w1, w2, dw1, dw2, tdcol, WrT, WkT, WvT, WgT, WoT, ucol, gamr, betr, ypo, ntt):
    m_loc_np, m_bnd_np = _masks_lt()
    mlocd = nc.inline_tensor(m_loc_np, name=uname("mloc")).ap()
    mbndd = nc.inline_tensor(m_bnd_np, name=uname("mbnd")).ap()
    idbf = nc.inline_tensor(np.eye(128, dtype=np.float32).astype(BF), name=uname("idbf")).ap()
    idf32 = nc.inline_tensor(np.eye(128, dtype=np.float32), name=uname("idf32")).ap()

    sb = lambda name, shape, dt=F32: st.enter_context(nc.sbuf_tensor(uname(name), shape, dt))
    psal = lambda name, shape, dt=F32: st.enter_context(nc.psum_tensor(uname(name), shape, dt))

    xt = sb("xt", [128, 16 * TTILE])
    xpv = sb("xpv", [128, 16 * TTILE])
    mix = sb("mix", [128, 16 * TTILE], BF16)
    zb = [sb(f"zb{f}", [32, TTILE], BF16) for f in range(5)]
    zd = sb("zd", [64, TTILE], BF16)
    logw = sb("logw", [128, AC * TTILE])
    Bp = logw
    Bs = sb("Bs", [128, AC * TTILE])
    n15 = sb("n15", [128, AC * 2 * NCH])
    dcol = sb("dcol", [128, AC * NCH])
    kk31 = sb("kk31", [128, AC * 2 * NCH])
    refq = sb("refq", [128, AC * 2 * NCH])
    refk = sb("refk", [128, AC * 2 * NCH])
    targ = [sb(f"targ{i}", [128, TTILE]) for i in range(2)]
    q_loc = sb("q_loc", [128, AC * TTILE], BF16)
    k_loc = sb("k_loc", [128, AC * TTILE], BF16)
    q_bnd = sb("q_bnd", [128, AC * TTILE], BF16)
    k_bnd = sb("k_bnd", [128, AC * TTILE], BF16)
    qpp = sb("qpp", [128, AC * TTILE])
    kpp = sb("kpp", [128, AC * TTILE])
    rpl = sb("rpl", [128, AC * TTILE])
    kpl = sb("kpl", [128, AC * TTILE])
    v_tok = sb("v_tok", [128, (TTILE // 128) * 1024], BF16)
    g_tok = sb("g_tok", [128, (TTILE // 128) * 1024], BF16)
    vpl = sb("vpl", [128, AC * TTILE])
    kp_tok = sb("kp_tok", [128, (TTILE // 128) * 1024], BF16)
    og_f = sb("og_f", [128, AC * TTILE], BF16)
    out_tok = sb("out_tok", [128, (TTILE // 128) * 1024])
    Ssb = sb("Ssb", [128, AC * 64])
    erk = sb("erk", [128, TTILE], BF16)
    drows = sb("drows", [16, TTILE])
    d_tok = sb("d_tok", [128, (TTILE // 128) * 16])
    gsum = sb("gsum", [128, (TTILE // 128) * 16])
    gsq = sb("gsq", [128, (TTILE // 128) * 16])
    gsqp = sb("gsqp", [128, 1024])
    A1 = sb("A1", [128, 128], BF16)
    A1b = sb("A1b", [128, 128], BF16)
    A2 = sb("A2", [128, 128], BF16)
    A2b = sb("A2b", [128, 128], BF16)
    gamb = sb("gamb", [128, 1024])
    betb = sb("betb", [128, 1024])
    mlocs = sb("mlocs", [128, 128], BF16)
    mbnds = sb("mbnds", [128, 128], BF16)
    idfs = sb("idfs", [128, 128])
    BCs = sb("BCs", [128, AC * 16], BF16)
    ucols = sb("ucols", [128, 8])
    tds = sb("tds", [128, 8])
    maas = {k: sb(f"maas_{k}", [128, 16]) for k in ("x", "w", "k", "v", "r", "g")}
    w1s = sb("w1s", [128, 16 * 160], BF16)
    w2r = [sb(f"w2r{i}", [32, 2048], BF16) for i in range(2)]
    dw1s = sb("dw1s", [128, 16 * 64], BF16)
    dw2s = sb("dw2s", [64, 8 * 128], BF16)
    wring = [sb(f"wring{i}", [128, 16 * 128], BF16) for i in range(2)]
    onescf = sb("onescf", [1, 128], F32)
    zerocol = sb("zerocol", [1, 128], BF16)
    zrow = sb("zrow", [1, 128], BF16)
    gamrs = sb("gamrs", [1, 1024])
    betrs = sb("betrs", [1, 1024])
    tmp1 = [sb(f"tmp1{i}", [128, TTILE]) for i in range(2)]
    tmp2 = tmp1
    gn1 = sb("gn1", [128, 16])
    gn2 = sb("gn2", [128, 16])

    ps1 = psal("ps1", [128, 512])
    psA0 = psal("psA0", [128, 512])
    psA1 = psal("psA1", [128, 512])
    ps_out = psal("ps_out", [128, 512])
    ps_st = [psal(f"ps_st{i}", [128, 512]) for i in range(4)]
    psP = [ps1, psA0]

    N = Node
    n_xt, n_xpv, n_mix = N("xt"), N("xpv"), N("mix")
    n_zb = nodes("zb", 5)
    n_zd = N("zd")
    n_logw, n_B, n_Bs = N("logw"), N("B"), N("Bs")
    n_cols = N("cols")
    n_targ = nodes("targ", 2)
    n_qloc, n_kloc, n_qbnd, n_kbnd = N("qloc"), N("kloc"), N("qbnd"), N("kbnd")
    n_qpp, n_kpp, n_rpl, n_kpl = N("qpp"), N("kpp"), N("rpl"), N("kpl")
    n_vtok, n_gtok, n_ogf, n_outtok = N("vtok"), N("gtok"), N("ogf"), N("outtok")
    n_vpl, n_kptok = N("vpl"), N("kptok")
    n_S = nodes("S", AC)
    n_erk, n_drows, n_dtok = N("erk"), N("drows"), N("dtok")
    n_gsum, n_gsq, n_gsqp = N("gsum"), N("gsq"), N("gsqp")
    n_A1, n_A1b, n_A2, n_A2b = N("A1"), N("A1b"), N("A2"), N("A2b")
    n_gamb, n_betb = N("gamb"), N("betb")
    n_consts = N("consts")
    n_w1, n_dw1, n_dw2 = N("w1"), N("dw1"), N("dw2")
    n_w2r = nodes("w2r", 2)
    n_wring = nodes("wring", 2)
    n_ps1, n_psA0, n_psA1, n_psout = N("ps1"), N("psA0"), N("psA1"), N("psout")
    nP = [n_ps1, n_psA0]
    n_psst = nodes("psst", 4)

    n_tmp1 = nodes("tmp1", 2)
    n_tmp2 = n_tmp1
    n_gn1, n_gn2 = N("gn1"), N("gn2")
    n_store = N("store")
    stores = []

    p.dma("gpsimd", mlocs[:], mlocd, writes=[n_consts])
    p.dma("gpsimd", mbnds[:], mbndd, writes=[n_consts])
    p.dma("gpsimd", idfs[:], idf32, writes=[n_consts])
    p.dma("gpsimd", ucols[:], ucol, writes=[n_consts])
    p.dma("gpsimd", tds[:], tdcol, writes=[n_consts])
    for k in maas:
        p.dma("gpsimd", maas[k][:], maa[k], writes=[n_consts])
    p.dma("gpsimd", gamrs[:], gamr, writes=[n_consts])
    p.dma("gpsimd", betrs[:], betr, writes=[n_consts])
    for i in range(CK):
        p.dma("gpsimd", w1s[:, i * 160:(i + 1) * 160], w1[i * 128:(i + 1) * 128, :], writes=[n_w1])
        p.dma("gpsimd", dw1s[:, i * 64:(i + 1) * 64], dw1[i * 128:(i + 1) * 128, :], writes=[n_dw1])
    for j in range(AC):
        p.dma("gpsimd", dw2s[:, j * 128:(j + 1) * 128], dw2[:, j * 128:(j + 1) * 128], writes=[n_dw2])
    p.op("vector", lambda e: e.memset(onescf[:], 1.0), writes=[n_consts])
    p.op("vector", lambda e: e.memset(zerocol[:], 0.0), writes=[n_consts])
    p.op("vector", lambda e: e.memset(zrow[:], 0.0), writes=[n_consts])
    for h2 in range(2):
        p.op("tensor", lambda e, h2=h2: e.matmul(
            ps1[:], onescf[:], gamrs[:, h2 * 512:(h2 + 1) * 512], start=True, stop=True),
            reads=[n_consts], writes=[n_ps1])
        p.op("vector", lambda e, h2=h2: e.tensor_copy(
            out=gamb[:, h2 * 512:(h2 + 1) * 512], in_=ps1[:]), reads=[n_ps1], writes=[n_gamb])
        p.op("tensor", lambda e, h2=h2: e.matmul(
            ps1[:], onescf[:], betrs[:, h2 * 512:(h2 + 1) * 512], start=True, stop=True),
            reads=[n_consts], writes=[n_ps1])
        p.op("vector", lambda e, h2=h2: e.tensor_copy(
            out=betb[:, h2 * 512:(h2 + 1) * 512], in_=ps1[:]), reads=[n_ps1], writes=[n_betb])
    p.op("vector", lambda e: e.memset(BCs[:], 0.0), writes=[n_consts])
    for j in range(AC):
        p.op("vector", lambda e, j=j: e.tensor_copy(
            out=BCs[0:64, j * 16 + 2 * j: j * 16 + 2 * j + 1], in_=ucols[0:64, j:j + 1]),
            writes=[n_consts])
        p.op("vector", lambda e, j=j: e.tensor_copy(
            out=BCs[64:128, j * 16 + 2 * j + 1: j * 16 + 2 * j + 2], in_=ucols[64:128, j:j + 1]),
            writes=[n_consts])
    p.op("vector", lambda e: e.memset(Ssb[:], 0.0), writes=list(n_S))

    def proj_full(wT, mix_node, out_cb, nach=AC, wofs=0):
        for ac in range(nach):
            wb = ac % 2
            pb = ac % 2
            p.dma("gpsimd", wring[wb][:].rearrange("p (k a) -> p k a", k=16),
                  wT[:, wofs + ac * 128: wofs + (ac + 1) * 128].rearrange("(k p) a -> p k a", p=128),
                  writes=[n_wring[wb]])

            def mm(e, wb=wb, pb=pb):
                last = None
                for ck in range(CK):
                    last = e.matmul(psP[pb][:, :TTILE], wring[wb][:, ck * 128:(ck + 1) * 128],
                                    mix[:, ck * TTILE:(ck + 1) * TTILE],
                                    start=(ck == 0), stop=(ck == CK - 1))
                return last
            p.op("tensor", mm, reads=[n_wring[wb], mix_node], writes=[nP[pb]])
            out_cb(ac, psP[pb], nP[pb])

    for tt in range(ntt):
        t0 = tt * TTILE
        for ck in range(CK):
            p.dma("gpsimd", xt[:, ck * TTILE:(ck + 1) * TTILE],
                  x2ap(ck, t0, t0 + TTILE), writes=[n_xt])
            if t0 == 0:
                p.op("vector", lambda e, ck=ck: e.memset(
                    xpv[:, ck * TTILE:ck * TTILE + 1], 0.0), writes=[n_xpv])
                p.dma("gpsimd", xpv[:, ck * TTILE + 1:(ck + 1) * TTILE],
                      x2ap(ck, 0, TTILE - 1), writes=[n_xpv])
            elif (t0 - 1) // 2048 != (t0 + TTILE - 2) // 2048:
                p.dma("gpsimd", xpv[:, ck * TTILE:ck * TTILE + 1],
                      x2ap(ck, t0 - 1, t0), writes=[n_xpv],
                      allow_slow_non_contiguous=True)
                p.dma("gpsimd", xpv[:, ck * TTILE + 1:(ck + 1) * TTILE],
                      x2ap(ck, t0, t0 + TTILE - 1), writes=[n_xpv])
            else:
                p.dma("gpsimd", xpv[:, ck * TTILE:(ck + 1) * TTILE],
                      x2ap(ck, t0 - 1, t0 + TTILE - 1), writes=[n_xpv])
        for ck in range(CK):
            p.op("vector", lambda e, ck=ck: e.tensor_sub(
                xpv[:, ck * TTILE:(ck + 1) * TTILE], xpv[:, ck * TTILE:(ck + 1) * TTILE],
                xt[:, ck * TTILE:(ck + 1) * TTILE]), reads=[n_xt], writes=[n_xpv])
        for ck in range(CK):
            p.op("vector", lambda e, ck=ck: e.scalar_tensor_tensor(
                out=mix[:, ck * TTILE:(ck + 1) * TTILE], in0=xpv[:, ck * TTILE:(ck + 1) * TTILE],
                scalar=maas["x"][:, ck:ck + 1], in1=xt[:, ck * TTILE:(ck + 1) * TTILE],
                op0=ALU.mult, op1=ALU.add),
                reads=[n_xpv, n_xt, n_consts], writes=[n_mix])
        def zmm(e, lo, hi, pb):
            last = None
            for ck in range(CK):
                last = e.matmul(psP[pb][:hi - lo, :TTILE], w1s[:, ck * 160 + lo: ck * 160 + hi],
                                mix[:, ck * TTILE:(ck + 1) * TTILE],
                                start=(ck == 0), stop=(ck == CK - 1))
            return last
        for pi, (lo, hi) in enumerate(((0, 64), (64, 128), (128, 160))):
            pb = pi % 2
            p.op("tensor", lambda e, lo=lo, hi=hi, pb=pb: zmm(e, lo, hi, pb),
                 reads=[n_w1, n_mix], writes=[nP[pb]])
            for f in range(lo // 32, hi // 32):
                fb = (f * 32 - lo)
                p.op("scalar", lambda e, f=f, fb=fb, pb=pb: e.activation(
                    out=zb[f][:], in_=psP[pb][fb:fb + 32, :TTILE], func=AF.Tanh),
                    reads=[nP[pb]], writes=[n_zb[f]])

        def mk_mix(f, key):
            wf = f % 2
            p.dma("gpsimd", w2r[wf][:], w2[f], writes=[n_w2r[wf]])
            for ck in range(CK):
                pb = ck % 2
                p.op("tensor", lambda e, f=f, ck=ck, pb=pb, wf=wf: e.matmul(
                    psP[pb][:, :TTILE], w2r[wf][:, ck * 128:(ck + 1) * 128],
                    zb[f][:], start=True, stop=True),
                    reads=[n_w2r[wf], n_zb[f]], writes=[nP[pb]])
                p.op("vector", lambda e, ck=ck, key=key, pb=pb: e.scalar_tensor_tensor(
                    out=tmp1[pb][:], in0=psP[pb][:, :TTILE], scalar=maas[key][:, ck:ck + 1],
                    in1=xpv[:, ck * TTILE:(ck + 1) * TTILE], op0=ALU.add, op1=ALU.mult),
                    reads=[nP[pb], n_consts, n_xpv], writes=[n_tmp1[pb]])
                p.op("vector", lambda e, ck=ck, pb=pb: e.tensor_add(
                    mix[:, ck * TTILE:(ck + 1) * TTILE], tmp1[pb][:],
                    xt[:, ck * TTILE:(ck + 1) * TTILE]),
                    reads=[n_tmp1[pb], n_xt], writes=[n_mix])

        mk_mix(1, "k")
        def k_out(ac, ps, nps):
            p.op("vector", lambda e, ac=ac, ps=ps: e.tensor_copy(
                out=kpl[:, ac * TTILE:(ac + 1) * TTILE], in_=ps[:, :TTILE]),
                reads=[nps], writes=[n_kpl])
        proj_full(WkT, n_mix, k_out)
        mk_mix(2, "v")
        def v_out(ac, ps, nps):
            p.op("scalar", lambda e, ac=ac, ps=ps: e.activation(
                out=vpl[:, ac * TTILE:(ac + 1) * TTILE], in_=ps[:, :TTILE],
                func=AF.Copy), reads=[nps], writes=[n_vpl])
        proj_full(WvT, n_mix, v_out)
        for ac in range(AC):
            for tc in range(TTILE // 128):
                pb = (ac * 2 + tc) % 2
                p.op("tensor", lambda e, ac=ac, tc=tc, pb=pb: e.transpose(
                    psP[pb][:, 128:256], vpl[:, ac * TTILE + tc * 128: ac * TTILE + (tc + 1) * 128],
                    idfs[:]), reads=[n_vpl, n_consts], writes=[nP[pb]])
                p.op("vector", lambda e, ac=ac, tc=tc, pb=pb: e.tensor_copy(
                    out=v_tok[:, tc * 1024 + ac * 128: tc * 1024 + (ac + 1) * 128],
                    in_=psP[pb][:, 128:256]), reads=[nP[pb]], writes=[n_vtok])

        def tok_proj(wT, dst, dst_node, act_silu):
            for ac in range(AC):
                wb = ac % 2
                p.dma("gpsimd", wring[wb][:].rearrange("p (k a) -> p k a", k=16),
                      wT[:, ac * 128:(ac + 1) * 128].rearrange("(k p) a -> p k a", p=128),
                      writes=[n_wring[wb]])
                for tc in range(TTILE // 128):
                    pb = (ac * 2 + tc) % 2

                    def mm(e, wb=wb, tc=tc, pb=pb):
                        last = None
                        for ck in range(CK):
                            last = e.matmul(
                                psP[pb][:, :128],
                                mix[:, ck * TTILE + tc * 128: ck * TTILE + (tc + 1) * 128],
                                wring[wb][:, ck * 128:(ck + 1) * 128],
                                start=(ck == 0), stop=(ck == CK - 1))
                        return last
                    p.op("tensor", mm, reads=[n_wring[wb], n_mix], writes=[nP[pb]])
                    if act_silu:
                        p.op("scalar", lambda e, tc=tc, ac=ac, pb=pb: e.activation(
                            out=dst[:, tc * 1024 + ac * 128: tc * 1024 + (ac + 1) * 128],
                            in_=psP[pb][:, :128], func=AF.Silu),
                            reads=[nP[pb]], writes=[dst_node])
                    else:
                        p.op("vector", lambda e, tc=tc, ac=ac, pb=pb: e.tensor_copy(
                            out=dst[:, tc * 1024 + ac * 128: tc * 1024 + (ac + 1) * 128],
                            in_=psP[pb][:, :128]),
                            reads=[nP[pb]], writes=[dst_node])
        mk_mix(3, "r")
        def r_out(ac, ps, nps):
            p.op("vector", lambda e, ac=ac, ps=ps: e.tensor_copy(
                out=rpl[:, ac * TTILE:(ac + 1) * TTILE], in_=ps[:, :TTILE]),
                reads=[nps], writes=[n_rpl])
        proj_full(WrT, n_mix, r_out)
        mk_mix(4, "g")
        tok_proj(WgT, g_tok, n_gtok, True)
        mk_mix(0, "w")
        def ddmm(e):
            last = None
            for ck in range(CK):
                last = e.matmul(ps1[:64, :TTILE], dw1s[:, ck * 64:(ck + 1) * 64],
                                mix[:, ck * TTILE:(ck + 1) * TTILE],
                                start=(ck == 0), stop=(ck == CK - 1))
            return last
        p.op("tensor", ddmm, reads=[n_dw1, n_mix], writes=[nP[0]])
        p.op("scalar", lambda e: e.activation(out=zd[:], in_=ps1[:64, :TTILE], func=AF.Tanh),
             reads=[nP[0]], writes=[n_zd])
        for ac in range(AC):
            pb = ac % 2
            p.op("tensor", lambda e, ac=ac, pb=pb: e.matmul(
                psP[pb][:, :TTILE], dw2s[:, ac * 128:(ac + 1) * 128], zd[:],
                start=True, stop=True), reads=[n_dw2, n_zd], writes=[nP[pb]])
            p.op("scalar", lambda e, ac=ac, pb=pb: e.activation(
                out=targ[pb][:], in_=psP[pb][:, :TTILE], func=AF.Exp, bias=tds[:, ac:ac + 1]),
                reads=[nP[pb], n_consts], writes=[n_targ[pb]])
            p.op("vector", lambda e, ac=ac, pb=pb: e.tensor_scalar(
                out=logw[:, ac * TTILE:(ac + 1) * TTILE], in0=targ[pb][:],
                scalar1=CLAMP, scalar2=-1.0, op0=ALU.min, op1=ALU.mult),
                reads=[n_targ[pb]], writes=[n_logw])

        for ac in range(AC):
            for ch in range(NCH):
                o = ac * TTILE + ch * 32
                p.op("vector", lambda e, o=o: e.tensor_tensor_scan(
                    out=Bp[:, o:o + 32], data0=logw[:, o:o + 32], data1=logw[:, o:o + 32],
                    initial=0.0, op0=ALU.add, op1=ALU.bypass),
                    reads=[n_logw], writes=[n_B])
            p.op("vector", lambda e, ac=ac: e.memset(
                Bs[:, ac * TTILE: (ac + 1) * TTILE: 32], 0.0), writes=[n_Bs])
            p.op("vector", lambda e, ac=ac: e.tensor_copy(
                out=Bs[:, ac * TTILE:(ac + 1) * TTILE].rearrange("p (c s) -> p c s", s=32)[:, :, 1:32],
                in_=Bp[:, ac * TTILE:(ac + 1) * TTILE].rearrange("p (c s) -> p c s", s=32)[:, :, 0:31]),
                reads=[n_B], writes=[n_Bs])
            for hf, mi in ((0, 7), (1, 23)):
                p.op("vector", lambda e, ac=ac, hf=hf, mi=mi: e.tensor_scalar(
                    out=n15[:, ac * 2 * NCH + hf:(ac + 1) * 2 * NCH:2],
                    in0=Bp[:, ac * TTILE + mi:(ac + 1) * TTILE:32],
                    scalar1=-1.0, scalar2=None, op0=ALU.mult, op1=ALU.bypass),
                    reads=[n_B], writes=[n_cols])
            p.op("scalar", lambda e, ac=ac: e.activation(
                out=dcol[:, ac * NCH:(ac + 1) * NCH],
                in_=Bp[:, ac * TTILE + 31:(ac + 1) * TTILE:32], func=AF.Exp),
                reads=[n_B], writes=[n_cols])
            for hf in range(2):
                p.op("vector", lambda e, ac=ac, hf=hf: e.tensor_tensor(
                    out=kk31[:, ac * 2 * NCH + hf:(ac + 1) * 2 * NCH:2],
                    in0=Bp[:, ac * TTILE + 31:(ac + 1) * TTILE:32],
                    in1=n15[:, ac * 2 * NCH + hf:(ac + 1) * 2 * NCH:2], op=ALU.add),
                    reads=[n_B, n_cols], writes=[n_cols])
            p.op("scalar", lambda e, ac=ac: e.activation(
                out=kk31[:, ac * 2 * NCH:(ac + 1) * 2 * NCH],
                in_=kk31[:, ac * 2 * NCH:(ac + 1) * 2 * NCH],
                func=AF.Exp), reads=[n_cols], writes=[n_cols])
            p.op("vector", lambda e, ac=ac: e.memset(
                refq[:, ac * 2 * NCH:(ac + 1) * 2 * NCH:2], 0.0), writes=[n_cols])
            p.op("vector", lambda e, ac=ac: e.tensor_copy(
                out=refq[:, ac * 2 * NCH + 1:(ac + 1) * 2 * NCH:2],
                in_=Bp[:, ac * TTILE + 15:(ac + 1) * TTILE:32]),
                reads=[n_B], writes=[n_cols])
            p.op("vector", lambda e, ac=ac: e.tensor_copy(
                out=refk[:, ac * 2 * NCH:(ac + 1) * 2 * NCH:2],
                in_=Bp[:, ac * TTILE + 15:(ac + 1) * TTILE:32]),
                reads=[n_B], writes=[n_cols])
            p.op("vector", lambda e, ac=ac: e.tensor_copy(
                out=refk[:, ac * 2 * NCH + 1:(ac + 1) * 2 * NCH:2],
                in_=Bp[:, ac * TTILE + 31:(ac + 1) * TTILE:32]),
                reads=[n_B], writes=[n_cols])

            asl = slice(ac * TTILE, (ac + 1) * TTILE)
            n15v = n15[:, ac * 2 * NCH:(ac + 1) * 2 * NCH].unsqueeze(2).broadcast_to([128, 2 * NCH, 16])
            refqv = refq[:, ac * 2 * NCH:(ac + 1) * 2 * NCH].unsqueeze(2).broadcast_to([128, 2 * NCH, 16])
            refkv = refk[:, ac * 2 * NCH:(ac + 1) * 2 * NCH].unsqueeze(2).broadcast_to([128, 2 * NCH, 16])
            kk31v = kk31[:, ac * 2 * NCH:(ac + 1) * 2 * NCH].unsqueeze(2).broadcast_to([128, 2 * NCH, 16])
            r3 = lambda apx, s=32: apx.rearrange("p (c s) -> p c s", s=s)
            p.op("vector", lambda e, asl=asl, n15v=n15v: e.tensor_tensor(
                out=r3(targ[0][:], 16), in0=r3(Bs[:, asl], 16), in1=n15v, op=ALU.add),
                reads=[n_Bs, n_cols], writes=[n_targ[0]])
            p.op("scalar", lambda e: e.activation(out=targ[0][:], in_=targ[0][:], func=AF.Exp),
                 reads=[n_targ[0]], writes=[n_targ[0]])
            p.op("vector", lambda e, asl=asl: e.tensor_mul(
                q_loc[:, asl], rpl[:, asl], targ[0][:]), reads=[n_rpl, n_targ[0]], writes=[n_qloc])
            p.op("vector", lambda e, asl=asl, n15v=n15v: e.tensor_tensor(
                out=r3(targ[1][:], 16), in0=r3(Bp[:, asl], 16), in1=n15v, op=ALU.add),
                reads=[n_B, n_cols], writes=[n_targ[1]])
            p.op("scalar", lambda e: e.activation(out=targ[1][:], in_=targ[1][:], func=AF.Exp,
                                                  scale=-1.0), reads=[n_targ[1]], writes=[n_targ[1]])
            p.op("vector", lambda e, asl=asl: e.tensor_mul(
                k_loc[:, asl], kpl[:, asl], targ[1][:]), reads=[n_kpl, n_targ[1]], writes=[n_kloc])
            p.op("vector", lambda e, asl=asl, refqv=refqv: e.tensor_tensor(
                out=r3(targ[0][:], 16), in0=r3(Bs[:, asl], 16), in1=refqv, op=ALU.subtract),
                reads=[n_Bs, n_cols], writes=[n_targ[0]])
            p.op("scalar", lambda e: e.activation(out=targ[0][:], in_=targ[0][:], func=AF.Exp),
                 reads=[n_targ[0]], writes=[n_targ[0]])
            p.op("vector", lambda e, asl=asl: e.tensor_mul(
                q_bnd[:, asl], rpl[:, asl], targ[0][:]), reads=[n_rpl, n_targ[0]], writes=[n_qbnd])
            p.op("vector", lambda e, asl=asl, refkv=refkv: e.tensor_tensor(
                out=r3(targ[1][:], 16), in0=r3(Bp[:, asl], 16), in1=refkv, op=ALU.subtract),
                reads=[n_B, n_cols], writes=[n_targ[1]])
            p.op("scalar", lambda e: e.activation(out=targ[1][:], in_=targ[1][:], func=AF.Exp,
                                                  scale=-1.0), reads=[n_targ[1]], writes=[n_targ[1]])
            p.op("vector", lambda e, asl=asl: e.tensor_mul(
                k_bnd[:, asl], kpl[:, asl], targ[1][:]), reads=[n_kpl, n_targ[1]], writes=[n_kbnd])
            p.op("scalar", lambda e, asl=asl: e.activation(out=targ[0][:], in_=Bs[:, asl], func=AF.Exp),
                 reads=[n_Bs], writes=[n_targ[0]])
            p.op("vector", lambda e, asl=asl: e.tensor_mul(
                qpp[:, asl], rpl[:, asl], targ[0][:]), reads=[n_rpl, n_targ[0]], writes=[n_qpp])
            p.op("vector", lambda e, asl=asl, kk31v=kk31v: e.tensor_tensor(
                out=r3(kpp[:, asl], 16), in0=r3(k_loc[:, asl], 16), in1=kk31v, op=ALU.mult),
                reads=[n_kloc, n_cols], writes=[n_kpp])

        for ac in range(AC):
            for tc in range(TTILE // 128):
                pb = (ac * 2 + tc) % 2
                p.op("tensor", lambda e, ac=ac, tc=tc, pb=pb: e.transpose(
                    psP[pb][:, 128:256], kpp[:, ac * TTILE + tc * 128: ac * TTILE + (tc + 1) * 128],
                    idfs[:]), reads=[n_kpp, n_consts], writes=[nP[pb]])
                p.op("vector", lambda e, ac=ac, tc=tc, pb=pb: e.tensor_copy(
                    out=kp_tok[:, tc * 1024 + ac * 128: tc * 1024 + (ac + 1) * 128],
                    in_=psP[pb][:, 128:256]), reads=[nP[pb]], writes=[n_kptok])

        for ac in range(AC):
            p.op("vector", lambda e, ac=ac: e.tensor_mul(
                erk[:], rpl[:, ac * TTILE:(ac + 1) * TTILE], kpl[:, ac * TTILE:(ac + 1) * TTILE]),
                reads=[n_rpl, n_kpl], writes=[n_erk])
            p.op("tensor", lambda e, ac=ac: e.matmul(
                ps1[:16, :TTILE], BCs[:, ac * 16:(ac + 1) * 16], erk[:],
                start=(ac == 0), stop=(ac == AC - 1)),
                reads=[n_consts, n_erk], writes=[nP[0]])
        p.op("vector", lambda e: e.tensor_copy(out=drows[:], in_=ps1[:16, :TTILE]),
             reads=[nP[0]], writes=[n_drows])
        for tc in range(TTILE // 128):
            pb = tc % 2
            p.op("tensor", lambda e, tc=tc, pb=pb: e.transpose(
                psP[pb][:, :16], drows[:, tc * 128:(tc + 1) * 128], idfs[0:16, 0:16]),
                reads=[n_drows, n_consts], writes=[nP[pb]])
            p.op("vector", lambda e, tc=tc, pb=pb: e.tensor_copy(
                out=d_tok[:, tc * 16:(tc + 1) * 16], in_=psP[pb][:, :16]),
                reads=[nP[pb]], writes=[n_dtok])

        for grp in range(TTILE // 128):
            sp = slice(grp * 128, (grp + 1) * 128)
            for ac in range(AC):
                gsl0 = ac * TTILE + grp * 128
                gsl = slice(gsl0, gsl0 + 128)

                def amm(e, gsl=gsl, qpl=q_loc, kplx=k_loc):
                    e.matmul(psA0[:, :128], kplx[0:64, gsl], qpl[0:64, gsl], start=True, stop=True)
                    return e.matmul(psA1[:, :128], kplx[64:128, gsl], qpl[64:128, gsl],
                                    start=True, stop=True, tile_position=(64, 0))
                p.op("tensor", lambda e, gsl=gsl: amm(e, gsl, q_loc, k_loc),
                     reads=[n_qloc, n_kloc], writes=[n_psA0, n_psA1])
                p.op("vector", lambda e: e.tensor_mul(A1[:], psA0[:, :128], mlocs[:]),
                     reads=[n_psA0, n_consts], writes=[n_A1])
                p.op("vector", lambda e: e.tensor_mul(A1b[:], psA1[:, :128], mlocs[:]),
                     reads=[n_psA1, n_consts], writes=[n_A1b])
                p.op("tensor", lambda e, gsl=gsl: amm(e, gsl, q_bnd, k_bnd),
                     reads=[n_qbnd, n_kbnd], writes=[n_psA0, n_psA1])
                p.op("vector", lambda e: e.tensor_mul(A2[:], psA0[:, :128], mbnds[:]),
                     reads=[n_psA0, n_consts], writes=[n_A2])
                p.op("vector", lambda e: e.tensor_mul(A2b[:], psA1[:, :128], mbnds[:]),
                     reads=[n_psA1, n_consts], writes=[n_A2b])

                def intramm(e, ac=ac, grp=grp):
                    vA = v_tok[:, grp * 1024 + ac * 128: grp * 1024 + ac * 128 + 64]
                    vB = v_tok[:, grp * 1024 + ac * 128 + 64: grp * 1024 + (ac + 1) * 128]
                    e.matmul(ps_out[:, 0:64], A1[:], vA, start=True, stop=False)
                    e.matmul(ps_out[:, 64:128], A1b[:], vB, start=False, stop=False)
                    e.matmul(ps_out[:, 64:128], A2b[:], vB, start=False, stop=False)
                    return e.matmul(ps_out[:, 0:64], A2[:], vA, start=False, stop=False)
                p.op("tensor", intramm,
                     reads=[n_A1, n_A1b, n_A2, n_A2b, n_vtok],
                     writes=[n_psout])
                for ch in range(4):
                    cidx = grp * 4 + ch
                    co = ac * TTILE + grp * 128 + ch * 32

                    def interA(e, ac=ac, ch=ch, co=co):
                        return e.matmul(ps_out[ch * 32:(ch + 1) * 32, 0:64],
                                        qpp[0:64, co:co + 32], Ssb[0:64, ac * 64:(ac + 1) * 64],
                                        start=False, stop=False, tile_position=(0, ch * 32))
                    p.op("tensor", interA, reads=[n_qpp, n_S[ac]], writes=[n_psout])

                    def interB(e, ac=ac, ch=ch, co=co):
                        return e.matmul(psA1[ch * 32:(ch + 1) * 32, 0:64],
                                        qpp[64:128, co:co + 32], Ssb[64:128, ac * 64:(ac + 1) * 64],
                                        start=True, stop=True, tile_position=(64, ch * 32))
                    p.op("tensor", interB, reads=[n_qpp, n_S[ac]], writes=[n_psA1])

                    def stmm(e, ac=ac, grp=grp, ch=ch):
                        rows = slice(ch * 32, (ch + 1) * 32)
                        kA = kp_tok[rows, grp * 1024 + ac * 128: grp * 1024 + ac * 128 + 64]
                        kB = kp_tok[rows, grp * 1024 + ac * 128 + 64: grp * 1024 + (ac + 1) * 128]
                        vA = v_tok[rows, grp * 1024 + ac * 128: grp * 1024 + ac * 128 + 64]
                        vB = v_tok[rows, grp * 1024 + ac * 128 + 64: grp * 1024 + (ac + 1) * 128]
                        e.matmul(ps_st[ch][0:64, 0:64], kA, vA, start=True, stop=True,
                                 tile_position=(ch * 32, 0))
                        return e.matmul(ps_st[ch][64:128, 0:64], kB, vB, start=True, stop=True,
                                        tile_position=(ch * 32, 64))
                    p.op("tensor", stmm, reads=[n_kptok, n_vtok], writes=[n_psst[ch]])
                    p.op("vector", lambda e, ac=ac, cidx=cidx, ch=ch: e.scalar_tensor_tensor(
                        out=Ssb[:, ac * 64:(ac + 1) * 64], in0=Ssb[:, ac * 64:(ac + 1) * 64],
                        scalar=dcol[:, ac * NCH + cidx: ac * NCH + cidx + 1],
                        in1=ps_st[ch][:, 0:64], op0=ALU.mult, op1=ALU.add),
                        reads=[n_psst[ch], n_cols], writes=[n_S[ac]])
                def closer(e):
                    return e.matmul(ps_out[:, 0:64], zerocol[:], zrow[:, 0:64],
                                    start=False, stop=True)
                p.op("tensor", closer, reads=[n_consts], writes=[n_psout])
                for hh in range(2):
                    sl_o = slice(grp * 1024 + ac * 128 + hh * 64,
                                 grp * 1024 + ac * 128 + (hh + 1) * 64)
                    p.op("vector", lambda e, sl_o=sl_o, grp=grp, ac=ac, hh=hh: e.scalar_tensor_tensor(
                        out=out_tok[:, sl_o],
                        in0=v_tok[:, sl_o],
                        scalar=d_tok[:, grp * 16 + 2 * ac + hh: grp * 16 + 2 * ac + hh + 1],
                        in1=ps_out[:, hh * 64:(hh + 1) * 64], op0=ALU.mult, op1=ALU.add),
                        reads=[n_vtok, n_dtok, n_psout], writes=[n_outtok])
                    if hh == 1:
                        p.op("vector", lambda e, sl_o=sl_o: e.tensor_add(
                            out_tok[:, sl_o], out_tok[:, sl_o], psA1[:, 0:64]),
                            reads=[n_psA1], writes=[n_outtok])

        for tc in range(TTILE // 128) if not os.environ.get("SKIPGN") else ():
            otv = out_tok[:, tc * 1024:(tc + 1) * 1024]
            p.op("vector", lambda e, tc=tc, otv=otv: e.reduce_sum(
                gsum[:, tc * 16:(tc + 1) * 16],
                otv.rearrange("p (h m) -> p h m", m=64), axis=AX.X),
                reads=[n_outtok], writes=[n_gsum])
            p.op("scalar", lambda e, tc=tc, otv=otv: e.activation(
                out=gsqp[:], in_=otv, func=AF.Square), reads=[n_outtok], writes=[n_gsqp])
            p.op("vector", lambda e, tc=tc: e.reduce_sum(
                gsq[:, tc * 16:(tc + 1) * 16],
                gsqp[:].rearrange("p (h m) -> p h m", m=64), axis=AX.X),
                reads=[n_gsqp], writes=[n_gsq])
            p.op("vector", lambda e, tc=tc: e.tensor_scalar(
                out=gn1[:], in0=gsum[:, tc * 16:(tc + 1) * 16], scalar1=1.0 / 64, scalar2=None,
                op0=ALU.mult, op1=ALU.bypass), reads=[n_gsum], writes=[n_gn1])
            p.op("vector", lambda e: e.tensor_mul(gn2[:], gn1[:], gn1[:]),
                 reads=[n_gn1], writes=[n_gn2])
            p.op("vector", lambda e: e.tensor_scalar(
                out=gn2[:], in0=gn2[:], scalar1=6.4e-4, scalar2=None,
                op0=ALU.subtract, op1=ALU.bypass), reads=[n_gn2], writes=[n_gn2])
            p.op("vector", lambda e, tc=tc: e.scalar_tensor_tensor(
                out=gn2[:], in0=gsq[:, tc * 16:(tc + 1) * 16], scalar=1.0 / 64, in1=gn2[:],
                op0=ALU.mult, op1=ALU.subtract), reads=[n_gsq, n_gn2], writes=[n_gn2])
            p.op("scalar", lambda e: e.activation(out=gn2[:], in_=gn2[:], func=AF.Ln),
                 reads=[n_gn2], writes=[n_gn2])
            p.op("scalar", lambda e: e.activation(out=gn2[:], in_=gn2[:], func=AF.Exp,
                                                  scale=-0.5), reads=[n_gn2], writes=[n_gn2])
            m_v = gn1[:].unsqueeze(2).broadcast_to([128, 16, 64])
            r_v = gn2[:].unsqueeze(2).broadcast_to([128, 16, 64])
            o3 = otv.rearrange("p (h m) -> p h m", m=64)
            p.op("vector", lambda e, o3=o3, m_v=m_v: e.tensor_tensor(
                out=o3, in0=o3, in1=m_v, op=ALU.subtract),
                reads=[n_gn1], writes=[n_outtok])
            p.op("vector", lambda e, o3=o3, r_v=r_v: e.tensor_tensor(
                out=o3, in0=o3, in1=r_v, op=ALU.mult),
                reads=[n_gn2], writes=[n_outtok])
            p.op("vector", lambda e, otv=otv: e.tensor_mul(otv, otv, gamb[:]),
                 reads=[n_gamb], writes=[n_outtok])
            p.op("vector", lambda e, otv=otv: e.tensor_add(otv, otv, betb[:]),
                 reads=[n_betb], writes=[n_outtok])
            p.op("vector", lambda e, tc=tc, otv=otv: e.tensor_mul(
                out_tok[:, tc * 1024:(tc + 1) * 1024], otv, g_tok[:, tc * 1024:(tc + 1) * 1024]),
                reads=[n_gtok], writes=[n_outtok])
        for tc in range(TTILE // 128):
            for ac in range(AC):
                pb = (tc * AC + ac) % 2
                p.op("tensor", lambda e, tc=tc, ac=ac, pb=pb: e.transpose(
                    psP[pb][:, 128:256],
                    out_tok[:, tc * 1024 + ac * 128: tc * 1024 + (ac + 1) * 128], idfs[:]),
                    reads=[n_outtok, n_consts], writes=[nP[pb]])
                p.op("vector", lambda e, tc=tc, ac=ac, pb=pb: e.tensor_copy(
                    out=og_f[:, ac * TTILE + tc * 128: ac * TTILE + (tc + 1) * 128],
                    in_=psP[pb][:, 128:256]), reads=[nP[pb]], writes=[n_ogf])
        for ck in range(CK):
            wb = ck % 2
            p.dma("gpsimd", wring[wb][:, :8 * 128].rearrange("p (k a) -> p k a", k=8),
                  WoT[:, ck * 128:(ck + 1) * 128].rearrange("(k p) a -> p k a", p=128),
                  writes=[n_wring[wb]])

            pb = ck % 2

            def womm(e, wb=wb, pb=pb):
                last = None
                for ac in range(AC):
                    last = e.matmul(psP[pb][:, :TTILE], wring[wb][:, ac * 128:(ac + 1) * 128],
                                    og_f[:, ac * TTILE:(ac + 1) * TTILE],
                                    start=(ac == 0), stop=(ac == AC - 1))
                return last
            p.op("tensor", womm, reads=[n_wring[wb], n_ogf], writes=[nP[pb]])
            p.op("vector", lambda e, ck=ck, pb=pb: e.tensor_copy(out=tmp2[pb][:], in_=psP[pb][:, :TTILE]),
                 reads=[nP[pb]], writes=[n_tmp2[pb]])
            stores.append(p.dma("sync", ypo[ck * 128:(ck + 1) * 128, t0:t0 + TTILE],
                                tmp2[pb][:], reads=[n_tmp2[pb]], store_node=n_store))

    return stores


def build_l1_attn(ntt=NTT):
    nc = bass.Bass("TRN2", target_bir_lowering=False, debug=False,
                   detect_race_conditions=bool(os.environ.get("RACEDET")))
    x2 = nc.dram_tensor("x2", [C, RT], F32, kind="ExternalInput").ap()
    maa = {k: nc.dram_tensor(f"maa_{k}", [128, 16], F32, kind="ExternalInput").ap()
           for k in ("x", "w", "k", "v", "r", "g")}
    w1 = nc.dram_tensor("w1", [C, 160], F32, kind="ExternalInput").ap()
    w2 = nc.dram_tensor("w2", [5, 32, C], F32, kind="ExternalInput").ap()
    dw1 = nc.dram_tensor("dw1", [C, 64], F32, kind="ExternalInput").ap()
    dw2 = nc.dram_tensor("dw2", [64, 1024], F32, kind="ExternalInput").ap()
    tdcol = nc.dram_tensor("tdcol", [128, 8], F32, kind="ExternalInput").ap()
    WrT = nc.dram_tensor("WrT", [C, 1024], F32, kind="ExternalInput").ap()
    WkT = nc.dram_tensor("WkT", [C, 1024], F32, kind="ExternalInput").ap()
    WvT = nc.dram_tensor("WvT", [C, 1024], F32, kind="ExternalInput").ap()
    WgT = nc.dram_tensor("WgT", [C, 1024], F32, kind="ExternalInput").ap()
    WoT = nc.dram_tensor("WoT", [1024, C], F32, kind="ExternalInput").ap()
    ucol = nc.dram_tensor("ucol", [128, 8], F32, kind="ExternalInput").ap()
    gamr = nc.dram_tensor("gamr", [1, 1024], F32, kind="ExternalInput").ap()
    betr = nc.dram_tensor("betr", [1, 1024], F32, kind="ExternalInput").ap()
    ypo = nc.dram_tensor("ypart", [C, RT], F32, kind="ExternalOutput").ap()

    x2ap = lambda ck, a, b: x2[ck * 128:(ck + 1) * 128, a:b]
    with ExitStack() as st:
        p = Prog(nc, st)
        stores = emit_l1_attn(p, nc, st, x2ap, maa, w1, w2, dw1, dw2, tdcol, WrT, WkT, WvT, WgT, WoT, ucol, gamr, betr, ypo, ntt)
        p.fence("sync", stores)
        p.finish()
    return nc

def run_l1_attn(inputs, x2_parts):
    nc = build_l1_attn()
    _ = (inputs["L1_attn_time_maa_x"], inputs["L1_attn_time_maa_w"],
         inputs["L1_attn_time_maa_k"], inputs["L1_attn_time_maa_v"],
         inputs["L1_attn_time_maa_r"], inputs["L1_attn_time_maa_g"])
    g = lambda k: np.asarray(inputs["L1_attn_" + k], np.float32)
    in_maps = []
    for c in range(8):
        b, hh = c // 2, c % 2
        asl = slice(1024 * hh, 1024 * hh + 1024)
        x2f = np.ascontiguousarray(np.concatenate([x2_parts[2 * b], x2_parts[2 * b + 1]], axis=1))
        im = dict(
            x2=x2f,
            w1=np.ascontiguousarray(g("time_maa_w1")),
            w2=np.ascontiguousarray(g("time_maa_w2")),
            dw1=np.ascontiguousarray(g("time_decay_w1")),
            dw2=np.ascontiguousarray(g("time_decay_w2")[:, asl]),
            tdcol=np.ascontiguousarray(g("time_decay")[0, 0, asl].reshape(8, 128).T),
            WrT=np.ascontiguousarray(g("W_r").T[:, asl]),
            WkT=np.ascontiguousarray(g("W_k").T[:, asl]),
            WvT=np.ascontiguousarray(g("W_v").T[:, asl]),
            WgT=np.ascontiguousarray(g("W_g").T[:, asl]),
            WoT=np.ascontiguousarray(g("W_o").T[asl, :]),
            ucol=np.ascontiguousarray(g("time_faaaa")[16 * hh:16 * hh + 16].reshape(8, 128).T),
            gamr=np.ascontiguousarray(g("ln_gamma")[asl].reshape(1, 1024)),
            betr=np.ascontiguousarray(g("ln_beta")[asl].reshape(1, 1024)),
        )
        for k in ("x", "w", "k", "v", "r", "g"):
            im[f"maa_{k}"] = np.ascontiguousarray(g("time_maa_" + k)[0, 0].reshape(16, 128).T)
        in_maps.append(im)
    res = run_bass_kernel_spmd(nc, in_maps, core_ids=list(range(8)))
    return [r["ypart"] for r in res.results]



def emit_combine(p, nc, st, x2, ypA, ypB, x3o):
    sb = lambda name, shape, dt=F32: st.enter_context(nc.sbuf_tensor(uname(name), shape, dt))
    psal = lambda name, shape: st.enter_context(nc.psum_tensor(uname(name), shape, F32))
    x3s = sb("x3s", [128, 16 * 512], F32)
    ta = [sb(f"ta{i}", [128, 512]) for i in range(2)]
    tb = [sb(f"tb{i}", [128, 512]) for i in range(2)]
    sqf = [sb(f"sqf{i}", [128, 512]) for i in range(2)]
    onescol = sb("onescol", [128, 1], F32)
    onescf = sb("onescf", [1, 128], F32)
    rowbuf = sb("rowbuf", [1, 512], F32)
    rowbuf2 = sb("rowbuf2", [1, 512], F32)
    rbsb = sb("rbsb", [128, 512], F32)
    ps_ss = psal("ps_ss", [1, 512])
    ps_ss2 = psal("ps_ss2", [1, 512])
    ps_b = psal("ps_b", [128, 512])
    ps_b2 = psal("ps_b2", [128, 512])
    N = Node
    n_x3s, n_ones, n_onescf = N("x3s"), N("ones"), N("onescf")
    n_ta = nodes("ta", 2)
    n_tb = nodes("tb", 2)
    n_sqf = nodes("sqf", 2)
    n_row, n_row2, n_rbsb = N("row"), N("row2"), N("rbsb")
    n_pss, n_pss2, n_psb, n_psb2 = N("pss"), N("pss2"), N("psb"), N("psb2")
    n_store = N("store")
    stores = []
    p.op("vector", lambda e: e.memset(onescol[:], 1.0), writes=[n_ones])
    p.op("vector", lambda e: e.memset(onescf[:], 1.0), writes=[n_onescf])
    for tg in range(4):
        t0 = tg * 512
        for ck in range(CK):
            b = ck % 2
            p.dma("gpsimd", ta[b][:], x2[ck * 128:(ck + 1) * 128, t0:t0 + 512], writes=[n_ta[b]])
            p.dma("sync", tb[b][:], ypA[ck * 128:(ck + 1) * 128, t0:t0 + 512], writes=[n_tb[b]])
            p.op("vector", lambda e, ck=ck, b=b: e.tensor_add(
                x3s[:, ck * 512:(ck + 1) * 512], ta[b][:], tb[b][:]),
                reads=[n_ta[b], n_tb[b]], writes=[n_x3s])
            p.dma("sync", tb[b][:], ypB[ck * 128:(ck + 1) * 128, t0:t0 + 512], writes=[n_tb[b]])
            p.op("vector", lambda e, ck=ck, b=b: e.tensor_add(
                x3s[:, ck * 512:(ck + 1) * 512], x3s[:, ck * 512:(ck + 1) * 512], tb[b][:]),
                reads=[n_tb[b]], writes=[n_x3s])
        for ck in range(CK):
            sq = ck % 2
            p.op("tensor", lambda e, ck=ck: e.matmul(
                ps_ss[:], onescol[:], x3s[:, ck * 512:(ck + 1) * 512],
                start=(ck == 0), stop=(ck == CK - 1)),
                reads=[n_ones, n_x3s], writes=[n_pss])
            p.op("scalar", lambda e, ck=ck, sq=sq: e.activation(
                out=sqf[sq][:], in_=x3s[:, ck * 512:(ck + 1) * 512], func=AF.Square),
                reads=[n_x3s], writes=[n_sqf[sq]])
            p.op("tensor", lambda e, ck=ck, sq=sq: e.matmul(
                ps_ss2[:], onescol[:], sqf[sq][:],
                start=(ck == 0), stop=(ck == CK - 1)),
                reads=[n_ones, n_sqf[sq]], writes=[n_pss2])
        p.op("vector", lambda e: e.tensor_scalar(
            out=rowbuf[:], in0=ps_ss[:], scalar1=1.0 / 2048.0, scalar2=None,
            op0=ALU.mult, op1=ALU.bypass), reads=[n_pss], writes=[n_row])
        p.op("vector", lambda e: e.tensor_mul(rowbuf2[:], rowbuf[:], rowbuf[:]),
             reads=[n_row], writes=[n_row2])
        p.op("vector", lambda e: e.tensor_scalar(
            out=rowbuf2[:], in0=rowbuf2[:], scalar1=1e-5, scalar2=None,
            op0=ALU.subtract, op1=ALU.bypass), reads=[n_row2], writes=[n_row2])
        p.op("vector", lambda e: e.scalar_tensor_tensor(
            out=rowbuf2[:], in0=ps_ss2[:], scalar=1.0 / 2048.0, in1=rowbuf2[:],
            op0=ALU.mult, op1=ALU.subtract), reads=[n_pss2, n_row2], writes=[n_row2])
        p.op("scalar", lambda e: e.activation(out=rowbuf2[:], in_=rowbuf2[:], func=AF.Ln),
             reads=[n_row2], writes=[n_row2])
        p.op("scalar", lambda e: e.activation(out=rowbuf2[:], in_=rowbuf2[:], func=AF.Exp,
                                              scale=-0.5), reads=[n_row2], writes=[n_row2])
        p.op("tensor", lambda e: e.matmul(ps_b[:], onescf[:], rowbuf[:], start=True, stop=True),
             reads=[n_onescf, n_row], writes=[n_psb])
        p.op("tensor", lambda e: e.matmul(ps_b2[:], onescf[:], rowbuf2[:], start=True, stop=True),
             reads=[n_onescf, n_row2], writes=[n_psb2])
        p.op("vector", lambda e: e.tensor_copy(out=rbsb[:], in_=ps_b2[:]),
             reads=[n_psb2], writes=[n_rbsb])
        for ck in range(CK):
            p.op("vector", lambda e, ck=ck: e.tensor_sub(
                x3s[:, ck * 512:(ck + 1) * 512], x3s[:, ck * 512:(ck + 1) * 512], ps_b[:]),
                reads=[n_psb], writes=[n_x3s])
            p.op("vector", lambda e, ck=ck: e.tensor_mul(
                x3s[:, ck * 512:(ck + 1) * 512], x3s[:, ck * 512:(ck + 1) * 512], rbsb[:]),
                reads=[n_rbsb], writes=[n_x3s])
            stores.append(p.dma(
                "sync", x3o[ck * 128:(ck + 1) * 128, t0:t0 + 512],
                x3s[:, ck * 512:(ck + 1) * 512], reads=[n_x3s], store_node=n_store))
    return stores


def build_combine():
    nc = bass.Bass("TRN2", target_bir_lowering=False, debug=False,
                   detect_race_conditions=bool(os.environ.get("RACEDET")))
    x2 = nc.dram_tensor("x2", [C, TOWN], F32, kind="ExternalInput").ap()
    ypA = nc.dram_tensor("ypA", [C, TOWN], F32, kind="ExternalInput").ap()
    ypB = nc.dram_tensor("ypB", [C, TOWN], F32, kind="ExternalInput").ap()
    x3o = nc.dram_tensor("x3", [C, TOWN], F32, kind="ExternalOutput").ap()

    with ExitStack() as st:
        p = Prog(nc, st)
        stores = emit_combine(p, nc, st, x2, ypA, ypB, x3o)
        p.fence("sync", stores)
        p.finish()
    return nc

def run_combine(x2_parts, yparts):
    nc = build_combine()
    in_maps = []
    for c in range(8):
        b, half = c // 2, c % 2
        sl = slice(half * 2048, (half + 1) * 2048)
        in_maps.append(dict(
            x2=np.ascontiguousarray(x2_parts[c]),
            ypA=np.ascontiguousarray(yparts[2 * b][:, sl]),
            ypB=np.ascontiguousarray(yparts[2 * b + 1][:, sl]),
        ))
    res = run_bass_kernel_spmd(nc, in_maps, core_ids=list(range(8)))
    return [r["x3"] for r in res.results]



def emit_combine2(p, nc, st, x2, ysum, mA, mB, x3o):
    sb = lambda name, shape, dt=F32: st.enter_context(nc.sbuf_tensor(uname(name), shape, dt))
    psal = lambda name, shape: st.enter_context(nc.psum_tensor(uname(name), shape, F32))
    x3s = sb("c2_x3s", [128, 16 * 512], F32)
    ta = [sb(f"c2_ta{i}", [128, 512]) for i in range(2)]
    tb = [sb(f"c2_tb{i}", [128, 512]) for i in range(2)]
    sqf = [sb(f"c2_sqf{i}", [128, 512]) for i in range(2)]
    masks = sb("c2_masks", [128, 2], F32)
    onescol = sb("c2_onescol", [128, 1], F32)
    onescf = sb("c2_onescf", [1, 128], F32)
    rowbuf = sb("c2_rowbuf", [1, 512], F32)
    rowbuf2 = sb("c2_rowbuf2", [1, 512], F32)
    rbsb = sb("c2_rbsb", [128, 512], F32)
    ps_ss = psal("c2_ps_ss", [1, 512])
    ps_ss2 = psal("c2_ps_ss2", [1, 512])
    ps_b = psal("c2_ps_b", [128, 512])
    ps_b2 = psal("c2_ps_b2", [128, 512])
    N = Node
    n_x3s, n_ones, n_onescf, n_masks = N("x3s"), N("ones"), N("onescf"), N("masks")
    n_ta = nodes("c2ta", 2)
    n_tb = nodes("c2tb", 2)
    n_sqf = nodes("c2sqf", 2)
    n_row, n_row2, n_rbsb = N("row"), N("row2"), N("rbsb")
    n_pss, n_pss2, n_psb, n_psb2 = N("pss"), N("pss2"), N("psb"), N("psb2")
    n_store = N("store")
    stores = []
    p.op("vector", lambda e: e.memset(onescol[:], 1.0), writes=[n_ones])
    p.op("vector", lambda e: e.memset(onescf[:], 1.0), writes=[n_onescf])
    p.dma("gpsimd", masks[:, 0:1], mA, writes=[n_masks])
    p.dma("gpsimd", masks[:, 1:2], mB, writes=[n_masks])
    for tg in range(4):
        t0 = tg * 512
        for ck in range(CK):
            b = ck % 2
            p.dma("gpsimd", ta[b][:], x2[ck * 128:(ck + 1) * 128, t0:t0 + 512], writes=[n_ta[b]])
            p.dma("sync", tb[b][:], ysum[ck * 128:(ck + 1) * 128, t0:t0 + 512], writes=[n_tb[b]])
            p.op("vector", lambda e, ck=ck, b=b: e.scalar_tensor_tensor(
                out=x3s[:, ck * 512:(ck + 1) * 512], in0=tb[b][:], scalar=masks[:, 0:1],
                in1=ta[b][:], op0=ALU.mult, op1=ALU.add),
                reads=[n_ta[b], n_tb[b], n_masks], writes=[n_x3s])
            p.dma("sync", tb[b][:], ysum[ck * 128:(ck + 1) * 128, 2048 + t0:2048 + t0 + 512],
                  writes=[n_tb[b]])
            p.op("vector", lambda e, ck=ck, b=b: e.scalar_tensor_tensor(
                out=x3s[:, ck * 512:(ck + 1) * 512], in0=tb[b][:], scalar=masks[:, 1:2],
                in1=x3s[:, ck * 512:(ck + 1) * 512], op0=ALU.mult, op1=ALU.add),
                reads=[n_tb[b], n_masks], writes=[n_x3s])
        for ck in range(CK):
            sq = ck % 2
            p.op("tensor", lambda e, ck=ck: e.matmul(
                ps_ss[:], onescol[:], x3s[:, ck * 512:(ck + 1) * 512],
                start=(ck == 0), stop=(ck == CK - 1)),
                reads=[n_ones, n_x3s], writes=[n_pss])
            p.op("scalar", lambda e, ck=ck, sq=sq: e.activation(
                out=sqf[sq][:], in_=x3s[:, ck * 512:(ck + 1) * 512], func=AF.Square),
                reads=[n_x3s], writes=[n_sqf[sq]])
            p.op("tensor", lambda e, ck=ck, sq=sq: e.matmul(
                ps_ss2[:], onescol[:], sqf[sq][:],
                start=(ck == 0), stop=(ck == CK - 1)),
                reads=[n_ones, n_sqf[sq]], writes=[n_pss2])
        p.op("vector", lambda e: e.tensor_scalar(
            out=rowbuf[:], in0=ps_ss[:], scalar1=1.0 / 2048.0, scalar2=None,
            op0=ALU.mult, op1=ALU.bypass), reads=[n_pss], writes=[n_row])
        p.op("vector", lambda e: e.tensor_mul(rowbuf2[:], rowbuf[:], rowbuf[:]),
             reads=[n_row], writes=[n_row2])
        p.op("vector", lambda e: e.tensor_scalar(
            out=rowbuf2[:], in0=rowbuf2[:], scalar1=1e-5, scalar2=None,
            op0=ALU.subtract, op1=ALU.bypass), reads=[n_row2], writes=[n_row2])
        p.op("vector", lambda e: e.scalar_tensor_tensor(
            out=rowbuf2[:], in0=ps_ss2[:], scalar=1.0 / 2048.0, in1=rowbuf2[:],
            op0=ALU.mult, op1=ALU.subtract), reads=[n_pss2, n_row2], writes=[n_row2])
        p.op("scalar", lambda e: e.activation(out=rowbuf2[:], in_=rowbuf2[:], func=AF.Ln),
             reads=[n_row2], writes=[n_row2])
        p.op("scalar", lambda e: e.activation(out=rowbuf2[:], in_=rowbuf2[:], func=AF.Exp,
                                              scale=-0.5), reads=[n_row2], writes=[n_row2])
        p.op("tensor", lambda e: e.matmul(ps_b[:], onescf[:], rowbuf[:], start=True, stop=True),
             reads=[n_onescf, n_row], writes=[n_psb])
        p.op("tensor", lambda e: e.matmul(ps_b2[:], onescf[:], rowbuf2[:], start=True, stop=True),
             reads=[n_onescf, n_row2], writes=[n_psb2])
        p.op("vector", lambda e: e.tensor_copy(out=rbsb[:], in_=ps_b2[:]),
             reads=[n_psb2], writes=[n_rbsb])
        for ck in range(CK):
            p.op("vector", lambda e, ck=ck: e.tensor_sub(
                x3s[:, ck * 512:(ck + 1) * 512], x3s[:, ck * 512:(ck + 1) * 512], ps_b[:]),
                reads=[n_psb], writes=[n_x3s])
            p.op("vector", lambda e, ck=ck: e.tensor_mul(
                x3s[:, ck * 512:(ck + 1) * 512], x3s[:, ck * 512:(ck + 1) * 512], rbsb[:]),
                reads=[n_rbsb], writes=[n_x3s])
            stores.append(p.dma(
                "sync", x3o[ck * 128:(ck + 1) * 128, t0:t0 + 512],
                x3s[:, ck * 512:(ck + 1) * 512], reads=[n_x3s], store_node=n_store))
    return stores


def build_mega():
    nc = bass.Bass("TRN2", target_bir_lowering=False, debug=False,
                   detect_race_conditions=bool(os.environ.get("RACEDET")))
    xctx = nc.dram_tensor("xctx", [C, TCTX], F32, kind="ExternalInput").ap()
    maskv = nc.dram_tensor("maskv", [128, 32], F32, kind="ExternalInput").ap()
    WqT = nc.dram_tensor("WqT", [C, 2048], F32, kind="ExternalInput").ap()
    WkT0 = nc.dram_tensor("WkT0", [C, 512], F32, kind="ExternalInput").ap()
    WvT0 = nc.dram_tensor("WvT0", [C, 512], F32, kind="ExternalInput").ap()
    WoT0 = nc.dram_tensor("WoT0", [2048, C], F32, kind="ExternalInput").ap()
    qb = nc.dram_tensor("qb", [128, 16], F32, kind="ExternalInput").ap()
    kb = nc.dram_tensor("kb", [128, 4], F32, kind="ExternalInput").ap()
    vbr = nc.dram_tensor("vbr", [1, 512], F32, kind="ExternalInput").ap()
    lnw = nc.dram_tensor("lnw", [128, 16], F32, kind="ExternalInput").ap()
    gup_p = nc.dram_tensor("gup_packed", [GUP_SH, C // 8], I32, kind="ExternalInput").ap()
    gup_s = nc.dram_tensor("gup_scales", [GUP_SH, 16], F16, kind="ExternalInput").ap()
    dwn_p = nc.dram_tensor("down_packed", [DWN_SH, 5632 // 8], I32, kind="ExternalInput").ap()
    dwn_s = nc.dram_tensor("down_scales", [DWN_SH, 44], F16, kind="ExternalInput").ap()
    gidx = nc.dram_tensor("gate_idx", [G_SH2, C], I32, kind="ExternalInput").ap()
    uidx = nc.dram_tensor("up_idx", [G_SH2, C], I32, kind="ExternalInput").ap()
    didx = nc.dram_tensor("down_idx", [GD_SH, 5504], I32, kind="ExternalInput").ap()
    lut = nc.dram_tensor("lut", [1, 16], F32, kind="ExternalInput").ap()
    maa = {k: nc.dram_tensor(f"maa_{k}", [128, 16], F32, kind="ExternalInput").ap()
           for k in ("x", "w", "k", "v", "r", "g")}
    w1 = nc.dram_tensor("w1", [C, 160], F32, kind="ExternalInput").ap()
    w2 = nc.dram_tensor("w2", [5, 32, C], F32, kind="ExternalInput").ap()
    dw1 = nc.dram_tensor("dw1", [C, 64], F32, kind="ExternalInput").ap()
    dw2 = nc.dram_tensor("dw2", [64, 1024], F32, kind="ExternalInput").ap()
    tdcol = nc.dram_tensor("tdcol", [128, 8], F32, kind="ExternalInput").ap()
    WrT = nc.dram_tensor("WrT", [C, 1024], F32, kind="ExternalInput").ap()
    WkT1 = nc.dram_tensor("WkT1", [C, 1024], F32, kind="ExternalInput").ap()
    WvT1 = nc.dram_tensor("WvT1", [C, 1024], F32, kind="ExternalInput").ap()
    WgT1 = nc.dram_tensor("WgT1", [C, 1024], F32, kind="ExternalInput").ap()
    WoT1 = nc.dram_tensor("WoT1", [1024, C], F32, kind="ExternalInput").ap()
    ucol = nc.dram_tensor("ucol", [128, 8], F32, kind="ExternalInput").ap()
    gamr = nc.dram_tensor("gamr", [1, 1024], F32, kind="ExternalInput").ap()
    betr = nc.dram_tensor("betr", [1, 1024], F32, kind="ExternalInput").ap()
    mA = nc.dram_tensor("mA", [128, 1], F32, kind="ExternalInput").ap()
    mB = nc.dram_tensor("mB", [128, 1], F32, kind="ExternalInput").ap()
    x4o = nc.dram_tensor("x4", [C, TOWN], F32, kind="ExternalOutput").ap()

    sh_gu = nc.dram_tensor("sh_gu", [C, GUP_SH], BF16, kind="Internal")
    sh_dn = nc.dram_tensor("sh_dn", [5632, DWN_SH], BF16, kind="Internal")
    sh_g = nc.dram_tensor("sh_g", [C, G_SH2], BF16, kind="Internal")
    sh_u = nc.dram_tensor("sh_u", [C, G_SH2], BF16, kind="Internal")
    sh_d = nc.dram_tensor("sh_d", [5504, GD_SH], BF16, kind="Internal")
    g_gu = nc.dram_tensor("g_gu", [8 * C, GUP_SH], BF16, kind="Internal")
    g_dn = nc.dram_tensor("g_dn", [8 * 5632, DWN_SH], BF16, kind="Internal")
    g_g = nc.dram_tensor("g_g", [8 * C, G_SH2], BF16, kind="Internal")
    g_u = nc.dram_tensor("g_u", [8 * C, G_SH2], BF16, kind="Internal")
    g_d = nc.dram_tensor("g_d", [8 * 5504, GD_SH], BF16, kind="Internal")
    st1 = nc.dram_tensor("st1", [C, TOWN], F32, kind="Internal")
    st2 = nc.dram_tensor("st2", [C, TOWN], F32, kind="Internal")
    x2g = nc.dram_tensor("x2g", [2 * C, TOWN], F32, kind="Internal")
    ypart = nc.dram_tensor("ypart", [C, RT], F32, kind="Internal")
    ysum = nc.dram_tensor("ysum", [C, RT], F32, kind="Internal")
    st3 = nc.dram_tensor("st3", [C, TOWN], F32, kind="Internal")

    PAIRS = [[0, 1], [2, 3], [4, 5], [6, 7]]
    ALL8 = [list(range(8))]

    with ExitStack() as stp:
        p = Prog(nc, stp)
        n_cc = Node("ccchain")

        def coll(kind, op, groups, inap, outap, rd_nodes, wr_node):
            p.op("gpsimd", lambda e: e.collective_compute(
                kind, op, replica_groups=groups, ins=[inap.ap().opt()],
                outs=[outap.ap().opt()]),
                reads=list(rd_nodes) + [n_cc], writes=[wr_node, n_cc], sigk=1)

        n_shards = Node("shards")
        n_gath = Node("gathered")
        n_st1, n_st2, n_x2g, n_yp, n_ys, n_st3 = (
            Node("st1"), Node("st2"), Node("x2g"), Node("yp"), Node("ys"), Node("st3"))

        with ExitStack() as st:
            stores = emit_dequant(p, nc, st, gup_p, gup_s, dwn_p, dwn_s, gidx, uidx,
                                  didx, lut, sh_gu.ap(), sh_dn.ap(), sh_g.ap(),
                                  sh_u.ap(), sh_d.ap())
            for ev in stores:
                n_shards.writers[ev[0]] = max(n_shards.writers.get(ev[0], 0), ev[1])
            coll("AllGather", ALU.bypass, ALL8, sh_gu, g_gu, [n_shards], n_gath)
            coll("AllGather", ALU.bypass, ALL8, sh_dn, g_dn, [n_shards], n_gath)
            coll("AllGather", ALU.bypass, ALL8, sh_g, g_g, [n_shards], n_gath)
            coll("AllGather", ALU.bypass, ALL8, sh_u, g_u, [n_shards], n_gath)
            coll("AllGather", ALU.bypass, ALL8, sh_d, g_d, [n_shards], n_gath)
            p.barrier_all()
        with ExitStack() as st:
            stores = emit_l0_attn(p, nc, st, xctx, WqT, WkT0, WvT0, WoT0, qb, kb, vbr,
                                  lnw, maskv, st1.ap(), 0, 4)
            for ev in stores:
                n_st1.writers[ev[0]] = max(n_st1.writers.get(ev[0], 0), ev[1])
            p.barrier_all()
        with ExitStack() as st:
            stores = emit_ffn(p, nc, st, st1.ap(), g_gu.ap()[0:4 * 2048, :],
                              g_gu.ap()[4 * 2048:8 * 2048, :], g_dn.ap(),
                              st2.ap(), 5632, True, 1408)
            for ev in stores:
                n_st2.writers[ev[0]] = max(n_st2.writers.get(ev[0], 0), ev[1])
            p.barrier_all()
        coll("AllGather", ALU.bypass, PAIRS, st2, x2g, [n_st2], n_x2g)
        p.fence("gpsimd", list(n_x2g.writers.items()))
        with ExitStack() as st:
            x2ap = lambda ck, a, b: (
                x2g.ap()[ck * 128:(ck + 1) * 128, a:b] if b <= 2048 else
                x2g.ap()[2048 + ck * 128: 2048 + (ck + 1) * 128, a - 2048:b - 2048])
            stores = emit_l1_attn(p, nc, st, x2ap, maa, w1, w2, dw1, dw2, tdcol,
                                  WrT, WkT1, WvT1, WgT1, WoT1, ucol, gamr, betr,
                                  ypart.ap(), NTT)
            for ev in stores:
                n_yp.writers[ev[0]] = max(n_yp.writers.get(ev[0], 0), ev[1])
            coll("AllReduce", ALU.add, PAIRS, ypart, ysum, [n_yp], n_ys)
            p.barrier_all()
        with ExitStack() as st:
            stores = emit_combine2(p, nc, st, st2.ap(), ysum.ap(), mA, mB, st3.ap())
            for ev in stores:
                n_st3.writers[ev[0]] = max(n_st3.writers.get(ev[0], 0), ev[1])
            p.barrier_all()
        with ExitStack() as st:
            stores = emit_ffn(p, nc, st, st3.ap(), g_g.ap(), g_u.ap(), g_d.ap(),
                              x4o, 5504, False, 768)
            p.fence("sync", stores)
        p.finish()
    return nc


def mega_in_maps(inputs):
    x_parts = split_x_parts(inputs["L0_attn_x"])
    dq_maps = dequant_in_maps(inputs)
    qbh = np.ascontiguousarray(np.asarray(inputs["L0_attn_q_bias"]).reshape(16, 128).T)
    kbh = np.ascontiguousarray(np.asarray(inputs["L0_attn_k_bias"]).reshape(4, 128).T)
    vbh = np.ascontiguousarray(np.asarray(inputs["L0_attn_v_bias"]).reshape(1, 512))
    lnwh = np.ascontiguousarray(np.asarray(inputs["L0_attn_ln_weight"]).reshape(16, 128).T)
    WqTh = np.ascontiguousarray(np.asarray(inputs["L0_attn_q_project"]).T)
    WkTh = np.ascontiguousarray(np.asarray(inputs["L0_attn_k_project"]).T)
    WvTh = np.ascontiguousarray(np.asarray(inputs["L0_attn_v_project"]).T)
    WoTh = np.ascontiguousarray(np.asarray(inputs["L0_attn_o_project"]).T)
    mask_pad = np.zeros((4096,), np.float32)
    mask_pad[:2048] = -1e30
    mask_pad = np.ascontiguousarray(mask_pad.reshape(32, 128).T)
    mask_zero = np.zeros((128, 32), np.float32)
    g = lambda k: np.asarray(inputs["L1_attn_" + k], np.float32)
    maps = []
    for c in range(8):
        b, half = c // 2, c % 2
        if half == 0:
            ctxc = np.concatenate([np.zeros_like(x_parts[c]), x_parts[c]], axis=1)
            mv = mask_pad
            mAv, mBv = 1.0, 0.0
        else:
            ctxc = np.concatenate([x_parts[c - 1], x_parts[c]], axis=1)
            mv = mask_zero
            mAv, mBv = 0.0, 1.0
        asl = slice(1024 * half, 1024 * half + 1024)
        im = dict(dq_maps[c])
        im.update(
            xctx=np.ascontiguousarray(ctxc), maskv=mv,
            WqT=WqTh, WkT0=WkTh, WvT0=WvTh, WoT0=WoTh,
            qb=qbh, kb=kbh, vbr=vbh, lnw=lnwh,
            w1=np.ascontiguousarray(g("time_maa_w1")),
            w2=np.ascontiguousarray(g("time_maa_w2")),
            dw1=np.ascontiguousarray(g("time_decay_w1")),
            dw2=np.ascontiguousarray(g("time_decay_w2")[:, asl]),
            tdcol=np.ascontiguousarray(g("time_decay")[0, 0, asl].reshape(8, 128).T),
            WrT=np.ascontiguousarray(g("W_r").T[:, asl]),
            WkT1=np.ascontiguousarray(g("W_k").T[:, asl]),
            WvT1=np.ascontiguousarray(g("W_v").T[:, asl]),
            WgT1=np.ascontiguousarray(g("W_g").T[:, asl]),
            WoT1=np.ascontiguousarray(g("W_o").T[asl, :]),
            ucol=np.ascontiguousarray(g("time_faaaa")[16 * half:16 * half + 16].reshape(8, 128).T),
            gamr=np.ascontiguousarray(g("ln_gamma")[asl].reshape(1, 1024)),
            betr=np.ascontiguousarray(g("ln_beta")[asl].reshape(1, 1024)),
            mA=np.full((128, 1), mAv, np.float32),
            mB=np.full((128, 1), mBv, np.float32),
        )
        for k in ("x", "w", "k", "v", "r", "g"):
            im[f"maa_{k}"] = np.ascontiguousarray(g("time_maa_" + k)[0, 0].reshape(16, 128).T)
        maps.append(im)
    return maps


def kernel(**inputs):
    nc = build_mega()
    res = run_bass_kernel_spmd(nc, mega_in_maps(inputs), core_ids=list(range(8)))
    out = np.empty((4, 4096, 2048), np.float32)
    for c in range(8):
        b, half = c // 2, c % 2
        out[b, half * 2048:(half + 1) * 2048, :] = res.results[c]["x4"].T
    return out


def kernel_multi(**inputs):
    dq = run_dequant(inputs)
    x_parts = split_x_parts(inputs["L0_attn_x"])
    x1_parts = run_l0_attn(inputs, x_parts)
    gug = dq["w_guT_g"]
    x2_parts = run_ffn(x1_parts,
                       np.ascontiguousarray(gug[:4 * 2048]),
                       np.ascontiguousarray(gug[4 * 2048:]),
                       dq["w_downT_g"], 5632, True, 1408)
    yparts = run_l1_attn(inputs, x2_parts)
    x3_parts = run_combine(x2_parts, yparts)
    x4_parts = run_ffn(x3_parts,
                       dq["w_gateT_g"], dq["w_upT_g"], dq["w_downgT_g"],
                       5504, False, 768)
    out = np.empty((4, 4096, 2048), np.float32)
    for c in range(8):
        b, half = c // 2, c % 2
        out[b, half * 2048:(half + 1) * 2048, :] = x4_parts[c].T
    return out


# version 08
# speedup vs baseline: 1.1266x; 1.1266x over previous
import os
import numpy as np
import ml_dtypes
from contextlib import ExitStack

import concourse.bass as bass
import concourse.mybir as mybir
from concourse.bass_utils import run_bass_kernel_spmd

F32 = mybir.dt.float32
BF16 = mybir.dt.bfloat16
F16 = mybir.dt.float16
I32 = mybir.dt.int32
AF = mybir.ActivationFunctionType
ALU = mybir.AluOpType
BF = ml_dtypes.bfloat16
AX = mybir.AxisListType

ENGINES = ("sync", "gpsimd", "tensor", "vector", "scalar")


def cdiv(a, b):
    return -(-a // b)


_UNIQ = [0]


def uname(n):
    _UNIQ[0] += 1
    return f"{n}_{_UNIQ[0]}"


class Ch:
    def __init__(self, nc, st, name):
        self.sem = st.enter_context(nc.semaphore(name))
        self.n = 0
        self.name = name


class Node:

    __slots__ = ("writers", "readers", "name", "_sem")

    def __init__(self, name=""):
        self.writers = {}
        self.readers = {}
        self.name = name
        self._sem = None

    def deps_for_read(self):
        return list(self.writers.items())

    def deps_for_write(self):
        return list(self.writers.items()) + list(self.readers.items())


def nodes(prefix, n):
    return [Node(f"{prefix}{i}") for i in range(n)]


class Prog:

    ENG_CH = {"sync": "cs", "gpsimd": "cg", "tensor": "cte", "vector": "cve", "scalar": "cca"}

    def __init__(self, nc, st):
        self.nc = nc
        self.st = st
        self.q = {e: [] for e in ENGINES}
        self.wm = {e: {} for e in ENGINES}
        self.ch = {e: Ch(nc, st, self.ENG_CH[e]) for e in ENGINES}

    def _waits(self, eng, waits):
        out = []
        wm = self.wm[eng]
        for ch, v in waits:
            if v > 0 and wm.get(ch, 0) < v:
                wm[ch] = v
                out.append((ch.sem, v))
        return out

    def op(self, eng, fn, reads=(), writes=(), waits=(), sigk=1):
        ch = self.ch[eng]
        allw = list(waits)
        for n in reads:
            allw += n.deps_for_read()
        for n in writes:
            allw += n.deps_for_write()
        ws = self._waits(eng, allw)
        ch.n += sigk
        val = ch.n

        def run(e):
            for sem, v in ws:
                e.wait_ge(sem, v)
            inst = fn(e)
            assert inst is not None, "op returned no instruction"
            inst.then_inc(ch.sem, sigk)

        self.q[eng].append(run)
        ev = (ch, val)
        for n in reads:
            n.readers[ch] = val
        for n in writes:
            n.writers[ch] = val
        return ev

    def node_ch(self, node):
        if not hasattr(node, "_sem") or node._sem is None:
            node._sem = Ch(self.nc, self.st, f"nd_{node.name}_{id(node) % 9999}")
        return node._sem

    def dma(self, eng, out, in_, reads=(), writes=(), waits=(), store_node=None, **kw):
        sig = self.node_ch(writes[0] if writes else store_node)
        allw = list(waits)
        for n in reads:
            allw += n.deps_for_read()
        for n in writes:
            allw += n.deps_for_write()
        ws = self._waits(eng, allw)
        sig.n += 16
        val = sig.n

        def run(e):
            for sem, v in ws:
                e.wait_ge(sem, v)
            e.dma_start(out=out, in_=in_, **kw).then_inc(sig.sem, 16)

        self.q[eng].append(run)
        ev = (sig, val)
        for n in reads:
            n.readers[sig] = val
        for n in writes:
            n.writers[sig] = val
        return ev

    def barrier_all(self):
        evs = [(c, c.n) for c in self.ch.values()]
        for eng in ENGINES:
            ws = self._waits(eng, evs)

            def run(e, ws=ws):
                for sem, v in ws:
                    e.wait_ge(sem, v)
            self.q[eng].append(run)

    def fence(self, eng, evs):
        evs = [e for e in evs if e]
        ws = self._waits(eng, evs)
        own = [(c, v) for c, v in evs if c is self.ch[eng]]
        mx = max([v for _, v in own], default=None)

        def run(e):
            for sem, v in ws:
                e.wait_ge(sem, v)
            if mx is not None:
                e.wait_ge(self.ch[eng].sem, mx)

        self.q[eng].append(run)

    def finish(self):
        with self.nc.Block() as block:
            for eng in ENGINES:
                ops = self.q[eng]
                getattr(block, eng)(lambda e, ops=ops: [f(e) for f in ops])



GUP_SH = 1408
DWN_SH = 256
G_SH = 688
G_SH2 = 768
GD_SH = 256


def emit_dequant(p, nc, st, gup_p, gup_s, dwn_p, dwn_s, gidx, uidx, didx, lut, o_gup, o_dwn, o_g, o_u, o_d):
    CB = 2048
    sbuf = lambda name, shape, dt: st.enter_context(nc.sbuf_tensor(uname(name), shape, dt))
    pk = [sbuf(f"pk{i}", [128, CB // 8], I32) for i in range(2)]
    scl = [sbuf(f"scl{i}", [128, CB // 128], F32) for i in range(2)]
    nib = sbuf("nib", [128, CB], I32)
    t_m = sbuf("t_m", [128, CB], I32)
    t_e = sbuf("t_e", [128, CB], I32)
    t_f = sbuf("t_f", [128, CB], F32)
    t_g = sbuf("t_g", [128, CB], F32)
    val = sbuf("val", [128, CB], F32)
    obuf = [sbuf(f"obuf{i}", [128, CB], BF16) for i in range(2)]
    idxb = [sbuf(f"idxb{i}", [128, CB], I32) for i in range(2)]
    idxf = sbuf("idxf", [128, CB], BF16)
    gtmp = sbuf("gtmp", [128, CB], BF16)
    gacc = [sbuf(f"gacc{i}", [128, CB], BF16) for i in range(2)]
    lutr = sbuf("lutr", [1, 16], F32)
    lutb = sbuf("lutb", [128, 16], F32)
    ones1 = sbuf("ones1", [1, 128], F32)
    idbs = sbuf("idbs", [128, 128], BF16)
    tbuf = [sbuf(f"tbuf{i}", [128, CB], BF16) for i in range(2)]
    pslut = st.enter_context(nc.psum_tensor(uname("pslut"), [128, 16], F32))
    ps_tr = st.enter_context(nc.psum_tensor(uname("ps_tr"), [128, 1024], BF16))
    idd = nc.inline_tensor(np.eye(128, dtype=np.float32).astype(BF), name=uname("idd")).ap()

    n_pk = nodes("pk", 2)
    n_scl = nodes("scl", 2)
    n_obuf = nodes("obuf", 2)
    n_idxb = nodes("idxb", 2)
    n_gacc = nodes("gacc", 2)
    n_lutr, n_ones, n_pslut, n_lutb = Node("lutr"), Node("ones"), Node("pslut"), Node("lutb")
    n_id, n_pstr = Node("id"), Node("pstr")
    n_tbuf = nodes("tbuf", 2)
    n_store = Node("stores")
    stores = []
    p.dma("gpsimd", idbs[:], idd, writes=[n_id])

    p.dma("sync", lutr[:], lut, writes=[n_lutr])
    p.op("vector", lambda e: e.memset(ones1[:], 1.0), writes=[n_ones])
    p.op("tensor", lambda e: e.matmul(pslut[:], ones1[:], lutr[:], start=True, stop=True),
         reads=[n_lutr, n_ones], writes=[n_pslut])
    p.op("vector", lambda e: e.tensor_copy(out=lutb[:], in_=pslut[:]),
         reads=[n_pslut], writes=[n_lutb])

    ti = 0

    def tr_store(srcbuf, n_src, dstT, rows, cols, b):
        nblk = cols // 128
        for tb_ in range(nblk):
            p.op("tensor", lambda e, tb_=tb_, srcbuf=srcbuf, rows=rows: e.transpose(
                ps_tr[:, tb_ % 2 * 128: tb_ % 2 * 128 + rows],
                srcbuf[:rows, tb_ * 128:(tb_ + 1) * 128], idbs[:rows, :rows]),
                reads=[n_src, n_id], writes=[n_pstr])
            p.op("vector", lambda e, tb_=tb_, rows=rows, b=b: e.tensor_copy(
                out=tbuf[b][:, tb_ * rows:(tb_ + 1) * rows],
                in_=ps_tr[:, tb_ % 2 * 128: tb_ % 2 * 128 + rows]),
                reads=[n_pstr], writes=[n_tbuf[b]])
        stores.append(p.dma(
            "sync", dstT.rearrange("(nb q) r -> q nb r", q=128),
            tbuf[b][:, :nblk * rows].rearrange("q (nb r) -> q nb r", r=rows),
            reads=[n_tbuf[b]], store_node=n_store))

    def fp4_tile(dst, packed_src, scales_src, rows, wq, cols, sgroups):
        nonlocal ti
        b = ti % 2
        ti += 1
        p.dma("gpsimd", pk[b][:rows, :wq], packed_src, writes=[n_pk[b]])
        p.dma("gpsimd", scl[b][:rows, :sgroups], scales_src, writes=[n_scl[b]])

        def decode(e, b=b, rows=rows, wq=wq, cols=cols, sgroups=sgroups):
            for k in range(8):
                e.tensor_scalar(out=nib[:rows, k:cols:8], in0=pk[b][:rows, :wq],
                                scalar1=4 * k, scalar2=15,
                                op0=ALU.logical_shift_right, op1=ALU.bitwise_and)
            e.tensor_scalar(out=t_m[:rows, :cols], in0=nib[:rows, :cols], scalar1=1,
                            scalar2=None, op0=ALU.bitwise_and, op1=ALU.bypass)
            e.tensor_scalar(out=t_e[:rows, :cols], in0=nib[:rows, :cols], scalar1=7,
                            scalar2=1, op0=ALU.bitwise_and, op1=ALU.logical_shift_right)
            e.tensor_scalar(out=t_f[:rows, :cols].bitcast(I32), in0=t_e[:rows, :cols],
                            scalar1=126, scalar2=1 << 23, op0=ALU.add, op1=ALU.mult)
            e.tensor_copy(out=t_g[:rows, :cols], in_=t_m[:rows, :cols])
            e.tensor_scalar(out=t_g[:rows, :cols], in0=t_g[:rows, :cols], scalar1=0.5,
                            scalar2=1.0, op0=ALU.mult, op1=ALU.add)
            e.tensor_mul(val[:rows, :cols], t_f[:rows, :cols], t_g[:rows, :cols])
            e.tensor_scalar(out=t_g[:rows, :cols], in0=t_g[:rows, :cols], scalar1=1.0,
                            scalar2=None, op0=ALU.subtract, op1=ALU.bypass)
            e.tensor_scalar(out=t_m[:rows, :cols], in0=t_e[:rows, :cols], scalar1=0,
                            scalar2=None, op0=ALU.is_equal, op1=ALU.bypass)
            e.select(out=val[:rows, :cols], mask=t_m[:rows, :cols],
                     on_true=t_g[:rows, :cols], on_false=val[:rows, :cols])
            e.tensor_scalar(out=t_e[:rows, :cols], in0=nib[:rows, :cols], scalar1=3,
                            scalar2=None, op0=ALU.logical_shift_right, op1=ALU.bypass)
            e.tensor_copy(out=t_g[:rows, :cols], in_=t_e[:rows, :cols])
            e.tensor_scalar(out=t_g[:rows, :cols], in0=t_g[:rows, :cols], scalar1=-2.0,
                            scalar2=1.0, op0=ALU.mult, op1=ALU.add)
            e.tensor_mul(val[:rows, :cols], val[:rows, :cols], t_g[:rows, :cols])
            sclv = scl[b][:rows, :sgroups].unsqueeze(2).broadcast_to([rows, sgroups, 128])
            return e.tensor_tensor(
                out=obuf[b][:rows, :cols].rearrange("p (g q) -> p g q", q=128),
                in0=val[:rows, :cols].rearrange("p (g q) -> p g q", q=128),
                in1=sclv, op=ALU.mult)

        p.op("vector", decode, reads=[n_pk[b], n_scl[b]], writes=[n_obuf[b]])
        tr_store(obuf[b], n_obuf[b], dst, rows, cols, b)

    def fp4_mat(dstT, src_p, src_s, R, COLS):
        for i in range(cdiv(R, 128)):
            r0 = i * 128
            rows = min(128, R - r0)
            for c0 in range(0, COLS, CB):
                cols = min(CB, COLS - c0)
                fp4_tile(dstT[c0:c0 + cols, r0:r0 + rows],
                         src_p[r0:r0 + rows, c0 // 8:(c0 + cols) // 8],
                         src_s[r0:r0 + rows, c0 // 128:(c0 + cols) // 128],
                         rows, cols // 8, cols, cols // 128)

    fp4_mat(o_gup, gup_p, gup_s, GUP_SH, 2048)
    fp4_mat(o_dwn, dwn_p, dwn_s, DWN_SH, 5632)

    gi = 0

    def ghost_tile(dst, idx_src, rows, cols):
        nonlocal gi
        b = gi % 2
        gi += 1
        p.dma("gpsimd", idxb[b][:rows, :cols], idx_src, writes=[n_idxb[b]])

        def comp(e, b=b, rows=rows, cols=cols):
            e.tensor_copy(out=idxf[:rows, :cols], in_=idxb[b][:rows, :cols])
            e.memset(gacc[b][:rows, :cols], 0.0)
            last = None
            for j in range(16):
                e.tensor_scalar(out=gtmp[:rows, :cols], in0=idxf[:rows, :cols],
                                scalar1=float(j), scalar2=lutb[:rows, j:j + 1],
                                op0=ALU.is_equal, op1=ALU.mult)
                last = e.tensor_add(gacc[b][:rows, :cols], gacc[b][:rows, :cols],
                                    gtmp[:rows, :cols])
            return last

        p.op("vector", comp, reads=[n_idxb[b], n_lutb], writes=[n_gacc[b]])
        tr_store(gacc[b], n_gacc[b], dst, rows, cols, b)

    def ghost_mat(dstT, src, R, COLS):
        for i in range(cdiv(R, 128)):
            r0 = i * 128
            rows = min(128, R - r0)
            for c0 in range(0, COLS, CB):
                cols = min(CB, COLS - c0)
                ghost_tile(dstT[c0:c0 + cols, r0:r0 + rows],
                           src[r0:r0 + rows, c0:c0 + cols], rows, cols)

    ghost_mat(o_g, gidx, G_SH2, 2048)
    ghost_mat(o_u, uidx, G_SH2, 2048)
    ghost_mat(o_d, didx, GD_SH, 5504)

    return stores


def build_dequant():
    nc = bass.Bass("TRN2", target_bir_lowering=False, debug=False, detect_race_conditions=bool(os.environ.get("RACEDET")))
    C = 2048
    gup_p = nc.dram_tensor("gup_packed", [GUP_SH, C // 8], I32, kind="ExternalInput").ap()
    gup_s = nc.dram_tensor("gup_scales", [GUP_SH, 16], F16, kind="ExternalInput").ap()
    dwn_p = nc.dram_tensor("down_packed", [DWN_SH, 5632 // 8], I32, kind="ExternalInput").ap()
    dwn_s = nc.dram_tensor("down_scales", [DWN_SH, 44], F16, kind="ExternalInput").ap()
    gidx = nc.dram_tensor("gate_idx", [G_SH2, C], I32, kind="ExternalInput").ap()
    uidx = nc.dram_tensor("up_idx", [G_SH2, C], I32, kind="ExternalInput").ap()
    didx = nc.dram_tensor("down_idx", [GD_SH, 5504], I32, kind="ExternalInput").ap()
    lut = nc.dram_tensor("lut", [1, 16], F32, kind="ExternalInput").ap()

    o_gup = nc.dram_tensor("w_guT", [C, GUP_SH], BF16, kind="ExternalOutput").ap()
    o_dwn = nc.dram_tensor("w_downT", [5632, DWN_SH], BF16, kind="ExternalOutput").ap()
    o_g = nc.dram_tensor("w_gateT", [C, G_SH2], BF16, kind="ExternalOutput").ap()
    o_u = nc.dram_tensor("w_upT", [C, G_SH2], BF16, kind="ExternalOutput").ap()
    o_d = nc.dram_tensor("w_downgT", [5504, GD_SH], BF16, kind="ExternalOutput").ap()

    with ExitStack() as st:
        p = Prog(nc, st)
        stores = emit_dequant(p, nc, st, gup_p, gup_s, dwn_p, dwn_s, gidx, uidx, didx, lut, o_gup, o_dwn, o_g, o_u, o_d)
        p.fence("sync", stores)
        p.finish()
    return nc

def dequant_in_maps(inputs):
    gup_sT = np.ascontiguousarray(np.asarray(inputs["L0_ffn_gate_up_scales"]).T)
    dwn_sT = np.ascontiguousarray(np.asarray(inputs["L0_ffn_down_scales"]).T)
    lut = np.ascontiguousarray(np.asarray(inputs["L1_ffn_lut"]).reshape(1, 16))
    gidx = np.zeros((6144, 2048), np.int32)
    gidx[:5504] = inputs["L1_ffn_gate_idx"]
    uidx = np.zeros((6144, 2048), np.int32)
    uidx[:5504] = inputs["L1_ffn_up_idx"]
    maps = []
    for c in range(8):
        maps.append(dict(
            gup_packed=np.ascontiguousarray(inputs["L0_ffn_gate_up_packed"][c * GUP_SH:(c + 1) * GUP_SH]),
            gup_scales=np.ascontiguousarray(gup_sT[c * GUP_SH:(c + 1) * GUP_SH]),
            down_packed=np.ascontiguousarray(inputs["L0_ffn_down_packed"][c * DWN_SH:(c + 1) * DWN_SH]),
            down_scales=np.ascontiguousarray(dwn_sT[c * DWN_SH:(c + 1) * DWN_SH]),
            gate_idx=np.ascontiguousarray(gidx[c * G_SH2:(c + 1) * G_SH2]),
            up_idx=np.ascontiguousarray(uidx[c * G_SH2:(c + 1) * G_SH2]),
            down_idx=np.ascontiguousarray(inputs["L1_ffn_down_idx"][c * GD_SH:(c + 1) * GD_SH]),
            lut=lut,
        ))
    return maps


def run_dequant(inputs):
    nc = build_dequant()
    res = run_bass_kernel_spmd(nc, dequant_in_maps(inputs), core_ids=list(range(8)))
    cat = lambda k: np.ascontiguousarray(np.concatenate([r[k] for r in res.results], axis=0))
    return dict(w_guT_g=cat("w_guT"), w_downT_g=cat("w_downT"), w_gateT_g=cat("w_gateT"),
                w_upT_g=cat("w_upT"), w_downgT_g=cat("w_downgT"))



C = 2048
CK = 16
TCTX = 4096
TOWN = 2048
NHQ = 16
NHKV = 4
D = 128


def _rope_tables():
    inv = 1.0 / (10000.0 ** (np.arange(0, D, 2, dtype=np.float64) / D))
    ang = np.arange(TCTX, dtype=np.float64)[:, None] * inv[None, :]
    cos = np.concatenate([np.cos(ang), np.cos(ang)], axis=1).T
    sin = np.concatenate([np.sin(ang), np.sin(ang)], axis=1).T
    return np.ascontiguousarray(cos.astype(np.float32)), np.ascontiguousarray(sin.astype(np.float32))


def _rmat():
    R = np.zeros((128, 128), np.float32)
    for j in range(64):
        R[64 + j, j] = -1.0
        R[j, 64 + j] = 1.0
    return R


def emit_l0_attn(p, nc, st, ctx, WqT, WkT, WvT, WoT, qb, kb, vbr, lnw, maskv, x1o, tq_lo, tq_hi, hook=None):
    cosT, sinT = _rope_tables()
    cosd = nc.inline_tensor(cosT, name=uname("cosT")).ap()
    sind = nc.inline_tensor(sinT, name=uname("sinT")).ap()
    Rd = nc.inline_tensor(_rmat(), name=uname("Rm")).ap()

    sb = lambda name, shape, dt=F32: st.enter_context(nc.sbuf_tensor(uname(name), shape, dt))
    psal = lambda name, shape: st.enter_context(nc.psum_tensor(uname(name), shape, F32))

    wv = sb("wv", [128, 16 * 512], BF16)
    ksb = sb("ksb", [128, 4 * 4096], BF16)
    vsb = sb("vsb", [128, 32 * 512], BF16)
    qsb = sb("qsb", [128, 16 * 512], BF16)
    hsb = sb("hsb", [128, 16 * 512], BF16)
    atn = sb("atn", [128, 16 * 512], BF16)
    x1s = sb("x1s", [128, 16 * 512], F32)
    Rsb = sb("Rsb", [128, 128], F32)
    cossb = sb("cossb", [128, 512], F32)
    sinsb = sb("sinsb", [128, 512], F32)
    qbs = sb("qbs", [128, 16], F32)
    kbs = sb("kbs", [128, 4], F32)
    vbs = sb("vbs", [1, 512], BF16)
    lnws = sb("lnws", [128, 16], F32)
    maskvs = sb("maskvs", [128, 32], F32)
    onesc = sb("onesc", [128, 1], BF16)
    onescol = sb("onescol", [128, 1], F32)
    onescf = sb("onescf", [1, 128], F32)
    zerocol = sb("zerocol", [1, 128], BF16)
    zrow = sb("zrow", [1, 128], BF16)
    onesrb = sb("onesrb", [1, 128], BF16)
    k2m = sb("k2m", [1, 4], F32)
    rowbuf = sb("rowbuf", [1, 512], F32)
    rowbuf2 = sb("rowbuf2", [1, 512], F32)
    negM = sb("negM", [1, 512], BF16)
    rbsb = sb("rbsb", [128, 512], F32)
    rbsb2 = rbsb
    ctxt = [sb(f"ctxt{i}", [128, 512]) for i in range(2)]
    sqt = sb("sqt", [128, 512], BF16)
    sqf = [sb(f"sqf{i}", [128, 512]) for i in range(2)]
    kf = sb("kf", [128, 512], F32)
    s1b = sb("s1b", [128, 512], F32)
    t2f = sb("t2f", [128, 512], F32)
    probs = [sb(f"probs{i}", [128, 512], BF16) for i in range(2)]
    wqt = [sb(f"wqt{i}", [128, 16 * 128], BF16) for i in range(2)]
    wot = [sb(f"wot{i}", [128, 16 * 128], BF16) for i in range(2)]

    ps_ss = psal("ps_ss", [1, 512])
    ps_ss2 = psal("ps_ss2", [1, 512])
    ps_b = psal("ps_b", [128, 512])
    ps_b2 = psal("ps_b2", [128, 512])
    ps_k = psal("ps_k", [128, 512])
    ps_r = ps_b
    ps_s = [psal(f"ps_s{i}", [128, 512]) for i in range(2)]
    ps_o = psal("ps_o", [128, 512])

    N = Node
    n_wv = N("wv")
    n_ksb, n_vsb, n_qsb, n_hsb, n_atn, n_x1s = N("ksb"), N("vsb"), N("qsb"), N("hsb"), N("atn"), N("x1s")
    n_R, n_cos, n_sin = N("R"), N("cos"), N("sin")
    n_qbs, n_kbs, n_vbs, n_lnws, n_maskv = N("qbs"), N("kbs"), N("vbs"), N("lnws"), N("maskv")
    n_onesc, n_onescol, n_onescf, n_onesrb = N("onesc"), N("onescol"), N("onescf"), N("onesrb")
    n_k2m, n_row, n_row2, n_negM = N("k2m"), N("row"), N("row2"), N("negM")
    n_rbsb = N("rbsb")
    n_rbsb2 = n_rbsb
    n_ctxt = nodes("ctxt", 2)
    n_sqt = N("sqt")
    n_sqf = nodes("sqf", 2)
    n_kf, n_s1b, n_t2f = N("kf"), N("s1b"), N("t2f")
    n_probs = nodes("probs", 2)
    n_wqt = nodes("wqt", 2)
    n_wot = nodes("wot", 2)
    n_pss, n_pss2, n_psb, n_psb2, n_psk, n_pso = (
        N("pss"), N("pss2"), N("psb"), N("psb2"), N("psk"), N("pso"))
    n_psr = n_psb
    n_pssc = nodes("pssc", 2)
    n_store = Node("stores")
    stores = []

    p.dma("gpsimd", Rsb[:], Rd, writes=[n_R])
    p.dma("gpsimd", qbs[:], qb, writes=[n_qbs])
    p.dma("gpsimd", kbs[:], kb, writes=[n_kbs])
    p.dma("gpsimd", vbs[:], vbr, writes=[n_vbs])
    p.dma("gpsimd", lnws[:], lnw, writes=[n_lnws])
    p.dma("gpsimd", maskvs[:], maskv, writes=[n_maskv])
    for i in range(CK):
        p.dma("gpsimd", wv[:, i * 512:(i + 1) * 512], WvT[i * 128:(i + 1) * 128, :], writes=[n_wv])
    p.op("vector", lambda e: e.memset(onesc[:], 1.0), writes=[n_onesc])
    p.op("vector", lambda e: e.memset(onescol[:], 1.0), writes=[n_onescol])
    p.op("vector", lambda e: e.memset(onescf[:], 1.0), writes=[n_onescf])
    p.op("vector", lambda e: e.memset(onesrb[:], 1.0), writes=[n_onesrb])
    p.op("vector", lambda e: e.memset(k2m[:], 0.0), writes=[n_k2m])

    def rms_h(t0):
        for ck in range(CK):
            b = ck % 2
            sq = ck % 2
            p.dma("gpsimd", ctxt[b][:], ctx[ck * 128:(ck + 1) * 128, t0:t0 + 512],
                  writes=[n_ctxt[b]])
            p.op("scalar", lambda e, b=b, sq=sq: e.activation(out=sqf[sq][:], in_=ctxt[b][:], func=AF.Square),
                 reads=[n_ctxt[b]], writes=[n_sqf[sq]])
            p.op("tensor", lambda e, ck=ck, sq=sq: e.matmul(
                ps_ss[:], onescol[:], sqf[sq][:], start=(ck == 0), stop=(ck == CK - 1)),
                reads=[n_onescol, n_sqf[sq]], writes=[n_pss])
        p.op("vector", lambda e: e.tensor_scalar(
            out=rowbuf[:], in0=ps_ss[:], scalar1=1.0 / 2048.0, scalar2=1e-6,
            op0=ALU.mult, op1=ALU.add), reads=[n_pss], writes=[n_row])
        p.op("scalar", lambda e: e.activation(out=rowbuf[:], in_=rowbuf[:], func=AF.Ln),
             reads=[n_row], writes=[n_row])
        p.op("scalar", lambda e: e.activation(out=rowbuf[:], in_=rowbuf[:], func=AF.Exp,
                                              scale=-0.5), reads=[n_row], writes=[n_row])
        p.op("tensor", lambda e: e.matmul(ps_b[:], onescf[:], rowbuf[:], start=True, stop=True),
             reads=[n_onescf, n_row], writes=[n_psb])
        for ck in range(CK):
            b = ck % 2
            p.dma("gpsimd", ctxt[b][:], ctx[ck * 128:(ck + 1) * 128, t0:t0 + 512],
                  writes=[n_ctxt[b]])
            p.op("vector", lambda e, ck=ck, b=b: e.scalar_tensor_tensor(
                out=hsb[:, ck * 512:(ck + 1) * 512], in0=ctxt[b][:],
                scalar=lnws[:, ck:ck + 1], in1=ps_b[:],
                op0=ALU.mult, op1=ALU.mult),
                reads=[n_ctxt[b], n_lnws, n_psb], writes=[n_hsb])

    def rope_tile(ps_src, n_ps_src, bias_col, n_bias, out_ap, n_out, t0):
        p.dma("gpsimd", cossb[:], cosd[:, t0:t0 + 512], writes=[n_cos])
        p.dma("gpsimd", sinsb[:], sind[:, t0:t0 + 512], writes=[n_sin])
        p.op("vector", lambda e: e.tensor_scalar(
            out=kf[:], in0=ps_src[:], scalar1=bias_col, scalar2=None,
            op0=ALU.add, op1=ALU.bypass), reads=[n_ps_src, n_bias], writes=[n_kf])
        p.op("vector", lambda e: e.tensor_mul(s1b[:], kf[:], sinsb[:]),
             reads=[n_kf, n_sin], writes=[n_s1b])
        p.op("tensor", lambda e: e.matmul(ps_r[:], Rsb[:], s1b[:], start=True, stop=True),
             reads=[n_R, n_s1b], writes=[n_psr])
        p.op("vector", lambda e: e.tensor_mul(t2f[:], kf[:], cossb[:]),
             reads=[n_kf, n_cos], writes=[n_t2f])
        p.op("vector", lambda e: e.tensor_add(out_ap, t2f[:], ps_r[:]),
             reads=[n_t2f, n_psr], writes=[n_out])

    for tb in range(8):
        t0 = tb * 512
        rms_h(t0)
        for ak in range(4):
            wbi = ak % 2
            p.dma("gpsimd", wqt[wbi][:].rearrange("p (k a) -> p k a", k=16),
                  WkT[:, ak * 128:(ak + 1) * 128].rearrange("(k p) a -> p k a", p=128),
                  writes=[n_wqt[wbi]])

            def kproj(e, wbi=wbi):
                last = None
                for ck in range(CK):
                    last = e.matmul(
                        ps_k[:], wqt[wbi][:, ck * 128:(ck + 1) * 128],
                        hsb[:, ck * 512:(ck + 1) * 512],
                        start=(ck == 0), stop=(ck == CK - 1))
                return last
            p.op("tensor", kproj, reads=[n_wqt[wbi], n_hsb], writes=[n_psk])
            rope_tile(ps_k, n_psk, kbs[:, ak:ak + 1], n_kbs,
                      ksb[:, ak * 4096 + t0: ak * 4096 + t0 + 512], n_ksb, t0)
            p.op("scalar", lambda e, ak=ak, t0=t0: e.activation(
                out=sqt[:], in_=ksb[:, ak * 4096 + t0: ak * 4096 + t0 + 512], func=AF.Square),
                reads=[n_ksb], writes=[n_sqt])
            p.op("tensor", lambda e: e.matmul(ps_ss2[:], onesc[:], sqt[:], start=True, stop=True),
                 reads=[n_onesc, n_sqt], writes=[n_pss2])
            p.op("vector", lambda e: e.reduce_max(rowbuf2[:, 0:1], ps_ss2[:], axis=AX.X),
                 reads=[n_pss2], writes=[n_row2])
            p.op("vector", lambda e, ak=ak: e.tensor_tensor(
                out=k2m[:, ak:ak + 1], in0=k2m[:, ak:ak + 1], in1=rowbuf2[:, 0:1],
                op=ALU.max), reads=[n_row2, n_k2m], writes=[n_k2m])
        for tt in range(4):
            tg = tb * 4 + tt

            def vproj(e, tt=tt):
                for ck in range(CK):
                    e.matmul(ps_k[:], hsb[:, ck * 512 + tt * 128: ck * 512 + (tt + 1) * 128],
                             wv[:, ck * 512:(ck + 1) * 512],
                             start=(ck == 0), stop=False)
                return e.matmul(ps_k[:], onesrb[:], vbs[:], start=False, stop=True)
            p.op("tensor", vproj, reads=[n_hsb, n_wv, n_onesrb, n_vbs], writes=[n_psk])
            p.op("vector", lambda e, tg=tg: e.tensor_copy(
                out=vsb[:, tg * 512:(tg + 1) * 512], in_=ps_k[:]),
                reads=[n_psk], writes=[n_vsb])

    if hook is not None:
        hook()
    SCALE = float(1.0 / np.sqrt(128.0))
    for tqg in range(tq_lo, tq_hi):
        if os.environ.get("TQBAR"):
            p.barrier_all()
        t0 = 2048 + tqg * 512
        rms_h(t0)
        for ak in range(NHQ):
            wbi = ak % 2
            p.dma("gpsimd", wqt[wbi][:].rearrange("p (k a) -> p k a", k=16),
                  WqT[:, ak * 128:(ak + 1) * 128].rearrange("(k p) a -> p k a", p=128),
                  writes=[n_wqt[wbi]])

            def qproj(e, wbi=wbi):
                last = None
                for ck in range(CK):
                    last = e.matmul(ps_k[:], wqt[wbi][:, ck * 128:(ck + 1) * 128],
                                    hsb[:, ck * 512:(ck + 1) * 512],
                                    start=(ck == 0), stop=(ck == CK - 1))
                return last
            p.op("tensor", qproj, reads=[n_wqt[wbi], n_hsb], writes=[n_psk])
            rope_tile(ps_k, n_psk, qbs[:, ak:ak + 1], n_qbs,
                      qsb[:, ak * 512:(ak + 1) * 512], n_qsb, t0)

        n_kv = (2048 + (tqg + 1) * 512) // 128
        for hq in range(NHQ):
            g = hq // 4
            p.op("scalar", lambda e, hq=hq: e.activation(
                out=sqt[:], in_=qsb[:, hq * 512:(hq + 1) * 512], func=AF.Square),
                reads=[n_qsb], writes=[n_sqt])
            p.op("tensor", lambda e: e.matmul(ps_ss[:], onesc[:], sqt[:], start=True, stop=True),
                 reads=[n_onesc, n_sqt], writes=[n_pss])
            p.op("vector", lambda e, g=g: e.tensor_scalar(
                out=negM[:], in0=ps_ss[:], scalar1=k2m[:, g:g + 1], scalar2=-0.5,
                op0=ALU.add, op1=ALU.mult), reads=[n_pss, n_k2m], writes=[n_negM])
            for j in range(n_kv):
                jb = j % 2

                def scores(e, g=g, j=j, jb=jb, hq=hq):
                    e.matmul(ps_s[jb][:],
                             ksb[:, g * 4096 + j * 128: g * 4096 + (j + 1) * 128],
                             qsb[:, hq * 512:(hq + 1) * 512], start=True, stop=False)
                    return e.matmul(ps_s[jb][:], onesrb[:], negM[:], start=False, stop=True)
                p.op("tensor", scores, reads=[n_ksb, n_qsb, n_onesrb, n_negM],
                     writes=[n_pssc[jb]])
                p.op("scalar", lambda e, j=j, jb=jb: e.activation(
                    out=probs[jb][:], in_=ps_s[jb][:], func=AF.Exp,
                    bias=maskvs[:, j:j + 1], scale=SCALE),
                    reads=[n_pssc[jb], n_maskv], writes=[n_probs[jb]])
                if (j + 1) * 128 > t0:
                    p.op("gpsimd", lambda e, j=j, jb=jb, t0=t0: e.affine_select(
                        out=probs[jb][:], in_=probs[jb][:],
                        pattern=[[1, 512]], compare_op=ALU.is_ge, fill=0.0,
                        base=t0 - j * 128, channel_multiplier=-1),
                        reads=[n_probs[jb]], writes=[n_probs[jb]])

                def pv(e, j=j, g=g, jb=jb, n_kv=n_kv):
                    e.matmul(ps_o[:], vsb[:, j * 512 + g * 128: j * 512 + (g + 1) * 128],
                             probs[jb][:], start=(j == 0), stop=(j == n_kv - 1))
                    return e.matmul(ps_ss2[:], onesc[:], probs[jb][:],
                                    start=(j == 0), stop=(j == n_kv - 1))
                p.op("tensor", pv, reads=[n_vsb, n_probs[jb], n_onesc],
                     writes=[n_pso, n_pss2])
            p.op("vector", lambda e: e.tensor_scalar(
                out=rowbuf[:], in0=ps_ss2[:], scalar1=1e-30, scalar2=None,
                op0=ALU.add, op1=ALU.bypass), reads=[n_pss2], writes=[n_row])
            p.op("vector", lambda e: e.reciprocal(out=rowbuf[:], in_=rowbuf[:]),
                 reads=[n_row], writes=[n_row])
            p.op("tensor", lambda e: e.matmul(ps_b2[:], onescf[:], rowbuf[:], start=True, stop=True),
                 reads=[n_onescf, n_row], writes=[n_psb2])
            p.op("vector", lambda e: e.tensor_copy(out=rbsb[:], in_=ps_b2[:]),
                 reads=[n_psb2], writes=[n_rbsb])
            p.op("vector", lambda e, hq=hq: e.tensor_mul(
                atn[:, hq * 512:(hq + 1) * 512], ps_o[:], rbsb[:]),
                reads=[n_pso, n_rbsb], writes=[n_atn])

        for ck in range(CK):
            wbi = ck % 2
            p.dma("gpsimd", wot[wbi][:].rearrange("p (k c) -> p k c", k=16),
                  WoT[:, ck * 128:(ck + 1) * 128].rearrange("(k p) c -> p k c", p=128),
                  writes=[n_wot[wbi]])

            def oproj(e, wbi=wbi):
                last = None
                for ak in range(NHQ):
                    last = e.matmul(ps_k[:], wot[wbi][:, ak * 128:(ak + 1) * 128],
                                    atn[:, ak * 512:(ak + 1) * 512],
                                    start=(ak == 0), stop=(ak == NHQ - 1))
                return last
            p.op("tensor", oproj, reads=[n_wot[wbi], n_atn], writes=[n_psk])
            b = ck % 2
            p.dma("gpsimd", ctxt[b][:], ctx[ck * 128:(ck + 1) * 128, t0:t0 + 512],
                  writes=[n_ctxt[b]])
            p.op("vector", lambda e, ck=ck, b=b: e.tensor_add(
                x1s[:, ck * 512:(ck + 1) * 512], ps_k[:], ctxt[b][:]),
                reads=[n_psk, n_ctxt[b]], writes=[n_x1s])

        for ck in range(CK):
            sq = ck % 2
            p.op("tensor", lambda e, ck=ck: e.matmul(
                ps_ss[:], onescol[:], x1s[:, ck * 512:(ck + 1) * 512],
                start=(ck == 0), stop=(ck == CK - 1)),
                reads=[n_onescol, n_x1s], writes=[n_pss])
            p.op("scalar", lambda e, ck=ck, sq=sq: e.activation(
                out=sqf[sq][:], in_=x1s[:, ck * 512:(ck + 1) * 512], func=AF.Square),
                reads=[n_x1s], writes=[n_sqf[sq]])
            p.op("tensor", lambda e, ck=ck, sq=sq: e.matmul(
                ps_ss2[:], onescol[:], sqf[sq][:],
                start=(ck == 0), stop=(ck == CK - 1)),
                reads=[n_onescol, n_sqf[sq]], writes=[n_pss2])
        p.op("vector", lambda e: e.tensor_scalar(
            out=rowbuf[:], in0=ps_ss[:], scalar1=1.0 / 2048.0, scalar2=None,
            op0=ALU.mult, op1=ALU.bypass), reads=[n_pss], writes=[n_row])
        p.op("vector", lambda e: e.tensor_mul(rowbuf2[:], rowbuf[:], rowbuf[:]),
             reads=[n_row], writes=[n_row2])
        p.op("vector", lambda e: e.tensor_scalar(
            out=rowbuf2[:], in0=rowbuf2[:], scalar1=1e-5, scalar2=None,
            op0=ALU.subtract, op1=ALU.bypass), reads=[n_row2], writes=[n_row2])
        p.op("vector", lambda e: e.scalar_tensor_tensor(
            out=rowbuf2[:], in0=ps_ss2[:], scalar=1.0 / 2048.0, in1=rowbuf2[:],
            op0=ALU.mult, op1=ALU.subtract), reads=[n_pss2, n_row2], writes=[n_row2])
        p.op("scalar", lambda e: e.activation(out=rowbuf2[:], in_=rowbuf2[:], func=AF.Ln),
             reads=[n_row2], writes=[n_row2])
        p.op("scalar", lambda e: e.activation(out=rowbuf2[:], in_=rowbuf2[:], func=AF.Exp,
                                              scale=-0.5), reads=[n_row2], writes=[n_row2])
        p.op("tensor", lambda e: e.matmul(ps_b[:], onescf[:], rowbuf[:], start=True, stop=True),
             reads=[n_onescf, n_row], writes=[n_psb])
        p.op("tensor", lambda e: e.matmul(ps_b2[:], onescf[:], rowbuf2[:], start=True, stop=True),
             reads=[n_onescf, n_row2], writes=[n_psb2])
        p.op("vector", lambda e: e.tensor_copy(out=rbsb2[:], in_=ps_b2[:]),
             reads=[n_psb2], writes=[n_rbsb2])
        for ck in range(CK):
            p.op("vector", lambda e, ck=ck: e.tensor_sub(
                x1s[:, ck * 512:(ck + 1) * 512], x1s[:, ck * 512:(ck + 1) * 512], ps_b[:]),
                reads=[n_psb], writes=[n_x1s])
            p.op("vector", lambda e, ck=ck: e.tensor_mul(
                x1s[:, ck * 512:(ck + 1) * 512], x1s[:, ck * 512:(ck + 1) * 512], rbsb2[:]),
                reads=[n_rbsb2], writes=[n_x1s])
            stores.append(p.dma(
                "sync", x1o[ck * 128:(ck + 1) * 128, tqg * 512:(tqg + 1) * 512],
                x1s[:, ck * 512:(ck + 1) * 512], reads=[n_x1s], store_node=n_store))

    return stores


def build_l0_attn(tq_lo=0, tq_hi=4):
    nc = bass.Bass("TRN2", target_bir_lowering=False, debug=False, detect_race_conditions=bool(os.environ.get("RACEDET")))
    ctx = nc.dram_tensor("ctx", [C, TCTX], F32, kind="ExternalInput").ap()
    WqT = nc.dram_tensor("WqT", [C, 2048], F32, kind="ExternalInput").ap()
    WkT = nc.dram_tensor("WkT", [C, 512], F32, kind="ExternalInput").ap()
    WvT = nc.dram_tensor("WvT", [C, 512], F32, kind="ExternalInput").ap()
    WoT = nc.dram_tensor("WoT", [2048, C], F32, kind="ExternalInput").ap()
    qb = nc.dram_tensor("qb", [128, 16], F32, kind="ExternalInput").ap()
    kb = nc.dram_tensor("kb", [128, 4], F32, kind="ExternalInput").ap()
    vbr = nc.dram_tensor("vbr", [1, 512], F32, kind="ExternalInput").ap()
    lnw = nc.dram_tensor("lnw", [128, 16], F32, kind="ExternalInput").ap()
    maskv = nc.dram_tensor("maskv", [128, 32], F32, kind="ExternalInput").ap()
    x1o = nc.dram_tensor("x1", [C, TOWN], F32, kind="ExternalOutput").ap()

    with ExitStack() as st:
        p = Prog(nc, st)
        stores = emit_l0_attn(p, nc, st, ctx, WqT, WkT, WvT, WoT, qb, kb, vbr, lnw, maskv, x1o, tq_lo, tq_hi)
        p.fence("sync", stores)
        p.finish()
    return nc

def run_l0_attn(inputs, x_parts):
    nc = build_l0_attn()
    qb = np.ascontiguousarray(inputs["L0_attn_q_bias"].reshape(16, 128).T)
    kb = np.ascontiguousarray(inputs["L0_attn_k_bias"].reshape(4, 128).T)
    vb = np.ascontiguousarray(inputs["L0_attn_v_bias"].reshape(1, 512))
    lnw = np.ascontiguousarray(inputs["L0_attn_ln_weight"].reshape(16, 128).T)
    WqT = np.ascontiguousarray(np.asarray(inputs["L0_attn_q_project"]).T)
    WkT = np.ascontiguousarray(np.asarray(inputs["L0_attn_k_project"]).T)
    WvT = np.ascontiguousarray(np.asarray(inputs["L0_attn_v_project"]).T)
    WoT = np.ascontiguousarray(np.asarray(inputs["L0_attn_o_project"]).T)
    mask_pad = np.zeros((4096,), np.float32)
    mask_pad[:2048] = -1e30
    mask_pad = np.ascontiguousarray(mask_pad.reshape(32, 128).T)
    mask_zero = np.zeros((128, 32), np.float32)
    in_maps = []
    for c in range(8):
        half = c % 2
        if half == 0:
            ctxc = np.concatenate([np.zeros_like(x_parts[c]), x_parts[c]], axis=1)
            mv = mask_pad
        else:
            ctxc = np.concatenate([x_parts[c - 1], x_parts[c]], axis=1)
            mv = mask_zero
        in_maps.append(dict(ctx=np.ascontiguousarray(ctxc), WqT=WqT, WkT=WkT, WvT=WvT,
                            WoT=WoT, qb=qb, kb=kb, vbr=vb, lnw=lnw, maskv=mv))
    res = run_bass_kernel_spmd(nc, in_maps, core_ids=list(range(8)))
    return [r["x1"] for r in res.results]


def split_x_parts(x):
    xt = np.ascontiguousarray(np.asarray(x).transpose(0, 2, 1))
    parts = []
    for c in range(8):
        b, half = c // 2, c % 2
        parts.append(np.ascontiguousarray(xt[b, :, half * 2048:(half + 1) * 2048]))
    return parts


def kernel(**inputs):
    raise NotImplementedError("phase A under construction")



def emit_ffn(p, nc, st, x1, wg_g, wu_g, wd_g, x2o, I, rms_ep, SHW):
    IK = I // 128
    nsh = cdiv(IK * 128, SHW)
    def gu_ap(g, i):
        o = i * 128
        r, ol = o // SHW, o % SHW
        return g[r * C:(r + 1) * C, ol:ol + 128].rearrange("(k p) a -> p k a", p=128)

    def dn_ap(ck):
        r, cl = ck // 2, (ck % 2) * 128
        return wd_g[r * I:(r + 1) * I, cl:cl + 128].rearrange("(k p) a -> p k a", p=128)

    sb = lambda name, shape, dt=F32: st.enter_context(nc.sbuf_tensor(uname(name), shape, dt))
    psal = lambda name, shape: st.enter_context(nc.psum_tensor(uname(name), shape, F32))

    xb = sb("xb", [128, 16 * 512], BF16)
    hsb = sb("hsb", [128, IK * 512], BF16)
    gt = sb("gt", [128, 512], BF16)
    ysb = sb("ysb", [128, 16 * 512], F32)
    x2s = sb("x2s", [128, 16 * 512], F32)
    ctxt = [sb(f"ctxt{i}", [128, 512]) for i in range(2)]
    sqf = [sb(f"sqf{i}", [128, 512]) for i in range(2)]
    wgt = [sb(f"wgt{i}", [128, 16 * 128], BF16) for i in range(2)]
    wut = [sb(f"wut{i}", [128, 16 * 128], BF16) for i in range(2)]
    wdt = [sb(f"wdt{i}", [128, IK * 128], BF16) for i in range(2)]
    onescol = sb("onescol", [128, 1], F32)
    onescf = sb("onescf", [1, 128], F32)
    zerocol = sb("zerocol", [1, 128], BF16)
    zrow = sb("zrow", [1, 128], BF16)
    rowbuf = sb("rowbuf", [1, 512], F32)
    rowbuf2 = sb("rowbuf2", [1, 512], F32)
    rbsb = sb("rbsb", [128, 512], F32)

    ps_g = psal("ps_g", [128, 512])
    ps_u = psal("ps_u", [128, 512])
    ps_d = psal("ps_d", [128, 512])
    ps_ss = psal("ps_ss", [1, 512])
    ps_ss2 = psal("ps_ss2", [1, 512])
    ps_b = psal("ps_b", [128, 512])
    ps_b2 = psal("ps_b2", [128, 512])

    N = Node
    n_xb, n_hsb, n_gt, n_ysb, n_x2s = N("xb"), N("hsb"), N("gt"), N("ysb"), N("x2s")
    n_ctxt = nodes("ctxt", 2)
    n_sqf = nodes("sqf", 2)
    n_wgt = nodes("wgt", 2)
    n_wut = nodes("wut", 2)
    n_wdt = nodes("wdt", 2)
    n_ones, n_onescf = N("ones"), N("onescf")
    n_row, n_row2, n_rbsb = N("row"), N("row2"), N("rbsb")
    n_psg, n_psu, n_psd = N("psg"), N("psu"), N("psd")
    n_pss, n_pss2, n_psb, n_psb2 = N("pss"), N("pss2"), N("psb"), N("psb2")
    n_store = Node("store")
    stores = []

    p.op("vector", lambda e: e.memset(onescol[:], 1.0), writes=[n_ones])
    p.op("vector", lambda e: e.memset(onescf[:], 1.0), writes=[n_onescf])

    for tg in range(4):
        t0 = tg * 512
        for ck in range(CK):
            p.dma("gpsimd", xb[:, ck * 512:(ck + 1) * 512],
                  x1[ck * 128:(ck + 1) * 128, t0:t0 + 512], writes=[n_xb])
        for ii in range(IK):
            wb = ii % 2
            p.dma("sync", wgt[wb][:].rearrange("p (k a) -> p k a", k=16),
                  gu_ap(wg_g, ii), writes=[n_wgt[wb]])
            p.dma("sync", wut[wb][:].rearrange("p (k a) -> p k a", k=16),
                  gu_ap(wu_g, ii), writes=[n_wut[wb]])

            def gmm(e, wb=wb, w=wgt, ps=ps_g):
                last = None
                for ck in range(CK):
                    last = e.matmul(ps[:], w[wb][:, ck * 128:(ck + 1) * 128],
                                    xb[:, ck * 512:(ck + 1) * 512],
                                    start=(ck == 0), stop=(ck == CK - 1))
                return last
            p.op("tensor", gmm, reads=[n_wgt[wb], n_xb], writes=[n_psg])
            p.op("scalar", lambda e: e.activation(out=gt[:], in_=ps_g[:], func=AF.Silu),
                 reads=[n_psg], writes=[n_gt])
            p.op("tensor", lambda e, wb=wb: gmm(e, wb=wb, w=wut, ps=ps_u),
                 reads=[n_wut[wb], n_xb], writes=[n_psu])
            p.op("vector", lambda e, ii=ii: e.tensor_mul(
                hsb[:, ii * 512:(ii + 1) * 512], ps_u[:], gt[:]),
                reads=[n_psu, n_gt], writes=[n_hsb])
        for ck in range(CK):
            wb = ck % 2
            p.dma("sync", wdt[wb][:].rearrange("p (k a) -> p k a", k=IK),
                  dn_ap(ck), writes=[n_wdt[wb]])

            def dmm(e, wb=wb):
                last = None
                for ii in range(IK):
                    last = e.matmul(ps_d[:], wdt[wb][:, ii * 128:(ii + 1) * 128],
                                    hsb[:, ii * 512:(ii + 1) * 512],
                                    start=(ii == 0), stop=(ii == IK - 1))
                return last
            p.op("tensor", dmm, reads=[n_wdt[wb], n_hsb], writes=[n_psd])
            p.op("vector", lambda e, ck=ck: e.tensor_copy(
                out=ysb[:, ck * 512:(ck + 1) * 512], in_=ps_d[:]),
                reads=[n_psd], writes=[n_ysb])

        if rms_ep:
            for ck in range(CK):
                sq = ck % 2
                p.op("scalar", lambda e, ck=ck, sq=sq: e.activation(
                    out=sqf[sq][:], in_=ysb[:, ck * 512:(ck + 1) * 512], func=AF.Square),
                    reads=[n_ysb], writes=[n_sqf[sq]])
                p.op("tensor", lambda e, ck=ck, sq=sq: e.matmul(
                    ps_ss[:], onescol[:], sqf[sq][:],
                    start=(ck == 0), stop=(ck == CK - 1)),
                    reads=[n_ones, n_sqf[sq]], writes=[n_pss])
            p.op("vector", lambda e: e.tensor_scalar(
                out=rowbuf[:], in0=ps_ss[:], scalar1=1.0 / 2048.0, scalar2=1e-6,
                op0=ALU.mult, op1=ALU.add), reads=[n_pss], writes=[n_row])
            p.op("scalar", lambda e: e.activation(out=rowbuf[:], in_=rowbuf[:], func=AF.Ln),
                 reads=[n_row], writes=[n_row])
            p.op("scalar", lambda e: e.activation(out=rowbuf[:], in_=rowbuf[:], func=AF.Exp,
                                                  scale=-0.5), reads=[n_row], writes=[n_row])
            p.op("tensor", lambda e: e.matmul(ps_b[:], onescf[:], rowbuf[:], start=True, stop=True),
                 reads=[n_onescf, n_row], writes=[n_psb])
        for ck in range(CK):
            b = ck % 2
            p.dma("gpsimd", ctxt[b][:], x1[ck * 128:(ck + 1) * 128, t0:t0 + 512],
                  writes=[n_ctxt[b]])
            if rms_ep:
                p.op("vector", lambda e, ck=ck, b=b: e.tensor_mul(
                    x2s[:, ck * 512:(ck + 1) * 512], ps_b[:], ysb[:, ck * 512:(ck + 1) * 512]),
                    reads=[n_psb, n_ysb], writes=[n_x2s])
                p.op("vector", lambda e, ck=ck, b=b: e.tensor_add(
                    x2s[:, ck * 512:(ck + 1) * 512], x2s[:, ck * 512:(ck + 1) * 512],
                    ctxt[b][:]), reads=[n_ctxt[b]], writes=[n_x2s])
            else:
                p.op("vector", lambda e, ck=ck, b=b: e.tensor_add(
                    x2s[:, ck * 512:(ck + 1) * 512], ysb[:, ck * 512:(ck + 1) * 512],
                    ctxt[b][:]), reads=[n_ysb, n_ctxt[b]], writes=[n_x2s])
        for ck in range(CK):
            sq = ck % 2
            p.op("tensor", lambda e, ck=ck: e.matmul(
                ps_ss[:], onescol[:], x2s[:, ck * 512:(ck + 1) * 512],
                start=(ck == 0), stop=(ck == CK - 1)),
                reads=[n_ones, n_x2s], writes=[n_pss])
            p.op("scalar", lambda e, ck=ck, sq=sq: e.activation(
                out=sqf[sq][:], in_=x2s[:, ck * 512:(ck + 1) * 512], func=AF.Square),
                reads=[n_x2s], writes=[n_sqf[sq]])
            p.op("tensor", lambda e, ck=ck, sq=sq: e.matmul(
                ps_ss2[:], onescol[:], sqf[sq][:],
                start=(ck == 0), stop=(ck == CK - 1)),
                reads=[n_ones, n_sqf[sq]], writes=[n_pss2])
        p.op("vector", lambda e: e.tensor_scalar(
            out=rowbuf[:], in0=ps_ss[:], scalar1=1.0 / 2048.0, scalar2=None,
            op0=ALU.mult, op1=ALU.bypass), reads=[n_pss], writes=[n_row])
        p.op("vector", lambda e: e.tensor_mul(rowbuf2[:], rowbuf[:], rowbuf[:]),
             reads=[n_row], writes=[n_row2])
        p.op("vector", lambda e: e.tensor_scalar(
            out=rowbuf2[:], in0=rowbuf2[:], scalar1=1e-5, scalar2=None,
            op0=ALU.subtract, op1=ALU.bypass), reads=[n_row2], writes=[n_row2])
        p.op("vector", lambda e: e.scalar_tensor_tensor(
            out=rowbuf2[:], in0=ps_ss2[:], scalar=1.0 / 2048.0, in1=rowbuf2[:],
            op0=ALU.mult, op1=ALU.subtract), reads=[n_pss2, n_row2], writes=[n_row2])
        p.op("scalar", lambda e: e.activation(out=rowbuf2[:], in_=rowbuf2[:], func=AF.Ln),
             reads=[n_row2], writes=[n_row2])
        p.op("scalar", lambda e: e.activation(out=rowbuf2[:], in_=rowbuf2[:], func=AF.Exp,
                                              scale=-0.5), reads=[n_row2], writes=[n_row2])
        p.op("tensor", lambda e: e.matmul(ps_b[:], onescf[:], rowbuf[:], start=True, stop=True),
             reads=[n_onescf, n_row], writes=[n_psb])
        p.op("tensor", lambda e: e.matmul(ps_b2[:], onescf[:], rowbuf2[:], start=True, stop=True),
             reads=[n_onescf, n_row2], writes=[n_psb2])
        p.op("vector", lambda e: e.tensor_copy(out=rbsb[:], in_=ps_b2[:]),
             reads=[n_psb2], writes=[n_rbsb])
        for ck in range(CK):
            p.op("vector", lambda e, ck=ck: e.tensor_sub(
                x2s[:, ck * 512:(ck + 1) * 512], x2s[:, ck * 512:(ck + 1) * 512], ps_b[:]),
                reads=[n_psb], writes=[n_x2s])
            p.op("vector", lambda e, ck=ck: e.tensor_mul(
                x2s[:, ck * 512:(ck + 1) * 512], x2s[:, ck * 512:(ck + 1) * 512], rbsb[:]),
                reads=[n_rbsb], writes=[n_x2s])
            stores.append(p.dma(
                "sync", x2o[ck * 128:(ck + 1) * 128, t0:t0 + 512],
                x2s[:, ck * 512:(ck + 1) * 512], reads=[n_x2s], store_node=n_store))

    return stores


def build_ffn(I, rms_ep, SHW):
    IK = I // 128
    nc = bass.Bass("TRN2", target_bir_lowering=False, debug=False,
                   detect_race_conditions=bool(os.environ.get("RACEDET")))
    nsh = cdiv(IK * 128, SHW)
    x1 = nc.dram_tensor("x1", [C, TOWN], F32, kind="ExternalInput").ap()
    wg_g = nc.dram_tensor("wg_g", [nsh * C, SHW], BF16, kind="ExternalInput").ap()
    wu_g = nc.dram_tensor("wu_g", [nsh * C, SHW], BF16, kind="ExternalInput").ap()
    wd_g = nc.dram_tensor("wd_g", [8 * I, 256], BF16, kind="ExternalInput").ap()
    x2o = nc.dram_tensor("x2", [C, TOWN], F32, kind="ExternalOutput").ap()

    with ExitStack() as st:
        p = Prog(nc, st)
        stores = emit_ffn(p, nc, st, x1, wg_g, wu_g, wd_g, x2o, I, rms_ep, SHW)
        p.fence("sync", stores)
        p.finish()
    return nc

def run_ffn(x_parts, wg_g, wu_g, wd_g, I, rms_ep, SHW):
    nc = build_ffn(I, rms_ep, SHW)
    in_maps = [dict(x1=np.ascontiguousarray(x_parts[c]), wg_g=wg_g, wu_g=wu_g, wd_g=wd_g)
               for c in range(8)]
    res = run_bass_kernel_spmd(nc, in_maps, core_ids=list(range(8)))
    return [r["x2"] for r in res.results]



RT = 4096
TTILE = 256
NTT = RT // TTILE
NCH = TTILE // 32
AC = 8
CLAMP = 4.8


def _masks_lt():
    s = np.arange(128)
    m_loc = ((s[:, None] // 16 == s[None, :] // 16) & (s[None, :] > s[:, None]))
    m_bnd = ((s[:, None] // 32 == s[None, :] // 32) & (s[:, None] % 32 < 16) & (s[None, :] % 32 >= 16))
    return m_loc.astype(np.float32).astype(BF), m_bnd.astype(np.float32).astype(BF)


def emit_l1_attn(p, nc, st, x2ap, maa, w1, w2, dw1, dw2, tdcol, WrT, WkT, WvT, WgT, WoT, ucol, gamr, betr, ypo, ntt):
    m_loc_np, m_bnd_np = _masks_lt()
    mlocd = nc.inline_tensor(m_loc_np, name=uname("mloc")).ap()
    mbndd = nc.inline_tensor(m_bnd_np, name=uname("mbnd")).ap()
    idbf = nc.inline_tensor(np.eye(128, dtype=np.float32).astype(BF), name=uname("idbf")).ap()
    idf32 = nc.inline_tensor(np.eye(128, dtype=np.float32), name=uname("idf32")).ap()

    sb = lambda name, shape, dt=F32: st.enter_context(nc.sbuf_tensor(uname(name), shape, dt))
    psal = lambda name, shape, dt=F32: st.enter_context(nc.psum_tensor(uname(name), shape, dt))

    xt = sb("xt", [128, 16 * TTILE])
    xpv = sb("xpv", [128, 16 * TTILE])
    mix = sb("mix", [128, 16 * TTILE], BF16)
    zb = [sb(f"zb{f}", [32, TTILE], BF16) for f in range(5)]
    zd = sb("zd", [64, TTILE], BF16)
    logw = sb("logw", [128, AC * TTILE])
    Bp = logw
    Bs = sb("Bs", [128, AC * TTILE])
    n15 = sb("n15", [128, AC * 2 * NCH])
    dcol = sb("dcol", [128, AC * NCH])
    kk31 = sb("kk31", [128, AC * 2 * NCH])
    refq = sb("refq", [128, AC * 2 * NCH])
    refk = sb("refk", [128, AC * 2 * NCH])
    targ = [sb(f"targ{i}", [128, TTILE]) for i in range(2)]
    q_loc = sb("q_loc", [128, AC * TTILE], BF16)
    k_loc = sb("k_loc", [128, AC * TTILE], BF16)
    q_bnd = sb("q_bnd", [128, AC * TTILE], BF16)
    k_bnd = sb("k_bnd", [128, AC * TTILE], BF16)
    qpp = sb("qpp", [128, AC * TTILE])
    kpp = sb("kpp", [128, AC * TTILE])
    rpl = sb("rpl", [128, AC * TTILE])
    kpl = sb("kpl", [128, AC * TTILE])
    v_tok = sb("v_tok", [128, (TTILE // 128) * 1024], BF16)
    g_tok = sb("g_tok", [128, (TTILE // 128) * 1024], BF16)
    vpl = sb("vpl", [128, AC * TTILE])
    kp_tok = sb("kp_tok", [128, (TTILE // 128) * 1024], BF16)
    og_f = sb("og_f", [128, AC * TTILE], BF16)
    out_tok = sb("out_tok", [128, (TTILE // 128) * 1024])
    Ssb = sb("Ssb", [128, AC * 64])
    erk = sb("erk", [128, TTILE], BF16)
    drows = sb("drows", [16, TTILE])
    d_tok = sb("d_tok", [128, (TTILE // 128) * 16])
    gsum = sb("gsum", [128, (TTILE // 128) * 16])
    gsq = sb("gsq", [128, (TTILE // 128) * 16])
    gsqp = sb("gsqp", [128, 1024])
    A1 = sb("A1", [128, 128], BF16)
    A1b = sb("A1b", [128, 128], BF16)
    A2 = sb("A2", [128, 128], BF16)
    A2b = sb("A2b", [128, 128], BF16)
    gamb = sb("gamb", [128, 1024])
    betb = sb("betb", [128, 1024])
    mlocs = sb("mlocs", [128, 128], BF16)
    mbnds = sb("mbnds", [128, 128], BF16)
    idfs = sb("idfs", [128, 128])
    BCs = sb("BCs", [128, AC * 16], BF16)
    ucols = sb("ucols", [128, 8])
    tds = sb("tds", [128, 8])
    maas = {k: sb(f"maas_{k}", [128, 16]) for k in ("x", "w", "k", "v", "r", "g")}
    w1s = sb("w1s", [128, 16 * 160], BF16)
    w2r = [sb(f"w2r{i}", [32, 2048], BF16) for i in range(2)]
    dw1s = sb("dw1s", [128, 16 * 64], BF16)
    dw2s = sb("dw2s", [64, 8 * 128], BF16)
    wring = [sb(f"wring{i}", [128, 16 * 128], BF16) for i in range(2)]
    onescf = sb("onescf", [1, 128], F32)
    zerocol = sb("zerocol", [1, 128], BF16)
    zrow = sb("zrow", [1, 128], BF16)
    gamrs = sb("gamrs", [1, 1024])
    betrs = sb("betrs", [1, 1024])
    tmp1 = [sb(f"tmp1{i}", [128, TTILE]) for i in range(2)]
    tmp2 = tmp1
    gn1 = sb("gn1", [128, 16])
    gn2 = sb("gn2", [128, 16])

    ps1 = psal("ps1", [128, 512])
    psA0 = psal("psA0", [128, 512])
    psA1 = psal("psA1", [128, 512])
    ps_out = psal("ps_out", [128, 512])
    ps_st = [psal(f"ps_st{i}", [128, 512]) for i in range(4)]
    psP = [ps1, psA0]

    N = Node
    n_xt, n_xpv, n_mix = N("xt"), N("xpv"), N("mix")
    n_zb = nodes("zb", 5)
    n_zd = N("zd")
    n_logw, n_B, n_Bs = N("logw"), N("B"), N("Bs")
    n_cols = N("cols")
    n_targ = nodes("targ", 2)
    n_qloc, n_kloc, n_qbnd, n_kbnd = N("qloc"), N("kloc"), N("qbnd"), N("kbnd")
    n_qpp, n_kpp, n_rpl, n_kpl = N("qpp"), N("kpp"), N("rpl"), N("kpl")
    n_vtok, n_gtok, n_ogf, n_outtok = N("vtok"), N("gtok"), N("ogf"), N("outtok")
    n_vpl, n_kptok = N("vpl"), N("kptok")
    n_S = nodes("S", AC)
    n_erk, n_drows, n_dtok = N("erk"), N("drows"), N("dtok")
    n_gsum, n_gsq, n_gsqp = N("gsum"), N("gsq"), N("gsqp")
    n_A1, n_A1b, n_A2, n_A2b = N("A1"), N("A1b"), N("A2"), N("A2b")
    n_gamb, n_betb = N("gamb"), N("betb")
    n_consts = N("consts")
    n_w1, n_dw1, n_dw2 = N("w1"), N("dw1"), N("dw2")
    n_w2r = nodes("w2r", 2)
    n_wring = nodes("wring", 2)
    n_ps1, n_psA0, n_psA1, n_psout = N("ps1"), N("psA0"), N("psA1"), N("psout")
    nP = [n_ps1, n_psA0]
    n_psst = nodes("psst", 4)

    n_tmp1 = nodes("tmp1", 2)
    n_tmp2 = n_tmp1
    n_gn1, n_gn2 = N("gn1"), N("gn2")
    n_store = N("store")
    stores = []

    p.dma("gpsimd", mlocs[:], mlocd, writes=[n_consts])
    p.dma("gpsimd", mbnds[:], mbndd, writes=[n_consts])
    p.dma("gpsimd", idfs[:], idf32, writes=[n_consts])
    p.dma("gpsimd", ucols[:], ucol, writes=[n_consts])
    p.dma("gpsimd", tds[:], tdcol, writes=[n_consts])
    for k in maas:
        p.dma("gpsimd", maas[k][:], maa[k], writes=[n_consts])
    p.dma("gpsimd", gamrs[:], gamr, writes=[n_consts])
    p.dma("gpsimd", betrs[:], betr, writes=[n_consts])
    for i in range(CK):
        p.dma("gpsimd", w1s[:, i * 160:(i + 1) * 160], w1[i * 128:(i + 1) * 128, :], writes=[n_w1])
        p.dma("gpsimd", dw1s[:, i * 64:(i + 1) * 64], dw1[i * 128:(i + 1) * 128, :], writes=[n_dw1])
    for j in range(AC):
        p.dma("gpsimd", dw2s[:, j * 128:(j + 1) * 128], dw2[:, j * 128:(j + 1) * 128], writes=[n_dw2])
    p.op("vector", lambda e: e.memset(onescf[:], 1.0), writes=[n_consts])
    p.op("vector", lambda e: e.memset(zerocol[:], 0.0), writes=[n_consts])
    p.op("vector", lambda e: e.memset(zrow[:], 0.0), writes=[n_consts])
    for h2 in range(2):
        p.op("tensor", lambda e, h2=h2: e.matmul(
            ps1[:], onescf[:], gamrs[:, h2 * 512:(h2 + 1) * 512], start=True, stop=True),
            reads=[n_consts], writes=[n_ps1])
        p.op("vector", lambda e, h2=h2: e.tensor_copy(
            out=gamb[:, h2 * 512:(h2 + 1) * 512], in_=ps1[:]), reads=[n_ps1], writes=[n_gamb])
        p.op("tensor", lambda e, h2=h2: e.matmul(
            ps1[:], onescf[:], betrs[:, h2 * 512:(h2 + 1) * 512], start=True, stop=True),
            reads=[n_consts], writes=[n_ps1])
        p.op("vector", lambda e, h2=h2: e.tensor_copy(
            out=betb[:, h2 * 512:(h2 + 1) * 512], in_=ps1[:]), reads=[n_ps1], writes=[n_betb])
    p.op("vector", lambda e: e.memset(BCs[:], 0.0), writes=[n_consts])
    for j in range(AC):
        p.op("vector", lambda e, j=j: e.tensor_copy(
            out=BCs[0:64, j * 16 + 2 * j: j * 16 + 2 * j + 1], in_=ucols[0:64, j:j + 1]),
            writes=[n_consts])
        p.op("vector", lambda e, j=j: e.tensor_copy(
            out=BCs[64:128, j * 16 + 2 * j + 1: j * 16 + 2 * j + 2], in_=ucols[64:128, j:j + 1]),
            writes=[n_consts])
    p.op("vector", lambda e: e.memset(Ssb[:], 0.0), writes=list(n_S))

    def proj_full(wT, mix_node, out_cb, nach=AC, wofs=0):
        for ac in range(nach):
            wb = ac % 2
            pb = ac % 2
            p.dma("gpsimd", wring[wb][:].rearrange("p (k a) -> p k a", k=16),
                  wT[:, wofs + ac * 128: wofs + (ac + 1) * 128].rearrange("(k p) a -> p k a", p=128),
                  writes=[n_wring[wb]])

            def mm(e, wb=wb, pb=pb):
                last = None
                for ck in range(CK):
                    last = e.matmul(psP[pb][:, :TTILE], wring[wb][:, ck * 128:(ck + 1) * 128],
                                    mix[:, ck * TTILE:(ck + 1) * TTILE],
                                    start=(ck == 0), stop=(ck == CK - 1))
                return last
            p.op("tensor", mm, reads=[n_wring[wb], mix_node], writes=[nP[pb]])
            out_cb(ac, psP[pb], nP[pb])

    for tt in range(ntt):
        t0 = tt * TTILE
        for ck in range(CK):
            p.dma("gpsimd", xt[:, ck * TTILE:(ck + 1) * TTILE],
                  x2ap(ck, t0, t0 + TTILE), writes=[n_xt])
            if t0 == 0:
                p.op("vector", lambda e, ck=ck: e.memset(
                    xpv[:, ck * TTILE:ck * TTILE + 1], 0.0), writes=[n_xpv])
                p.dma("gpsimd", xpv[:, ck * TTILE + 1:(ck + 1) * TTILE],
                      x2ap(ck, 0, TTILE - 1), writes=[n_xpv])
            elif (t0 - 1) // 2048 != (t0 + TTILE - 2) // 2048:
                p.dma("gpsimd", xpv[:, ck * TTILE:ck * TTILE + 1],
                      x2ap(ck, t0 - 1, t0), writes=[n_xpv],
                      allow_slow_non_contiguous=True)
                p.dma("gpsimd", xpv[:, ck * TTILE + 1:(ck + 1) * TTILE],
                      x2ap(ck, t0, t0 + TTILE - 1), writes=[n_xpv])
            else:
                p.dma("gpsimd", xpv[:, ck * TTILE:(ck + 1) * TTILE],
                      x2ap(ck, t0 - 1, t0 + TTILE - 1), writes=[n_xpv])
        for ck in range(CK):
            p.op("vector", lambda e, ck=ck: e.tensor_sub(
                xpv[:, ck * TTILE:(ck + 1) * TTILE], xpv[:, ck * TTILE:(ck + 1) * TTILE],
                xt[:, ck * TTILE:(ck + 1) * TTILE]), reads=[n_xt], writes=[n_xpv])
        for ck in range(CK):
            p.op("vector", lambda e, ck=ck: e.scalar_tensor_tensor(
                out=mix[:, ck * TTILE:(ck + 1) * TTILE], in0=xpv[:, ck * TTILE:(ck + 1) * TTILE],
                scalar=maas["x"][:, ck:ck + 1], in1=xt[:, ck * TTILE:(ck + 1) * TTILE],
                op0=ALU.mult, op1=ALU.add),
                reads=[n_xpv, n_xt, n_consts], writes=[n_mix])
        def zmm(e, lo, hi, pb):
            last = None
            for ck in range(CK):
                last = e.matmul(psP[pb][:hi - lo, :TTILE], w1s[:, ck * 160 + lo: ck * 160 + hi],
                                mix[:, ck * TTILE:(ck + 1) * TTILE],
                                start=(ck == 0), stop=(ck == CK - 1))
            return last
        for pi, (lo, hi) in enumerate(((0, 64), (64, 128), (128, 160))):
            pb = pi % 2
            p.op("tensor", lambda e, lo=lo, hi=hi, pb=pb: zmm(e, lo, hi, pb),
                 reads=[n_w1, n_mix], writes=[nP[pb]])
            for f in range(lo // 32, hi // 32):
                fb = (f * 32 - lo)
                p.op("scalar", lambda e, f=f, fb=fb, pb=pb: e.activation(
                    out=zb[f][:], in_=psP[pb][fb:fb + 32, :TTILE], func=AF.Tanh),
                    reads=[nP[pb]], writes=[n_zb[f]])

        def mk_mix(f, key):
            wf = f % 2
            p.dma("gpsimd", w2r[wf][:], w2[f], writes=[n_w2r[wf]])
            for ck in range(CK):
                pb = ck % 2
                p.op("tensor", lambda e, f=f, ck=ck, pb=pb, wf=wf: e.matmul(
                    psP[pb][:, :TTILE], w2r[wf][:, ck * 128:(ck + 1) * 128],
                    zb[f][:], start=True, stop=True),
                    reads=[n_w2r[wf], n_zb[f]], writes=[nP[pb]])
                p.op("vector", lambda e, ck=ck, key=key, pb=pb: e.scalar_tensor_tensor(
                    out=tmp1[pb][:], in0=psP[pb][:, :TTILE], scalar=maas[key][:, ck:ck + 1],
                    in1=xpv[:, ck * TTILE:(ck + 1) * TTILE], op0=ALU.add, op1=ALU.mult),
                    reads=[nP[pb], n_consts, n_xpv], writes=[n_tmp1[pb]])
                p.op("vector", lambda e, ck=ck, pb=pb: e.tensor_add(
                    mix[:, ck * TTILE:(ck + 1) * TTILE], tmp1[pb][:],
                    xt[:, ck * TTILE:(ck + 1) * TTILE]),
                    reads=[n_tmp1[pb], n_xt], writes=[n_mix])

        mk_mix(1, "k")
        def k_out(ac, ps, nps):
            p.op("vector", lambda e, ac=ac, ps=ps: e.tensor_copy(
                out=kpl[:, ac * TTILE:(ac + 1) * TTILE], in_=ps[:, :TTILE]),
                reads=[nps], writes=[n_kpl])
        proj_full(WkT, n_mix, k_out)
        mk_mix(2, "v")
        def v_out(ac, ps, nps):
            p.op("scalar", lambda e, ac=ac, ps=ps: e.activation(
                out=vpl[:, ac * TTILE:(ac + 1) * TTILE], in_=ps[:, :TTILE],
                func=AF.Copy), reads=[nps], writes=[n_vpl])
        proj_full(WvT, n_mix, v_out)
        for ac in range(AC):
            for tc in range(TTILE // 128):
                pb = (ac * 2 + tc) % 2
                p.op("tensor", lambda e, ac=ac, tc=tc, pb=pb: e.transpose(
                    psP[pb][:, 128:256], vpl[:, ac * TTILE + tc * 128: ac * TTILE + (tc + 1) * 128],
                    idfs[:]), reads=[n_vpl, n_consts], writes=[nP[pb]])
                p.op("vector", lambda e, ac=ac, tc=tc, pb=pb: e.tensor_copy(
                    out=v_tok[:, tc * 1024 + ac * 128: tc * 1024 + (ac + 1) * 128],
                    in_=psP[pb][:, 128:256]), reads=[nP[pb]], writes=[n_vtok])

        def tok_proj(wT, dst, dst_node, act_silu):
            for ac in range(AC):
                wb = ac % 2
                p.dma("gpsimd", wring[wb][:].rearrange("p (k a) -> p k a", k=16),
                      wT[:, ac * 128:(ac + 1) * 128].rearrange("(k p) a -> p k a", p=128),
                      writes=[n_wring[wb]])
                for tc in range(TTILE // 128):
                    pb = (ac * 2 + tc) % 2

                    def mm(e, wb=wb, tc=tc, pb=pb):
                        last = None
                        for ck in range(CK):
                            last = e.matmul(
                                psP[pb][:, :128],
                                mix[:, ck * TTILE + tc * 128: ck * TTILE + (tc + 1) * 128],
                                wring[wb][:, ck * 128:(ck + 1) * 128],
                                start=(ck == 0), stop=(ck == CK - 1))
                        return last
                    p.op("tensor", mm, reads=[n_wring[wb], n_mix], writes=[nP[pb]])
                    if act_silu:
                        p.op("scalar", lambda e, tc=tc, ac=ac, pb=pb: e.activation(
                            out=dst[:, tc * 1024 + ac * 128: tc * 1024 + (ac + 1) * 128],
                            in_=psP[pb][:, :128], func=AF.Silu),
                            reads=[nP[pb]], writes=[dst_node])
                    else:
                        p.op("vector", lambda e, tc=tc, ac=ac, pb=pb: e.tensor_copy(
                            out=dst[:, tc * 1024 + ac * 128: tc * 1024 + (ac + 1) * 128],
                            in_=psP[pb][:, :128]),
                            reads=[nP[pb]], writes=[dst_node])
        mk_mix(3, "r")
        def r_out(ac, ps, nps):
            p.op("vector", lambda e, ac=ac, ps=ps: e.tensor_copy(
                out=rpl[:, ac * TTILE:(ac + 1) * TTILE], in_=ps[:, :TTILE]),
                reads=[nps], writes=[n_rpl])
        proj_full(WrT, n_mix, r_out)
        mk_mix(4, "g")
        tok_proj(WgT, g_tok, n_gtok, True)
        mk_mix(0, "w")
        def ddmm(e):
            last = None
            for ck in range(CK):
                last = e.matmul(ps1[:64, :TTILE], dw1s[:, ck * 64:(ck + 1) * 64],
                                mix[:, ck * TTILE:(ck + 1) * TTILE],
                                start=(ck == 0), stop=(ck == CK - 1))
            return last
        p.op("tensor", ddmm, reads=[n_dw1, n_mix], writes=[nP[0]])
        p.op("scalar", lambda e: e.activation(out=zd[:], in_=ps1[:64, :TTILE], func=AF.Tanh),
             reads=[nP[0]], writes=[n_zd])
        for ac in range(AC):
            pb = ac % 2
            p.op("tensor", lambda e, ac=ac, pb=pb: e.matmul(
                psP[pb][:, :TTILE], dw2s[:, ac * 128:(ac + 1) * 128], zd[:],
                start=True, stop=True), reads=[n_dw2, n_zd], writes=[nP[pb]])
            p.op("scalar", lambda e, ac=ac, pb=pb: e.activation(
                out=targ[pb][:], in_=psP[pb][:, :TTILE], func=AF.Exp, bias=tds[:, ac:ac + 1]),
                reads=[nP[pb], n_consts], writes=[n_targ[pb]])
            p.op("vector", lambda e, ac=ac, pb=pb: e.tensor_scalar(
                out=logw[:, ac * TTILE:(ac + 1) * TTILE], in0=targ[pb][:],
                scalar1=CLAMP, scalar2=-1.0, op0=ALU.min, op1=ALU.mult),
                reads=[n_targ[pb]], writes=[n_logw])

        for ac in range(AC):
            for ch in range(NCH):
                o = ac * TTILE + ch * 32
                p.op("vector", lambda e, o=o: e.tensor_tensor_scan(
                    out=Bp[:, o:o + 32], data0=logw[:, o:o + 32], data1=logw[:, o:o + 32],
                    initial=0.0, op0=ALU.add, op1=ALU.bypass),
                    reads=[n_logw], writes=[n_B])
            p.op("vector", lambda e, ac=ac: e.memset(
                Bs[:, ac * TTILE: (ac + 1) * TTILE: 32], 0.0), writes=[n_Bs])
            p.op("vector", lambda e, ac=ac: e.tensor_copy(
                out=Bs[:, ac * TTILE:(ac + 1) * TTILE].rearrange("p (c s) -> p c s", s=32)[:, :, 1:32],
                in_=Bp[:, ac * TTILE:(ac + 1) * TTILE].rearrange("p (c s) -> p c s", s=32)[:, :, 0:31]),
                reads=[n_B], writes=[n_Bs])
            for hf, mi in ((0, 7), (1, 23)):
                p.op("vector", lambda e, ac=ac, hf=hf, mi=mi: e.tensor_scalar(
                    out=n15[:, ac * 2 * NCH + hf:(ac + 1) * 2 * NCH:2],
                    in0=Bp[:, ac * TTILE + mi:(ac + 1) * TTILE:32],
                    scalar1=-1.0, scalar2=None, op0=ALU.mult, op1=ALU.bypass),
                    reads=[n_B], writes=[n_cols])
            p.op("scalar", lambda e, ac=ac: e.activation(
                out=dcol[:, ac * NCH:(ac + 1) * NCH],
                in_=Bp[:, ac * TTILE + 31:(ac + 1) * TTILE:32], func=AF.Exp),
                reads=[n_B], writes=[n_cols])
            for hf in range(2):
                p.op("vector", lambda e, ac=ac, hf=hf: e.tensor_tensor(
                    out=kk31[:, ac * 2 * NCH + hf:(ac + 1) * 2 * NCH:2],
                    in0=Bp[:, ac * TTILE + 31:(ac + 1) * TTILE:32],
                    in1=n15[:, ac * 2 * NCH + hf:(ac + 1) * 2 * NCH:2], op=ALU.add),
                    reads=[n_B, n_cols], writes=[n_cols])
            p.op("scalar", lambda e, ac=ac: e.activation(
                out=kk31[:, ac * 2 * NCH:(ac + 1) * 2 * NCH],
                in_=kk31[:, ac * 2 * NCH:(ac + 1) * 2 * NCH],
                func=AF.Exp), reads=[n_cols], writes=[n_cols])
            p.op("vector", lambda e, ac=ac: e.memset(
                refq[:, ac * 2 * NCH:(ac + 1) * 2 * NCH:2], 0.0), writes=[n_cols])
            p.op("vector", lambda e, ac=ac: e.tensor_copy(
                out=refq[:, ac * 2 * NCH + 1:(ac + 1) * 2 * NCH:2],
                in_=Bp[:, ac * TTILE + 15:(ac + 1) * TTILE:32]),
                reads=[n_B], writes=[n_cols])
            p.op("vector", lambda e, ac=ac: e.tensor_copy(
                out=refk[:, ac * 2 * NCH:(ac + 1) * 2 * NCH:2],
                in_=Bp[:, ac * TTILE + 15:(ac + 1) * TTILE:32]),
                reads=[n_B], writes=[n_cols])
            p.op("vector", lambda e, ac=ac: e.tensor_copy(
                out=refk[:, ac * 2 * NCH + 1:(ac + 1) * 2 * NCH:2],
                in_=Bp[:, ac * TTILE + 31:(ac + 1) * TTILE:32]),
                reads=[n_B], writes=[n_cols])

            asl = slice(ac * TTILE, (ac + 1) * TTILE)
            n15v = n15[:, ac * 2 * NCH:(ac + 1) * 2 * NCH].unsqueeze(2).broadcast_to([128, 2 * NCH, 16])
            refqv = refq[:, ac * 2 * NCH:(ac + 1) * 2 * NCH].unsqueeze(2).broadcast_to([128, 2 * NCH, 16])
            refkv = refk[:, ac * 2 * NCH:(ac + 1) * 2 * NCH].unsqueeze(2).broadcast_to([128, 2 * NCH, 16])
            kk31v = kk31[:, ac * 2 * NCH:(ac + 1) * 2 * NCH].unsqueeze(2).broadcast_to([128, 2 * NCH, 16])
            r3 = lambda apx, s=32: apx.rearrange("p (c s) -> p c s", s=s)
            p.op("vector", lambda e, asl=asl, n15v=n15v: e.tensor_tensor(
                out=r3(targ[0][:], 16), in0=r3(Bs[:, asl], 16), in1=n15v, op=ALU.add),
                reads=[n_Bs, n_cols], writes=[n_targ[0]])
            p.op("scalar", lambda e: e.activation(out=targ[0][:], in_=targ[0][:], func=AF.Exp),
                 reads=[n_targ[0]], writes=[n_targ[0]])
            p.op("vector", lambda e, asl=asl: e.tensor_mul(
                q_loc[:, asl], rpl[:, asl], targ[0][:]), reads=[n_rpl, n_targ[0]], writes=[n_qloc])
            p.op("vector", lambda e, asl=asl, n15v=n15v: e.tensor_tensor(
                out=r3(targ[1][:], 16), in0=r3(Bp[:, asl], 16), in1=n15v, op=ALU.add),
                reads=[n_B, n_cols], writes=[n_targ[1]])
            p.op("scalar", lambda e: e.activation(out=targ[1][:], in_=targ[1][:], func=AF.Exp,
                                                  scale=-1.0), reads=[n_targ[1]], writes=[n_targ[1]])
            p.op("vector", lambda e, asl=asl: e.tensor_mul(
                k_loc[:, asl], kpl[:, asl], targ[1][:]), reads=[n_kpl, n_targ[1]], writes=[n_kloc])
            p.op("vector", lambda e, asl=asl, refqv=refqv: e.tensor_tensor(
                out=r3(targ[0][:], 16), in0=r3(Bs[:, asl], 16), in1=refqv, op=ALU.subtract),
                reads=[n_Bs, n_cols], writes=[n_targ[0]])
            p.op("scalar", lambda e: e.activation(out=targ[0][:], in_=targ[0][:], func=AF.Exp),
                 reads=[n_targ[0]], writes=[n_targ[0]])
            p.op("vector", lambda e, asl=asl: e.tensor_mul(
                q_bnd[:, asl], rpl[:, asl], targ[0][:]), reads=[n_rpl, n_targ[0]], writes=[n_qbnd])
            p.op("vector", lambda e, asl=asl, refkv=refkv: e.tensor_tensor(
                out=r3(targ[1][:], 16), in0=r3(Bp[:, asl], 16), in1=refkv, op=ALU.subtract),
                reads=[n_B, n_cols], writes=[n_targ[1]])
            p.op("scalar", lambda e: e.activation(out=targ[1][:], in_=targ[1][:], func=AF.Exp,
                                                  scale=-1.0), reads=[n_targ[1]], writes=[n_targ[1]])
            p.op("vector", lambda e, asl=asl: e.tensor_mul(
                k_bnd[:, asl], kpl[:, asl], targ[1][:]), reads=[n_kpl, n_targ[1]], writes=[n_kbnd])
            p.op("scalar", lambda e, asl=asl: e.activation(out=targ[0][:], in_=Bs[:, asl], func=AF.Exp),
                 reads=[n_Bs], writes=[n_targ[0]])
            p.op("vector", lambda e, asl=asl: e.tensor_mul(
                qpp[:, asl], rpl[:, asl], targ[0][:]), reads=[n_rpl, n_targ[0]], writes=[n_qpp])
            p.op("vector", lambda e, asl=asl, kk31v=kk31v: e.tensor_tensor(
                out=r3(kpp[:, asl], 16), in0=r3(k_loc[:, asl], 16), in1=kk31v, op=ALU.mult),
                reads=[n_kloc, n_cols], writes=[n_kpp])

        for ac in range(AC):
            for tc in range(TTILE // 128):
                pb = (ac * 2 + tc) % 2
                p.op("tensor", lambda e, ac=ac, tc=tc, pb=pb: e.transpose(
                    psP[pb][:, 128:256], kpp[:, ac * TTILE + tc * 128: ac * TTILE + (tc + 1) * 128],
                    idfs[:]), reads=[n_kpp, n_consts], writes=[nP[pb]])
                p.op("vector", lambda e, ac=ac, tc=tc, pb=pb: e.tensor_copy(
                    out=kp_tok[:, tc * 1024 + ac * 128: tc * 1024 + (ac + 1) * 128],
                    in_=psP[pb][:, 128:256]), reads=[nP[pb]], writes=[n_kptok])

        for ac in range(AC):
            p.op("vector", lambda e, ac=ac: e.tensor_mul(
                erk[:], rpl[:, ac * TTILE:(ac + 1) * TTILE], kpl[:, ac * TTILE:(ac + 1) * TTILE]),
                reads=[n_rpl, n_kpl], writes=[n_erk])
            p.op("tensor", lambda e, ac=ac: e.matmul(
                ps1[:16, :TTILE], BCs[:, ac * 16:(ac + 1) * 16], erk[:],
                start=(ac == 0), stop=(ac == AC - 1)),
                reads=[n_consts, n_erk], writes=[nP[0]])
        p.op("vector", lambda e: e.tensor_copy(out=drows[:], in_=ps1[:16, :TTILE]),
             reads=[nP[0]], writes=[n_drows])
        for tc in range(TTILE // 128):
            pb = tc % 2
            p.op("tensor", lambda e, tc=tc, pb=pb: e.transpose(
                psP[pb][:, :16], drows[:, tc * 128:(tc + 1) * 128], idfs[0:16, 0:16]),
                reads=[n_drows, n_consts], writes=[nP[pb]])
            p.op("vector", lambda e, tc=tc, pb=pb: e.tensor_copy(
                out=d_tok[:, tc * 16:(tc + 1) * 16], in_=psP[pb][:, :16]),
                reads=[nP[pb]], writes=[n_dtok])

        for grp in range(TTILE // 128):
            sp = slice(grp * 128, (grp + 1) * 128)
            for ac in range(AC):
                gsl0 = ac * TTILE + grp * 128
                gsl = slice(gsl0, gsl0 + 128)

                def amm(e, gsl=gsl, qpl=q_loc, kplx=k_loc):
                    e.matmul(psA0[:, :128], kplx[0:64, gsl], qpl[0:64, gsl], start=True, stop=True)
                    return e.matmul(psA1[:, :128], kplx[64:128, gsl], qpl[64:128, gsl],
                                    start=True, stop=True, tile_position=(64, 0))
                p.op("tensor", lambda e, gsl=gsl: amm(e, gsl, q_loc, k_loc),
                     reads=[n_qloc, n_kloc], writes=[n_psA0, n_psA1])
                p.op("vector", lambda e: e.tensor_mul(A1[:], psA0[:, :128], mlocs[:]),
                     reads=[n_psA0, n_consts], writes=[n_A1])
                p.op("vector", lambda e: e.tensor_mul(A1b[:], psA1[:, :128], mlocs[:]),
                     reads=[n_psA1, n_consts], writes=[n_A1b])
                p.op("tensor", lambda e, gsl=gsl: amm(e, gsl, q_bnd, k_bnd),
                     reads=[n_qbnd, n_kbnd], writes=[n_psA0, n_psA1])
                p.op("vector", lambda e: e.tensor_mul(A2[:], psA0[:, :128], mbnds[:]),
                     reads=[n_psA0, n_consts], writes=[n_A2])
                p.op("vector", lambda e: e.tensor_mul(A2b[:], psA1[:, :128], mbnds[:]),
                     reads=[n_psA1, n_consts], writes=[n_A2b])

                def intramm(e, ac=ac, grp=grp):
                    vA = v_tok[:, grp * 1024 + ac * 128: grp * 1024 + ac * 128 + 64]
                    vB = v_tok[:, grp * 1024 + ac * 128 + 64: grp * 1024 + (ac + 1) * 128]
                    e.matmul(ps_out[:, 0:64], A1[:], vA, start=True, stop=False)
                    e.matmul(ps_out[:, 64:128], A1b[:], vB, start=False, stop=False)
                    e.matmul(ps_out[:, 64:128], A2b[:], vB, start=False, stop=False)
                    return e.matmul(ps_out[:, 0:64], A2[:], vA, start=False, stop=False)
                p.op("tensor", intramm,
                     reads=[n_A1, n_A1b, n_A2, n_A2b, n_vtok],
                     writes=[n_psout])
                for ch in range(4):
                    cidx = grp * 4 + ch
                    co = ac * TTILE + grp * 128 + ch * 32

                    def interA(e, ac=ac, ch=ch, co=co):
                        return e.matmul(ps_out[ch * 32:(ch + 1) * 32, 0:64],
                                        qpp[0:64, co:co + 32], Ssb[0:64, ac * 64:(ac + 1) * 64],
                                        start=False, stop=False, tile_position=(0, ch * 32))
                    p.op("tensor", interA, reads=[n_qpp, n_S[ac]], writes=[n_psout])

                    def interB(e, ac=ac, ch=ch, co=co):
                        return e.matmul(psA1[ch * 32:(ch + 1) * 32, 0:64],
                                        qpp[64:128, co:co + 32], Ssb[64:128, ac * 64:(ac + 1) * 64],
                                        start=True, stop=True, tile_position=(64, ch * 32))
                    p.op("tensor", interB, reads=[n_qpp, n_S[ac]], writes=[n_psA1])

                    def stmm(e, ac=ac, grp=grp, ch=ch):
                        rows = slice(ch * 32, (ch + 1) * 32)
                        kA = kp_tok[rows, grp * 1024 + ac * 128: grp * 1024 + ac * 128 + 64]
                        kB = kp_tok[rows, grp * 1024 + ac * 128 + 64: grp * 1024 + (ac + 1) * 128]
                        vA = v_tok[rows, grp * 1024 + ac * 128: grp * 1024 + ac * 128 + 64]
                        vB = v_tok[rows, grp * 1024 + ac * 128 + 64: grp * 1024 + (ac + 1) * 128]
                        e.matmul(ps_st[ch][0:64, 0:64], kA, vA, start=True, stop=True,
                                 tile_position=(ch * 32, 0))
                        return e.matmul(ps_st[ch][64:128, 0:64], kB, vB, start=True, stop=True,
                                        tile_position=(ch * 32, 64))
                    p.op("tensor", stmm, reads=[n_kptok, n_vtok], writes=[n_psst[ch]])
                    p.op("vector", lambda e, ac=ac, cidx=cidx, ch=ch: e.scalar_tensor_tensor(
                        out=Ssb[:, ac * 64:(ac + 1) * 64], in0=Ssb[:, ac * 64:(ac + 1) * 64],
                        scalar=dcol[:, ac * NCH + cidx: ac * NCH + cidx + 1],
                        in1=ps_st[ch][:, 0:64], op0=ALU.mult, op1=ALU.add),
                        reads=[n_psst[ch], n_cols], writes=[n_S[ac]])
                def closer(e):
                    return e.matmul(ps_out[:, 0:64], zerocol[:], zrow[:, 0:64],
                                    start=False, stop=True)
                p.op("tensor", closer, reads=[n_consts], writes=[n_psout])
                for hh in range(2):
                    sl_o = slice(grp * 1024 + ac * 128 + hh * 64,
                                 grp * 1024 + ac * 128 + (hh + 1) * 64)
                    p.op("vector", lambda e, sl_o=sl_o, grp=grp, ac=ac, hh=hh: e.scalar_tensor_tensor(
                        out=out_tok[:, sl_o],
                        in0=v_tok[:, sl_o],
                        scalar=d_tok[:, grp * 16 + 2 * ac + hh: grp * 16 + 2 * ac + hh + 1],
                        in1=ps_out[:, hh * 64:(hh + 1) * 64], op0=ALU.mult, op1=ALU.add),
                        reads=[n_vtok, n_dtok, n_psout], writes=[n_outtok])
                    if hh == 1:
                        p.op("vector", lambda e, sl_o=sl_o: e.tensor_add(
                            out_tok[:, sl_o], out_tok[:, sl_o], psA1[:, 0:64]),
                            reads=[n_psA1], writes=[n_outtok])

        for tc in range(TTILE // 128) if not os.environ.get("SKIPGN") else ():
            otv = out_tok[:, tc * 1024:(tc + 1) * 1024]
            p.op("vector", lambda e, tc=tc, otv=otv: e.reduce_sum(
                gsum[:, tc * 16:(tc + 1) * 16],
                otv.rearrange("p (h m) -> p h m", m=64), axis=AX.X),
                reads=[n_outtok], writes=[n_gsum])
            p.op("scalar", lambda e, tc=tc, otv=otv: e.activation(
                out=gsqp[:], in_=otv, func=AF.Square), reads=[n_outtok], writes=[n_gsqp])
            p.op("vector", lambda e, tc=tc: e.reduce_sum(
                gsq[:, tc * 16:(tc + 1) * 16],
                gsqp[:].rearrange("p (h m) -> p h m", m=64), axis=AX.X),
                reads=[n_gsqp], writes=[n_gsq])
            p.op("vector", lambda e, tc=tc: e.tensor_scalar(
                out=gn1[:], in0=gsum[:, tc * 16:(tc + 1) * 16], scalar1=1.0 / 64, scalar2=None,
                op0=ALU.mult, op1=ALU.bypass), reads=[n_gsum], writes=[n_gn1])
            p.op("vector", lambda e: e.tensor_mul(gn2[:], gn1[:], gn1[:]),
                 reads=[n_gn1], writes=[n_gn2])
            p.op("vector", lambda e: e.tensor_scalar(
                out=gn2[:], in0=gn2[:], scalar1=6.4e-4, scalar2=None,
                op0=ALU.subtract, op1=ALU.bypass), reads=[n_gn2], writes=[n_gn2])
            p.op("vector", lambda e, tc=tc: e.scalar_tensor_tensor(
                out=gn2[:], in0=gsq[:, tc * 16:(tc + 1) * 16], scalar=1.0 / 64, in1=gn2[:],
                op0=ALU.mult, op1=ALU.subtract), reads=[n_gsq, n_gn2], writes=[n_gn2])
            p.op("scalar", lambda e: e.activation(out=gn2[:], in_=gn2[:], func=AF.Ln),
                 reads=[n_gn2], writes=[n_gn2])
            p.op("scalar", lambda e: e.activation(out=gn2[:], in_=gn2[:], func=AF.Exp,
                                                  scale=-0.5), reads=[n_gn2], writes=[n_gn2])
            m_v = gn1[:].unsqueeze(2).broadcast_to([128, 16, 64])
            r_v = gn2[:].unsqueeze(2).broadcast_to([128, 16, 64])
            o3 = otv.rearrange("p (h m) -> p h m", m=64)
            p.op("vector", lambda e, o3=o3, m_v=m_v: e.tensor_tensor(
                out=o3, in0=o3, in1=m_v, op=ALU.subtract),
                reads=[n_gn1], writes=[n_outtok])
            p.op("vector", lambda e, o3=o3, r_v=r_v: e.tensor_tensor(
                out=o3, in0=o3, in1=r_v, op=ALU.mult),
                reads=[n_gn2], writes=[n_outtok])
            p.op("vector", lambda e, otv=otv: e.tensor_mul(otv, otv, gamb[:]),
                 reads=[n_gamb], writes=[n_outtok])
            p.op("vector", lambda e, otv=otv: e.tensor_add(otv, otv, betb[:]),
                 reads=[n_betb], writes=[n_outtok])
            p.op("vector", lambda e, tc=tc, otv=otv: e.tensor_mul(
                out_tok[:, tc * 1024:(tc + 1) * 1024], otv, g_tok[:, tc * 1024:(tc + 1) * 1024]),
                reads=[n_gtok], writes=[n_outtok])
        for tc in range(TTILE // 128):
            for ac in range(AC):
                pb = (tc * AC + ac) % 2
                p.op("tensor", lambda e, tc=tc, ac=ac, pb=pb: e.transpose(
                    psP[pb][:, 128:256],
                    out_tok[:, tc * 1024 + ac * 128: tc * 1024 + (ac + 1) * 128], idfs[:]),
                    reads=[n_outtok, n_consts], writes=[nP[pb]])
                p.op("vector", lambda e, tc=tc, ac=ac, pb=pb: e.tensor_copy(
                    out=og_f[:, ac * TTILE + tc * 128: ac * TTILE + (tc + 1) * 128],
                    in_=psP[pb][:, 128:256]), reads=[nP[pb]], writes=[n_ogf])
        for ck in range(CK):
            wb = ck % 2
            p.dma("gpsimd", wring[wb][:, :8 * 128].rearrange("p (k a) -> p k a", k=8),
                  WoT[:, ck * 128:(ck + 1) * 128].rearrange("(k p) a -> p k a", p=128),
                  writes=[n_wring[wb]])

            pb = ck % 2

            def womm(e, wb=wb, pb=pb):
                last = None
                for ac in range(AC):
                    last = e.matmul(psP[pb][:, :TTILE], wring[wb][:, ac * 128:(ac + 1) * 128],
                                    og_f[:, ac * TTILE:(ac + 1) * TTILE],
                                    start=(ac == 0), stop=(ac == AC - 1))
                return last
            p.op("tensor", womm, reads=[n_wring[wb], n_ogf], writes=[nP[pb]])
            p.op("vector", lambda e, ck=ck, pb=pb: e.tensor_copy(out=tmp2[pb][:], in_=psP[pb][:, :TTILE]),
                 reads=[nP[pb]], writes=[n_tmp2[pb]])
            stores.append(p.dma("sync", ypo[ck * 128:(ck + 1) * 128, t0:t0 + TTILE],
                                tmp2[pb][:], reads=[n_tmp2[pb]], store_node=n_store))

    return stores


def build_l1_attn(ntt=NTT):
    nc = bass.Bass("TRN2", target_bir_lowering=False, debug=False,
                   detect_race_conditions=bool(os.environ.get("RACEDET")))
    x2 = nc.dram_tensor("x2", [C, RT], F32, kind="ExternalInput").ap()
    maa = {k: nc.dram_tensor(f"maa_{k}", [128, 16], F32, kind="ExternalInput").ap()
           for k in ("x", "w", "k", "v", "r", "g")}
    w1 = nc.dram_tensor("w1", [C, 160], F32, kind="ExternalInput").ap()
    w2 = nc.dram_tensor("w2", [5, 32, C], F32, kind="ExternalInput").ap()
    dw1 = nc.dram_tensor("dw1", [C, 64], F32, kind="ExternalInput").ap()
    dw2 = nc.dram_tensor("dw2", [64, 1024], F32, kind="ExternalInput").ap()
    tdcol = nc.dram_tensor("tdcol", [128, 8], F32, kind="ExternalInput").ap()
    WrT = nc.dram_tensor("WrT", [C, 1024], F32, kind="ExternalInput").ap()
    WkT = nc.dram_tensor("WkT", [C, 1024], F32, kind="ExternalInput").ap()
    WvT = nc.dram_tensor("WvT", [C, 1024], F32, kind="ExternalInput").ap()
    WgT = nc.dram_tensor("WgT", [C, 1024], F32, kind="ExternalInput").ap()
    WoT = nc.dram_tensor("WoT", [1024, C], F32, kind="ExternalInput").ap()
    ucol = nc.dram_tensor("ucol", [128, 8], F32, kind="ExternalInput").ap()
    gamr = nc.dram_tensor("gamr", [1, 1024], F32, kind="ExternalInput").ap()
    betr = nc.dram_tensor("betr", [1, 1024], F32, kind="ExternalInput").ap()
    ypo = nc.dram_tensor("ypart", [C, RT], F32, kind="ExternalOutput").ap()

    x2ap = lambda ck, a, b: x2[ck * 128:(ck + 1) * 128, a:b]
    with ExitStack() as st:
        p = Prog(nc, st)
        stores = emit_l1_attn(p, nc, st, x2ap, maa, w1, w2, dw1, dw2, tdcol, WrT, WkT, WvT, WgT, WoT, ucol, gamr, betr, ypo, ntt)
        p.fence("sync", stores)
        p.finish()
    return nc

def run_l1_attn(inputs, x2_parts):
    nc = build_l1_attn()
    _ = (inputs["L1_attn_time_maa_x"], inputs["L1_attn_time_maa_w"],
         inputs["L1_attn_time_maa_k"], inputs["L1_attn_time_maa_v"],
         inputs["L1_attn_time_maa_r"], inputs["L1_attn_time_maa_g"])
    g = lambda k: np.asarray(inputs["L1_attn_" + k], np.float32)
    in_maps = []
    for c in range(8):
        b, hh = c // 2, c % 2
        asl = slice(1024 * hh, 1024 * hh + 1024)
        x2f = np.ascontiguousarray(np.concatenate([x2_parts[2 * b], x2_parts[2 * b + 1]], axis=1))
        im = dict(
            x2=x2f,
            w1=np.ascontiguousarray(g("time_maa_w1")),
            w2=np.ascontiguousarray(g("time_maa_w2")),
            dw1=np.ascontiguousarray(g("time_decay_w1")),
            dw2=np.ascontiguousarray(g("time_decay_w2")[:, asl]),
            tdcol=np.ascontiguousarray(g("time_decay")[0, 0, asl].reshape(8, 128).T),
            WrT=np.ascontiguousarray(g("W_r").T[:, asl]),
            WkT=np.ascontiguousarray(g("W_k").T[:, asl]),
            WvT=np.ascontiguousarray(g("W_v").T[:, asl]),
            WgT=np.ascontiguousarray(g("W_g").T[:, asl]),
            WoT=np.ascontiguousarray(g("W_o").T[asl, :]),
            ucol=np.ascontiguousarray(g("time_faaaa")[16 * hh:16 * hh + 16].reshape(8, 128).T),
            gamr=np.ascontiguousarray(g("ln_gamma")[asl].reshape(1, 1024)),
            betr=np.ascontiguousarray(g("ln_beta")[asl].reshape(1, 1024)),
        )
        for k in ("x", "w", "k", "v", "r", "g"):
            im[f"maa_{k}"] = np.ascontiguousarray(g("time_maa_" + k)[0, 0].reshape(16, 128).T)
        in_maps.append(im)
    res = run_bass_kernel_spmd(nc, in_maps, core_ids=list(range(8)))
    return [r["ypart"] for r in res.results]



def emit_combine(p, nc, st, x2, ypA, ypB, x3o):
    sb = lambda name, shape, dt=F32: st.enter_context(nc.sbuf_tensor(uname(name), shape, dt))
    psal = lambda name, shape: st.enter_context(nc.psum_tensor(uname(name), shape, F32))
    x3s = sb("x3s", [128, 16 * 512], F32)
    ta = [sb(f"ta{i}", [128, 512]) for i in range(2)]
    tb = [sb(f"tb{i}", [128, 512]) for i in range(2)]
    sqf = [sb(f"sqf{i}", [128, 512]) for i in range(2)]
    onescol = sb("onescol", [128, 1], F32)
    onescf = sb("onescf", [1, 128], F32)
    rowbuf = sb("rowbuf", [1, 512], F32)
    rowbuf2 = sb("rowbuf2", [1, 512], F32)
    rbsb = sb("rbsb", [128, 512], F32)
    ps_ss = psal("ps_ss", [1, 512])
    ps_ss2 = psal("ps_ss2", [1, 512])
    ps_b = psal("ps_b", [128, 512])
    ps_b2 = psal("ps_b2", [128, 512])
    N = Node
    n_x3s, n_ones, n_onescf = N("x3s"), N("ones"), N("onescf")
    n_ta = nodes("ta", 2)
    n_tb = nodes("tb", 2)
    n_sqf = nodes("sqf", 2)
    n_row, n_row2, n_rbsb = N("row"), N("row2"), N("rbsb")
    n_pss, n_pss2, n_psb, n_psb2 = N("pss"), N("pss2"), N("psb"), N("psb2")
    n_store = N("store")
    stores = []
    p.op("vector", lambda e: e.memset(onescol[:], 1.0), writes=[n_ones])
    p.op("vector", lambda e: e.memset(onescf[:], 1.0), writes=[n_onescf])
    for tg in range(4):
        t0 = tg * 512
        for ck in range(CK):
            b = ck % 2
            p.dma("gpsimd", ta[b][:], x2[ck * 128:(ck + 1) * 128, t0:t0 + 512], writes=[n_ta[b]])
            p.dma("sync", tb[b][:], ypA[ck * 128:(ck + 1) * 128, t0:t0 + 512], writes=[n_tb[b]])
            p.op("vector", lambda e, ck=ck, b=b: e.tensor_add(
                x3s[:, ck * 512:(ck + 1) * 512], ta[b][:], tb[b][:]),
                reads=[n_ta[b], n_tb[b]], writes=[n_x3s])
            p.dma("sync", tb[b][:], ypB[ck * 128:(ck + 1) * 128, t0:t0 + 512], writes=[n_tb[b]])
            p.op("vector", lambda e, ck=ck, b=b: e.tensor_add(
                x3s[:, ck * 512:(ck + 1) * 512], x3s[:, ck * 512:(ck + 1) * 512], tb[b][:]),
                reads=[n_tb[b]], writes=[n_x3s])
        for ck in range(CK):
            sq = ck % 2
            p.op("tensor", lambda e, ck=ck: e.matmul(
                ps_ss[:], onescol[:], x3s[:, ck * 512:(ck + 1) * 512],
                start=(ck == 0), stop=(ck == CK - 1)),
                reads=[n_ones, n_x3s], writes=[n_pss])
            p.op("scalar", lambda e, ck=ck, sq=sq: e.activation(
                out=sqf[sq][:], in_=x3s[:, ck * 512:(ck + 1) * 512], func=AF.Square),
                reads=[n_x3s], writes=[n_sqf[sq]])
            p.op("tensor", lambda e, ck=ck, sq=sq: e.matmul(
                ps_ss2[:], onescol[:], sqf[sq][:],
                start=(ck == 0), stop=(ck == CK - 1)),
                reads=[n_ones, n_sqf[sq]], writes=[n_pss2])
        p.op("vector", lambda e: e.tensor_scalar(
            out=rowbuf[:], in0=ps_ss[:], scalar1=1.0 / 2048.0, scalar2=None,
            op0=ALU.mult, op1=ALU.bypass), reads=[n_pss], writes=[n_row])
        p.op("vector", lambda e: e.tensor_mul(rowbuf2[:], rowbuf[:], rowbuf[:]),
             reads=[n_row], writes=[n_row2])
        p.op("vector", lambda e: e.tensor_scalar(
            out=rowbuf2[:], in0=rowbuf2[:], scalar1=1e-5, scalar2=None,
            op0=ALU.subtract, op1=ALU.bypass), reads=[n_row2], writes=[n_row2])
        p.op("vector", lambda e: e.scalar_tensor_tensor(
            out=rowbuf2[:], in0=ps_ss2[:], scalar=1.0 / 2048.0, in1=rowbuf2[:],
            op0=ALU.mult, op1=ALU.subtract), reads=[n_pss2, n_row2], writes=[n_row2])
        p.op("scalar", lambda e: e.activation(out=rowbuf2[:], in_=rowbuf2[:], func=AF.Ln),
             reads=[n_row2], writes=[n_row2])
        p.op("scalar", lambda e: e.activation(out=rowbuf2[:], in_=rowbuf2[:], func=AF.Exp,
                                              scale=-0.5), reads=[n_row2], writes=[n_row2])
        p.op("tensor", lambda e: e.matmul(ps_b[:], onescf[:], rowbuf[:], start=True, stop=True),
             reads=[n_onescf, n_row], writes=[n_psb])
        p.op("tensor", lambda e: e.matmul(ps_b2[:], onescf[:], rowbuf2[:], start=True, stop=True),
             reads=[n_onescf, n_row2], writes=[n_psb2])
        p.op("vector", lambda e: e.tensor_copy(out=rbsb[:], in_=ps_b2[:]),
             reads=[n_psb2], writes=[n_rbsb])
        for ck in range(CK):
            p.op("vector", lambda e, ck=ck: e.tensor_sub(
                x3s[:, ck * 512:(ck + 1) * 512], x3s[:, ck * 512:(ck + 1) * 512], ps_b[:]),
                reads=[n_psb], writes=[n_x3s])
            p.op("vector", lambda e, ck=ck: e.tensor_mul(
                x3s[:, ck * 512:(ck + 1) * 512], x3s[:, ck * 512:(ck + 1) * 512], rbsb[:]),
                reads=[n_rbsb], writes=[n_x3s])
            stores.append(p.dma(
                "sync", x3o[ck * 128:(ck + 1) * 128, t0:t0 + 512],
                x3s[:, ck * 512:(ck + 1) * 512], reads=[n_x3s], store_node=n_store))
    return stores


def build_combine():
    nc = bass.Bass("TRN2", target_bir_lowering=False, debug=False,
                   detect_race_conditions=bool(os.environ.get("RACEDET")))
    x2 = nc.dram_tensor("x2", [C, TOWN], F32, kind="ExternalInput").ap()
    ypA = nc.dram_tensor("ypA", [C, TOWN], F32, kind="ExternalInput").ap()
    ypB = nc.dram_tensor("ypB", [C, TOWN], F32, kind="ExternalInput").ap()
    x3o = nc.dram_tensor("x3", [C, TOWN], F32, kind="ExternalOutput").ap()

    with ExitStack() as st:
        p = Prog(nc, st)
        stores = emit_combine(p, nc, st, x2, ypA, ypB, x3o)
        p.fence("sync", stores)
        p.finish()
    return nc

def run_combine(x2_parts, yparts):
    nc = build_combine()
    in_maps = []
    for c in range(8):
        b, half = c // 2, c % 2
        sl = slice(half * 2048, (half + 1) * 2048)
        in_maps.append(dict(
            x2=np.ascontiguousarray(x2_parts[c]),
            ypA=np.ascontiguousarray(yparts[2 * b][:, sl]),
            ypB=np.ascontiguousarray(yparts[2 * b + 1][:, sl]),
        ))
    res = run_bass_kernel_spmd(nc, in_maps, core_ids=list(range(8)))
    return [r["x3"] for r in res.results]



def emit_combine2(p, nc, st, x2, ysum, mA, mB, x3o):
    sb = lambda name, shape, dt=F32: st.enter_context(nc.sbuf_tensor(uname(name), shape, dt))
    psal = lambda name, shape: st.enter_context(nc.psum_tensor(uname(name), shape, F32))
    x3s = sb("c2_x3s", [128, 16 * 512], F32)
    ta = [sb(f"c2_ta{i}", [128, 512]) for i in range(2)]
    tb = [sb(f"c2_tb{i}", [128, 512]) for i in range(2)]
    sqf = [sb(f"c2_sqf{i}", [128, 512]) for i in range(2)]
    masks = sb("c2_masks", [128, 2], F32)
    onescol = sb("c2_onescol", [128, 1], F32)
    onescf = sb("c2_onescf", [1, 128], F32)
    rowbuf = sb("c2_rowbuf", [1, 512], F32)
    rowbuf2 = sb("c2_rowbuf2", [1, 512], F32)
    rbsb = sb("c2_rbsb", [128, 512], F32)
    ps_ss = psal("c2_ps_ss", [1, 512])
    ps_ss2 = psal("c2_ps_ss2", [1, 512])
    ps_b = psal("c2_ps_b", [128, 512])
    ps_b2 = psal("c2_ps_b2", [128, 512])
    N = Node
    n_x3s, n_ones, n_onescf, n_masks = N("x3s"), N("ones"), N("onescf"), N("masks")
    n_ta = nodes("c2ta", 2)
    n_tb = nodes("c2tb", 2)
    n_sqf = nodes("c2sqf", 2)
    n_row, n_row2, n_rbsb = N("row"), N("row2"), N("rbsb")
    n_pss, n_pss2, n_psb, n_psb2 = N("pss"), N("pss2"), N("psb"), N("psb2")
    n_store = N("store")
    stores = []
    p.op("vector", lambda e: e.memset(onescol[:], 1.0), writes=[n_ones])
    p.op("vector", lambda e: e.memset(onescf[:], 1.0), writes=[n_onescf])
    p.dma("gpsimd", masks[:, 0:1], mA, writes=[n_masks])
    p.dma("gpsimd", masks[:, 1:2], mB, writes=[n_masks])
    for tg in range(4):
        t0 = tg * 512
        for ck in range(CK):
            b = ck % 2
            p.dma("gpsimd", ta[b][:], x2[ck * 128:(ck + 1) * 128, t0:t0 + 512], writes=[n_ta[b]])
            p.dma("sync", tb[b][:], ysum[ck * 128:(ck + 1) * 128, t0:t0 + 512], writes=[n_tb[b]])
            p.op("vector", lambda e, ck=ck, b=b: e.scalar_tensor_tensor(
                out=x3s[:, ck * 512:(ck + 1) * 512], in0=tb[b][:], scalar=masks[:, 0:1],
                in1=ta[b][:], op0=ALU.mult, op1=ALU.add),
                reads=[n_ta[b], n_tb[b], n_masks], writes=[n_x3s])
            p.dma("sync", tb[b][:], ysum[ck * 128:(ck + 1) * 128, 2048 + t0:2048 + t0 + 512],
                  writes=[n_tb[b]])
            p.op("vector", lambda e, ck=ck, b=b: e.scalar_tensor_tensor(
                out=x3s[:, ck * 512:(ck + 1) * 512], in0=tb[b][:], scalar=masks[:, 1:2],
                in1=x3s[:, ck * 512:(ck + 1) * 512], op0=ALU.mult, op1=ALU.add),
                reads=[n_tb[b], n_masks], writes=[n_x3s])
        for ck in range(CK):
            sq = ck % 2
            p.op("tensor", lambda e, ck=ck: e.matmul(
                ps_ss[:], onescol[:], x3s[:, ck * 512:(ck + 1) * 512],
                start=(ck == 0), stop=(ck == CK - 1)),
                reads=[n_ones, n_x3s], writes=[n_pss])
            p.op("scalar", lambda e, ck=ck, sq=sq: e.activation(
                out=sqf[sq][:], in_=x3s[:, ck * 512:(ck + 1) * 512], func=AF.Square),
                reads=[n_x3s], writes=[n_sqf[sq]])
            p.op("tensor", lambda e, ck=ck, sq=sq: e.matmul(
                ps_ss2[:], onescol[:], sqf[sq][:],
                start=(ck == 0), stop=(ck == CK - 1)),
                reads=[n_ones, n_sqf[sq]], writes=[n_pss2])
        p.op("vector", lambda e: e.tensor_scalar(
            out=rowbuf[:], in0=ps_ss[:], scalar1=1.0 / 2048.0, scalar2=None,
            op0=ALU.mult, op1=ALU.bypass), reads=[n_pss], writes=[n_row])
        p.op("vector", lambda e: e.tensor_mul(rowbuf2[:], rowbuf[:], rowbuf[:]),
             reads=[n_row], writes=[n_row2])
        p.op("vector", lambda e: e.tensor_scalar(
            out=rowbuf2[:], in0=rowbuf2[:], scalar1=1e-5, scalar2=None,
            op0=ALU.subtract, op1=ALU.bypass), reads=[n_row2], writes=[n_row2])
        p.op("vector", lambda e: e.scalar_tensor_tensor(
            out=rowbuf2[:], in0=ps_ss2[:], scalar=1.0 / 2048.0, in1=rowbuf2[:],
            op0=ALU.mult, op1=ALU.subtract), reads=[n_pss2, n_row2], writes=[n_row2])
        p.op("scalar", lambda e: e.activation(out=rowbuf2[:], in_=rowbuf2[:], func=AF.Ln),
             reads=[n_row2], writes=[n_row2])
        p.op("scalar", lambda e: e.activation(out=rowbuf2[:], in_=rowbuf2[:], func=AF.Exp,
                                              scale=-0.5), reads=[n_row2], writes=[n_row2])
        p.op("tensor", lambda e: e.matmul(ps_b[:], onescf[:], rowbuf[:], start=True, stop=True),
             reads=[n_onescf, n_row], writes=[n_psb])
        p.op("tensor", lambda e: e.matmul(ps_b2[:], onescf[:], rowbuf2[:], start=True, stop=True),
             reads=[n_onescf, n_row2], writes=[n_psb2])
        p.op("vector", lambda e: e.tensor_copy(out=rbsb[:], in_=ps_b2[:]),
             reads=[n_psb2], writes=[n_rbsb])
        for ck in range(CK):
            p.op("vector", lambda e, ck=ck: e.tensor_sub(
                x3s[:, ck * 512:(ck + 1) * 512], x3s[:, ck * 512:(ck + 1) * 512], ps_b[:]),
                reads=[n_psb], writes=[n_x3s])
            p.op("vector", lambda e, ck=ck: e.tensor_mul(
                x3s[:, ck * 512:(ck + 1) * 512], x3s[:, ck * 512:(ck + 1) * 512], rbsb[:]),
                reads=[n_rbsb], writes=[n_x3s])
            stores.append(p.dma(
                "sync", x3o[ck * 128:(ck + 1) * 128, t0:t0 + 512],
                x3s[:, ck * 512:(ck + 1) * 512], reads=[n_x3s], store_node=n_store))
    return stores


def build_mega():
    nc = bass.Bass("TRN2", target_bir_lowering=False, debug=False,
                   detect_race_conditions=bool(os.environ.get("RACEDET")))
    xctx = nc.dram_tensor("xctx", [C, TCTX], F32, kind="ExternalInput").ap()
    maskv = nc.dram_tensor("maskv", [128, 32], F32, kind="ExternalInput").ap()
    WqT = nc.dram_tensor("WqT", [C, 2048], F32, kind="ExternalInput").ap()
    WkT0 = nc.dram_tensor("WkT0", [C, 512], F32, kind="ExternalInput").ap()
    WvT0 = nc.dram_tensor("WvT0", [C, 512], F32, kind="ExternalInput").ap()
    WoT0 = nc.dram_tensor("WoT0", [2048, C], F32, kind="ExternalInput").ap()
    qb = nc.dram_tensor("qb", [128, 16], F32, kind="ExternalInput").ap()
    kb = nc.dram_tensor("kb", [128, 4], F32, kind="ExternalInput").ap()
    vbr = nc.dram_tensor("vbr", [1, 512], F32, kind="ExternalInput").ap()
    lnw = nc.dram_tensor("lnw", [128, 16], F32, kind="ExternalInput").ap()
    gup_p = nc.dram_tensor("gup_packed", [GUP_SH, C // 8], I32, kind="ExternalInput").ap()
    gup_s = nc.dram_tensor("gup_scales", [GUP_SH, 16], F16, kind="ExternalInput").ap()
    dwn_p = nc.dram_tensor("down_packed", [DWN_SH, 5632 // 8], I32, kind="ExternalInput").ap()
    dwn_s = nc.dram_tensor("down_scales", [DWN_SH, 44], F16, kind="ExternalInput").ap()
    gidx = nc.dram_tensor("gate_idx", [G_SH2, C], I32, kind="ExternalInput").ap()
    uidx = nc.dram_tensor("up_idx", [G_SH2, C], I32, kind="ExternalInput").ap()
    didx = nc.dram_tensor("down_idx", [GD_SH, 5504], I32, kind="ExternalInput").ap()
    lut = nc.dram_tensor("lut", [1, 16], F32, kind="ExternalInput").ap()
    maa = {k: nc.dram_tensor(f"maa_{k}", [128, 16], F32, kind="ExternalInput").ap()
           for k in ("x", "w", "k", "v", "r", "g")}
    w1 = nc.dram_tensor("w1", [C, 160], F32, kind="ExternalInput").ap()
    w2 = nc.dram_tensor("w2", [5, 32, C], F32, kind="ExternalInput").ap()
    dw1 = nc.dram_tensor("dw1", [C, 64], F32, kind="ExternalInput").ap()
    dw2 = nc.dram_tensor("dw2", [64, 1024], F32, kind="ExternalInput").ap()
    tdcol = nc.dram_tensor("tdcol", [128, 8], F32, kind="ExternalInput").ap()
    WrT = nc.dram_tensor("WrT", [C, 1024], F32, kind="ExternalInput").ap()
    WkT1 = nc.dram_tensor("WkT1", [C, 1024], F32, kind="ExternalInput").ap()
    WvT1 = nc.dram_tensor("WvT1", [C, 1024], F32, kind="ExternalInput").ap()
    WgT1 = nc.dram_tensor("WgT1", [C, 1024], F32, kind="ExternalInput").ap()
    WoT1 = nc.dram_tensor("WoT1", [1024, C], F32, kind="ExternalInput").ap()
    ucol = nc.dram_tensor("ucol", [128, 8], F32, kind="ExternalInput").ap()
    gamr = nc.dram_tensor("gamr", [1, 1024], F32, kind="ExternalInput").ap()
    betr = nc.dram_tensor("betr", [1, 1024], F32, kind="ExternalInput").ap()
    mA = nc.dram_tensor("mA", [128, 1], F32, kind="ExternalInput").ap()
    mB = nc.dram_tensor("mB", [128, 1], F32, kind="ExternalInput").ap()
    x4o = nc.dram_tensor("x4", [C, TOWN], F32, kind="ExternalOutput").ap()

    sh_gu = nc.dram_tensor("sh_gu", [C, GUP_SH], BF16, kind="Internal")
    sh_dn = nc.dram_tensor("sh_dn", [5632, DWN_SH], BF16, kind="Internal")
    sh_g = nc.dram_tensor("sh_g", [C, G_SH2], BF16, kind="Internal")
    sh_u = nc.dram_tensor("sh_u", [C, G_SH2], BF16, kind="Internal")
    sh_d = nc.dram_tensor("sh_d", [5504, GD_SH], BF16, kind="Internal")
    g_gu = nc.dram_tensor("g_gu", [8 * C, GUP_SH], BF16, kind="Internal")
    g_dn = nc.dram_tensor("g_dn", [8 * 5632, DWN_SH], BF16, kind="Internal")
    g_g = nc.dram_tensor("g_g", [8 * C, G_SH2], BF16, kind="Internal")
    g_u = nc.dram_tensor("g_u", [8 * C, G_SH2], BF16, kind="Internal")
    g_d = nc.dram_tensor("g_d", [8 * 5504, GD_SH], BF16, kind="Internal")
    st1 = nc.dram_tensor("st1", [C, TOWN], F32, kind="Internal")
    st2 = nc.dram_tensor("st2", [C, TOWN], F32, kind="Internal")
    x2g = nc.dram_tensor("x2g", [2 * C, TOWN], F32, kind="Internal")
    ypart = nc.dram_tensor("ypart", [C, RT], F32, kind="Internal")
    ysum = nc.dram_tensor("ysum", [C, RT], F32, kind="Internal")
    st3 = nc.dram_tensor("st3", [C, TOWN], F32, kind="Internal")

    PAIRS = [[0, 1], [2, 3], [4, 5], [6, 7]]
    ALL8 = [list(range(8))]

    with ExitStack() as stp:
        p = Prog(nc, stp)
        n_cc = Node("ccchain")

        def coll(kind, op, groups, inap, outap, rd_nodes, wr_node):
            p.op("gpsimd", lambda e: e.collective_compute(
                kind, op, replica_groups=groups, ins=[inap.ap().opt()],
                outs=[outap.ap().opt()]),
                reads=list(rd_nodes) + [n_cc], writes=[wr_node, n_cc], sigk=1)

        n_shards = Node("shards")
        n_gath = Node("gathered")
        n_st1, n_st2, n_x2g, n_yp, n_ys, n_st3 = (
            Node("st1"), Node("st2"), Node("x2g"), Node("yp"), Node("ys"), Node("st3"))

        with ExitStack() as st:
            stores = emit_dequant(p, nc, st, gup_p, gup_s, dwn_p, dwn_s, gidx, uidx,
                                  didx, lut, sh_gu.ap(), sh_dn.ap(), sh_g.ap(),
                                  sh_u.ap(), sh_d.ap())
            for ev in stores:
                n_shards.writers[ev[0]] = max(n_shards.writers.get(ev[0], 0), ev[1])
            p.barrier_all()

        def gath_hook():
            coll("AllGather", ALU.bypass, ALL8, sh_gu, g_gu, [n_shards], n_gath)
            coll("AllGather", ALU.bypass, ALL8, sh_dn, g_dn, [n_shards], n_gath)
            coll("AllGather", ALU.bypass, ALL8, sh_g, g_g, [n_shards], n_gath)
            coll("AllGather", ALU.bypass, ALL8, sh_u, g_u, [n_shards], n_gath)
            coll("AllGather", ALU.bypass, ALL8, sh_d, g_d, [n_shards], n_gath)
        with ExitStack() as st:
            stores = emit_l0_attn(p, nc, st, xctx, WqT, WkT0, WvT0, WoT0, qb, kb, vbr,
                                  lnw, maskv, st1.ap(), 0, 4, hook=gath_hook)
            for ev in stores:
                n_st1.writers[ev[0]] = max(n_st1.writers.get(ev[0], 0), ev[1])
            p.barrier_all()
        p.fence("sync", list(n_gath.writers.items()))
        with ExitStack() as st:
            stores = emit_ffn(p, nc, st, st1.ap(), g_gu.ap()[0:4 * 2048, :],
                              g_gu.ap()[4 * 2048:8 * 2048, :], g_dn.ap(),
                              st2.ap(), 5632, True, 1408)
            for ev in stores:
                n_st2.writers[ev[0]] = max(n_st2.writers.get(ev[0], 0), ev[1])
            p.barrier_all()
        coll("AllGather", ALU.bypass, PAIRS, st2, x2g, [n_st2], n_x2g)
        p.fence("gpsimd", list(n_x2g.writers.items()))
        with ExitStack() as st:
            x2ap = lambda ck, a, b: (
                x2g.ap()[ck * 128:(ck + 1) * 128, a:b] if b <= 2048 else
                x2g.ap()[2048 + ck * 128: 2048 + (ck + 1) * 128, a - 2048:b - 2048])
            stores = emit_l1_attn(p, nc, st, x2ap, maa, w1, w2, dw1, dw2, tdcol,
                                  WrT, WkT1, WvT1, WgT1, WoT1, ucol, gamr, betr,
                                  ypart.ap(), NTT)
            for ev in stores:
                n_yp.writers[ev[0]] = max(n_yp.writers.get(ev[0], 0), ev[1])
            coll("AllReduce", ALU.add, PAIRS, ypart, ysum, [n_yp], n_ys)
            p.barrier_all()
        with ExitStack() as st:
            stores = emit_combine2(p, nc, st, st2.ap(), ysum.ap(), mA, mB, st3.ap())
            for ev in stores:
                n_st3.writers[ev[0]] = max(n_st3.writers.get(ev[0], 0), ev[1])
            p.barrier_all()
        with ExitStack() as st:
            stores = emit_ffn(p, nc, st, st3.ap(), g_g.ap(), g_u.ap(), g_d.ap(),
                              x4o, 5504, False, 768)
            p.fence("sync", stores)
        p.finish()
    return nc


def mega_in_maps(inputs):
    x_parts = split_x_parts(inputs["L0_attn_x"])
    dq_maps = dequant_in_maps(inputs)
    qbh = np.ascontiguousarray(np.asarray(inputs["L0_attn_q_bias"]).reshape(16, 128).T)
    kbh = np.ascontiguousarray(np.asarray(inputs["L0_attn_k_bias"]).reshape(4, 128).T)
    vbh = np.ascontiguousarray(np.asarray(inputs["L0_attn_v_bias"]).reshape(1, 512))
    lnwh = np.ascontiguousarray(np.asarray(inputs["L0_attn_ln_weight"]).reshape(16, 128).T)
    WqTh = np.ascontiguousarray(np.asarray(inputs["L0_attn_q_project"]).T)
    WkTh = np.ascontiguousarray(np.asarray(inputs["L0_attn_k_project"]).T)
    WvTh = np.ascontiguousarray(np.asarray(inputs["L0_attn_v_project"]).T)
    WoTh = np.ascontiguousarray(np.asarray(inputs["L0_attn_o_project"]).T)
    mask_pad = np.zeros((4096,), np.float32)
    mask_pad[:2048] = -1e30
    mask_pad = np.ascontiguousarray(mask_pad.reshape(32, 128).T)
    mask_zero = np.zeros((128, 32), np.float32)
    g = lambda k: np.asarray(inputs["L1_attn_" + k], np.float32)
    maps = []
    for c in range(8):
        b, half = c // 2, c % 2
        if half == 0:
            ctxc = np.concatenate([np.zeros_like(x_parts[c]), x_parts[c]], axis=1)
            mv = mask_pad
            mAv, mBv = 1.0, 0.0
        else:
            ctxc = np.concatenate([x_parts[c - 1], x_parts[c]], axis=1)
            mv = mask_zero
            mAv, mBv = 0.0, 1.0
        asl = slice(1024 * half, 1024 * half + 1024)
        im = dict(dq_maps[c])
        im.update(
            xctx=np.ascontiguousarray(ctxc), maskv=mv,
            WqT=WqTh, WkT0=WkTh, WvT0=WvTh, WoT0=WoTh,
            qb=qbh, kb=kbh, vbr=vbh, lnw=lnwh,
            w1=np.ascontiguousarray(g("time_maa_w1")),
            w2=np.ascontiguousarray(g("time_maa_w2")),
            dw1=np.ascontiguousarray(g("time_decay_w1")),
            dw2=np.ascontiguousarray(g("time_decay_w2")[:, asl]),
            tdcol=np.ascontiguousarray(g("time_decay")[0, 0, asl].reshape(8, 128).T),
            WrT=np.ascontiguousarray(g("W_r").T[:, asl]),
            WkT1=np.ascontiguousarray(g("W_k").T[:, asl]),
            WvT1=np.ascontiguousarray(g("W_v").T[:, asl]),
            WgT1=np.ascontiguousarray(g("W_g").T[:, asl]),
            WoT1=np.ascontiguousarray(g("W_o").T[asl, :]),
            ucol=np.ascontiguousarray(g("time_faaaa")[16 * half:16 * half + 16].reshape(8, 128).T),
            gamr=np.ascontiguousarray(g("ln_gamma")[asl].reshape(1, 1024)),
            betr=np.ascontiguousarray(g("ln_beta")[asl].reshape(1, 1024)),
            mA=np.full((128, 1), mAv, np.float32),
            mB=np.full((128, 1), mBv, np.float32),
        )
        for k in ("x", "w", "k", "v", "r", "g"):
            im[f"maa_{k}"] = np.ascontiguousarray(g("time_maa_" + k)[0, 0].reshape(16, 128).T)
        maps.append(im)
    return maps


def kernel(**inputs):
    nc = build_mega()
    res = run_bass_kernel_spmd(nc, mega_in_maps(inputs), core_ids=list(range(8)))
    out = np.empty((4, 4096, 2048), np.float32)
    for c in range(8):
        b, half = c // 2, c % 2
        out[b, half * 2048:(half + 1) * 2048, :] = res.results[c]["x4"].T
    return out


def kernel_multi(**inputs):
    dq = run_dequant(inputs)
    x_parts = split_x_parts(inputs["L0_attn_x"])
    x1_parts = run_l0_attn(inputs, x_parts)
    gug = dq["w_guT_g"]
    x2_parts = run_ffn(x1_parts,
                       np.ascontiguousarray(gug[:4 * 2048]),
                       np.ascontiguousarray(gug[4 * 2048:]),
                       dq["w_downT_g"], 5632, True, 1408)
    yparts = run_l1_attn(inputs, x2_parts)
    x3_parts = run_combine(x2_parts, yparts)
    x4_parts = run_ffn(x3_parts,
                       dq["w_gateT_g"], dq["w_upT_g"], dq["w_downgT_g"],
                       5504, False, 768)
    out = np.empty((4, 4096, 2048), np.float32)
    for c in range(8):
        b, half = c // 2, c % 2
        out[b, half * 2048:(half + 1) * 2048, :] = x4_parts[c].T
    return out


# version 09
# speedup vs baseline: 1.1604x; 1.0301x over previous
import os
import numpy as np
import ml_dtypes
from contextlib import ExitStack

import concourse.bass as bass
import concourse.mybir as mybir
from concourse.bass_utils import run_bass_kernel_spmd

F32 = mybir.dt.float32
BF16 = mybir.dt.bfloat16
F16 = mybir.dt.float16
I32 = mybir.dt.int32
AF = mybir.ActivationFunctionType
ALU = mybir.AluOpType
BF = ml_dtypes.bfloat16
AX = mybir.AxisListType

ENGINES = ("sync", "gpsimd", "tensor", "vector", "scalar")


def cdiv(a, b):
    return -(-a // b)


_UNIQ = [0]


def uname(n):
    _UNIQ[0] += 1
    return f"{n}_{_UNIQ[0]}"


class Ch:
    def __init__(self, nc, st, name):
        self.sem = st.enter_context(nc.semaphore(name))
        self.n = 0
        self.name = name


class Node:

    __slots__ = ("writers", "readers", "name", "_sem")

    def __init__(self, name=""):
        self.writers = {}
        self.readers = {}
        self.name = name
        self._sem = None

    def deps_for_read(self):
        return list(self.writers.items())

    def deps_for_write(self):
        return list(self.writers.items()) + list(self.readers.items())


def nodes(prefix, n):
    return [Node(f"{prefix}{i}") for i in range(n)]


class Prog:

    ENG_CH = {"sync": "cs", "gpsimd": "cg", "tensor": "cte", "vector": "cve", "scalar": "cca"}

    def __init__(self, nc, st):
        self.nc = nc
        self.st = st
        self.q = {e: [] for e in ENGINES}
        self.wm = {e: {} for e in ENGINES}
        self.ch = {e: Ch(nc, st, self.ENG_CH[e]) for e in ENGINES}

    def _waits(self, eng, waits):
        out = []
        wm = self.wm[eng]
        for ch, v in waits:
            if v > 0 and wm.get(ch, 0) < v:
                wm[ch] = v
                out.append((ch.sem, v))
        return out

    def op(self, eng, fn, reads=(), writes=(), waits=(), sigk=1):
        ch = self.ch[eng]
        allw = list(waits)
        for n in reads:
            allw += n.deps_for_read()
        for n in writes:
            allw += n.deps_for_write()
        ws = self._waits(eng, allw)
        ch.n += sigk
        val = ch.n

        def run(e):
            for sem, v in ws:
                e.wait_ge(sem, v)
            inst = fn(e)
            assert inst is not None, "op returned no instruction"
            inst.then_inc(ch.sem, sigk)

        self.q[eng].append(run)
        ev = (ch, val)
        for n in reads:
            n.readers[ch] = val
        for n in writes:
            n.writers[ch] = val
        return ev

    def node_ch(self, node):
        if not hasattr(node, "_sem") or node._sem is None:
            node._sem = Ch(self.nc, self.st, f"nd_{node.name}_{id(node) % 9999}")
        return node._sem

    def dma(self, eng, out, in_, reads=(), writes=(), waits=(), store_node=None, **kw):
        sig = self.node_ch(writes[0] if writes else store_node)
        allw = list(waits)
        for n in reads:
            allw += n.deps_for_read()
        for n in writes:
            allw += n.deps_for_write()
        ws = self._waits(eng, allw)
        sig.n += 16
        val = sig.n

        def run(e):
            for sem, v in ws:
                e.wait_ge(sem, v)
            e.dma_start(out=out, in_=in_, **kw).then_inc(sig.sem, 16)

        self.q[eng].append(run)
        ev = (sig, val)
        for n in reads:
            n.readers[sig] = val
        for n in writes:
            n.writers[sig] = val
        return ev

    def barrier_all(self):
        evs = [(c, c.n) for c in self.ch.values()]
        for eng in ENGINES:
            ws = self._waits(eng, evs)

            def run(e, ws=ws):
                for sem, v in ws:
                    e.wait_ge(sem, v)
            self.q[eng].append(run)

    def fence(self, eng, evs):
        evs = [e for e in evs if e]
        ws = self._waits(eng, evs)
        own = [(c, v) for c, v in evs if c is self.ch[eng]]
        mx = max([v for _, v in own], default=None)

        def run(e):
            for sem, v in ws:
                e.wait_ge(sem, v)
            if mx is not None:
                e.wait_ge(self.ch[eng].sem, mx)

        self.q[eng].append(run)

    def finish(self):
        with self.nc.Block() as block:
            for eng in ENGINES:
                ops = self.q[eng]
                getattr(block, eng)(lambda e, ops=ops: [f(e) for f in ops])



GUP_SH = 1408
DWN_SH = 256
G_SH = 688
G_SH2 = 768
GD_SH = 256


def emit_dequant(p, nc, st, gup_p, gup_s, dwn_p, dwn_s, gidx, uidx, didx, lut, o_gup, o_dwn, o_g, o_u, o_d):
    CB = 2048
    sbuf = lambda name, shape, dt: st.enter_context(nc.sbuf_tensor(uname(name), shape, dt))
    pk = [sbuf(f"pk{i}", [128, CB // 8], I32) for i in range(2)]
    scl = [sbuf(f"scl{i}", [128, CB // 128], F32) for i in range(2)]
    nib = sbuf("nib", [128, CB], I32)
    t_m = sbuf("t_m", [128, CB], I32)
    t_e = sbuf("t_e", [128, CB], I32)
    t_f = sbuf("t_f", [128, CB], F32)
    t_g = sbuf("t_g", [128, CB], F32)
    val = sbuf("val", [128, CB], F32)
    obuf = [sbuf(f"obuf{i}", [128, CB], BF16) for i in range(2)]
    idxb = [sbuf(f"idxb{i}", [128, CB], I32) for i in range(2)]
    idxf = sbuf("idxf", [128, CB], BF16)
    gtmp = sbuf("gtmp", [128, CB], BF16)
    gacc = [sbuf(f"gacc{i}", [128, CB], BF16) for i in range(2)]
    lutr = sbuf("lutr", [1, 16], F32)
    lutb = sbuf("lutb", [128, 16], F32)
    ones1 = sbuf("ones1", [1, 128], F32)
    idbs = sbuf("idbs", [128, 128], BF16)
    tbuf = [sbuf(f"tbuf{i}", [128, CB], BF16) for i in range(2)]
    pslut = st.enter_context(nc.psum_tensor(uname("pslut"), [128, 16], F32))
    ps_tr = st.enter_context(nc.psum_tensor(uname("ps_tr"), [128, 1024], BF16))
    idd = nc.inline_tensor(np.eye(128, dtype=np.float32).astype(BF), name=uname("idd")).ap()

    n_pk = nodes("pk", 2)
    n_scl = nodes("scl", 2)
    n_obuf = nodes("obuf", 2)
    n_idxb = nodes("idxb", 2)
    n_gacc = nodes("gacc", 2)
    n_lutr, n_ones, n_pslut, n_lutb = Node("lutr"), Node("ones"), Node("pslut"), Node("lutb")
    n_id, n_pstr = Node("id"), Node("pstr")
    n_tbuf = nodes("tbuf", 2)
    n_store = Node("stores")
    stores = []
    p.dma("gpsimd", idbs[:], idd, writes=[n_id])

    p.dma("sync", lutr[:], lut, writes=[n_lutr])
    p.op("vector", lambda e: e.memset(ones1[:], 1.0), writes=[n_ones])
    p.op("tensor", lambda e: e.matmul(pslut[:], ones1[:], lutr[:], start=True, stop=True),
         reads=[n_lutr, n_ones], writes=[n_pslut])
    p.op("vector", lambda e: e.tensor_copy(out=lutb[:], in_=pslut[:]),
         reads=[n_pslut], writes=[n_lutb])

    ti = 0

    def tr_store(srcbuf, n_src, dstT, rows, cols, b):
        nblk = cols // 128
        for tb_ in range(nblk):
            p.op("tensor", lambda e, tb_=tb_, srcbuf=srcbuf, rows=rows: e.transpose(
                ps_tr[:, tb_ % 2 * 128: tb_ % 2 * 128 + rows],
                srcbuf[:rows, tb_ * 128:(tb_ + 1) * 128], idbs[:rows, :rows]),
                reads=[n_src, n_id], writes=[n_pstr])
            p.op("vector", lambda e, tb_=tb_, rows=rows, b=b: e.tensor_copy(
                out=tbuf[b][:, tb_ * rows:(tb_ + 1) * rows],
                in_=ps_tr[:, tb_ % 2 * 128: tb_ % 2 * 128 + rows]),
                reads=[n_pstr], writes=[n_tbuf[b]])
        stores.append(p.dma(
            "sync", dstT.rearrange("(nb q) r -> q nb r", q=128),
            tbuf[b][:, :nblk * rows].rearrange("q (nb r) -> q nb r", r=rows),
            reads=[n_tbuf[b]], store_node=n_store))

    def fp4_tile(dst, packed_src, scales_src, rows, wq, cols, sgroups):
        nonlocal ti
        b = ti % 2
        ti += 1
        p.dma("gpsimd", pk[b][:rows, :wq], packed_src, writes=[n_pk[b]])
        p.dma("gpsimd", scl[b][:rows, :sgroups], scales_src, writes=[n_scl[b]])

        def decode(e, b=b, rows=rows, wq=wq, cols=cols, sgroups=sgroups):
            for k in range(8):
                e.tensor_scalar(out=nib[:rows, k:cols:8], in0=pk[b][:rows, :wq],
                                scalar1=4 * k, scalar2=15,
                                op0=ALU.logical_shift_right, op1=ALU.bitwise_and)
            e.tensor_scalar(out=t_m[:rows, :cols], in0=nib[:rows, :cols], scalar1=1,
                            scalar2=None, op0=ALU.bitwise_and, op1=ALU.bypass)
            e.tensor_scalar(out=t_e[:rows, :cols], in0=nib[:rows, :cols], scalar1=7,
                            scalar2=1, op0=ALU.bitwise_and, op1=ALU.logical_shift_right)
            e.tensor_scalar(out=t_f[:rows, :cols].bitcast(I32), in0=t_e[:rows, :cols],
                            scalar1=126, scalar2=1 << 23, op0=ALU.add, op1=ALU.mult)
            e.tensor_copy(out=t_g[:rows, :cols], in_=t_m[:rows, :cols])
            e.tensor_scalar(out=t_g[:rows, :cols], in0=t_g[:rows, :cols], scalar1=0.5,
                            scalar2=1.0, op0=ALU.mult, op1=ALU.add)
            e.tensor_mul(val[:rows, :cols], t_f[:rows, :cols], t_g[:rows, :cols])
            e.tensor_scalar(out=t_g[:rows, :cols], in0=t_g[:rows, :cols], scalar1=1.0,
                            scalar2=None, op0=ALU.subtract, op1=ALU.bypass)
            e.tensor_scalar(out=t_m[:rows, :cols], in0=t_e[:rows, :cols], scalar1=0,
                            scalar2=None, op0=ALU.is_equal, op1=ALU.bypass)
            e.select(out=val[:rows, :cols], mask=t_m[:rows, :cols],
                     on_true=t_g[:rows, :cols], on_false=val[:rows, :cols])
            e.tensor_scalar(out=t_e[:rows, :cols], in0=nib[:rows, :cols], scalar1=3,
                            scalar2=None, op0=ALU.logical_shift_right, op1=ALU.bypass)
            e.tensor_copy(out=t_g[:rows, :cols], in_=t_e[:rows, :cols])
            e.tensor_scalar(out=t_g[:rows, :cols], in0=t_g[:rows, :cols], scalar1=-2.0,
                            scalar2=1.0, op0=ALU.mult, op1=ALU.add)
            e.tensor_mul(val[:rows, :cols], val[:rows, :cols], t_g[:rows, :cols])
            sclv = scl[b][:rows, :sgroups].unsqueeze(2).broadcast_to([rows, sgroups, 128])
            return e.tensor_tensor(
                out=obuf[b][:rows, :cols].rearrange("p (g q) -> p g q", q=128),
                in0=val[:rows, :cols].rearrange("p (g q) -> p g q", q=128),
                in1=sclv, op=ALU.mult)

        p.op("vector", decode, reads=[n_pk[b], n_scl[b]], writes=[n_obuf[b]])
        tr_store(obuf[b], n_obuf[b], dst, rows, cols, b)

    def fp4_mat(dstT, src_p, src_s, R, COLS):
        for i in range(cdiv(R, 128)):
            r0 = i * 128
            rows = min(128, R - r0)
            for c0 in range(0, COLS, CB):
                cols = min(CB, COLS - c0)
                fp4_tile(dstT[c0:c0 + cols, r0:r0 + rows],
                         src_p[r0:r0 + rows, c0 // 8:(c0 + cols) // 8],
                         src_s[r0:r0 + rows, c0 // 128:(c0 + cols) // 128],
                         rows, cols // 8, cols, cols // 128)

    fp4_mat(o_gup, gup_p, gup_s, GUP_SH, 2048)
    fp4_mat(o_dwn, dwn_p, dwn_s, DWN_SH, 5632)

    gi = 0

    def ghost_tile(dst, idx_src, rows, cols):
        nonlocal gi
        b = gi % 2
        gi += 1
        p.dma("gpsimd", idxb[b][:rows, :cols], idx_src, writes=[n_idxb[b]])

        def comp(e, b=b, rows=rows, cols=cols):
            e.tensor_copy(out=idxf[:rows, :cols], in_=idxb[b][:rows, :cols])
            e.memset(gacc[b][:rows, :cols], 0.0)
            last = None
            for j in range(16):
                e.tensor_scalar(out=gtmp[:rows, :cols], in0=idxf[:rows, :cols],
                                scalar1=float(j), scalar2=lutb[:rows, j:j + 1],
                                op0=ALU.is_equal, op1=ALU.mult)
                last = e.tensor_add(gacc[b][:rows, :cols], gacc[b][:rows, :cols],
                                    gtmp[:rows, :cols])
            return last

        p.op("vector", comp, reads=[n_idxb[b], n_lutb], writes=[n_gacc[b]])
        tr_store(gacc[b], n_gacc[b], dst, rows, cols, b)

    def ghost_mat(dstT, src, R, COLS):
        for i in range(cdiv(R, 128)):
            r0 = i * 128
            rows = min(128, R - r0)
            for c0 in range(0, COLS, CB):
                cols = min(CB, COLS - c0)
                ghost_tile(dstT[c0:c0 + cols, r0:r0 + rows],
                           src[r0:r0 + rows, c0:c0 + cols], rows, cols)

    ghost_mat(o_g, gidx, G_SH2, 2048)
    ghost_mat(o_u, uidx, G_SH2, 2048)
    ghost_mat(o_d, didx, GD_SH, 5504)

    return stores


def build_dequant():
    nc = bass.Bass("TRN2", target_bir_lowering=False, debug=False, detect_race_conditions=bool(os.environ.get("RACEDET")))
    C = 2048
    gup_p = nc.dram_tensor("gup_packed", [GUP_SH, C // 8], I32, kind="ExternalInput").ap()
    gup_s = nc.dram_tensor("gup_scales", [GUP_SH, 16], F16, kind="ExternalInput").ap()
    dwn_p = nc.dram_tensor("down_packed", [DWN_SH, 5632 // 8], I32, kind="ExternalInput").ap()
    dwn_s = nc.dram_tensor("down_scales", [DWN_SH, 44], F16, kind="ExternalInput").ap()
    gidx = nc.dram_tensor("gate_idx", [G_SH2, C], I32, kind="ExternalInput").ap()
    uidx = nc.dram_tensor("up_idx", [G_SH2, C], I32, kind="ExternalInput").ap()
    didx = nc.dram_tensor("down_idx", [GD_SH, 5504], I32, kind="ExternalInput").ap()
    lut = nc.dram_tensor("lut", [1, 16], F32, kind="ExternalInput").ap()

    o_gup = nc.dram_tensor("w_guT", [C, GUP_SH], BF16, kind="ExternalOutput").ap()
    o_dwn = nc.dram_tensor("w_downT", [5632, DWN_SH], BF16, kind="ExternalOutput").ap()
    o_g = nc.dram_tensor("w_gateT", [C, G_SH2], BF16, kind="ExternalOutput").ap()
    o_u = nc.dram_tensor("w_upT", [C, G_SH2], BF16, kind="ExternalOutput").ap()
    o_d = nc.dram_tensor("w_downgT", [5504, GD_SH], BF16, kind="ExternalOutput").ap()

    with ExitStack() as st:
        p = Prog(nc, st)
        stores = emit_dequant(p, nc, st, gup_p, gup_s, dwn_p, dwn_s, gidx, uidx, didx, lut, o_gup, o_dwn, o_g, o_u, o_d)
        p.fence("sync", stores)
        p.finish()
    return nc

def dequant_in_maps(inputs):
    gup_sT = np.ascontiguousarray(np.asarray(inputs["L0_ffn_gate_up_scales"]).T)
    dwn_sT = np.ascontiguousarray(np.asarray(inputs["L0_ffn_down_scales"]).T)
    lut = np.ascontiguousarray(np.asarray(inputs["L1_ffn_lut"]).reshape(1, 16))
    gidx = np.zeros((6144, 2048), np.int32)
    gidx[:5504] = inputs["L1_ffn_gate_idx"]
    uidx = np.zeros((6144, 2048), np.int32)
    uidx[:5504] = inputs["L1_ffn_up_idx"]
    maps = []
    for c in range(8):
        maps.append(dict(
            gup_packed=np.ascontiguousarray(inputs["L0_ffn_gate_up_packed"][c * GUP_SH:(c + 1) * GUP_SH]),
            gup_scales=np.ascontiguousarray(gup_sT[c * GUP_SH:(c + 1) * GUP_SH]),
            down_packed=np.ascontiguousarray(inputs["L0_ffn_down_packed"][c * DWN_SH:(c + 1) * DWN_SH]),
            down_scales=np.ascontiguousarray(dwn_sT[c * DWN_SH:(c + 1) * DWN_SH]),
            gate_idx=np.ascontiguousarray(gidx[c * G_SH2:(c + 1) * G_SH2]),
            up_idx=np.ascontiguousarray(uidx[c * G_SH2:(c + 1) * G_SH2]),
            down_idx=np.ascontiguousarray(inputs["L1_ffn_down_idx"][c * GD_SH:(c + 1) * GD_SH]),
            lut=lut,
        ))
    return maps


def run_dequant(inputs):
    nc = build_dequant()
    res = run_bass_kernel_spmd(nc, dequant_in_maps(inputs), core_ids=list(range(8)))
    cat = lambda k: np.ascontiguousarray(np.concatenate([r[k] for r in res.results], axis=0))
    return dict(w_guT_g=cat("w_guT"), w_downT_g=cat("w_downT"), w_gateT_g=cat("w_gateT"),
                w_upT_g=cat("w_upT"), w_downgT_g=cat("w_downgT"))



C = 2048
CK = 16
TCTX = 4096
TOWN = 2048
NHQ = 16
NHKV = 4
D = 128


def _rope_tables():
    inv = 1.0 / (10000.0 ** (np.arange(0, D, 2, dtype=np.float64) / D))
    ang = np.arange(TCTX, dtype=np.float64)[:, None] * inv[None, :]
    cos = np.concatenate([np.cos(ang), np.cos(ang)], axis=1).T
    sin = np.concatenate([np.sin(ang), np.sin(ang)], axis=1).T
    return np.ascontiguousarray(cos.astype(np.float32)), np.ascontiguousarray(sin.astype(np.float32))


def _rmat():
    R = np.zeros((128, 128), np.float32)
    for j in range(64):
        R[64 + j, j] = -1.0
        R[j, 64 + j] = 1.0
    return R


def emit_l0_attn(p, nc, st, ctx, WqT, WkT, WvT, WoT, qb, kb, vbr, lnw, maskv, x1o, tq_lo, tq_hi, hook=None):
    cosT, sinT = _rope_tables()
    cosd = nc.inline_tensor(cosT, name=uname("cosT")).ap()
    sind = nc.inline_tensor(sinT, name=uname("sinT")).ap()
    Rd = nc.inline_tensor(_rmat(), name=uname("Rm")).ap()

    sb = lambda name, shape, dt=F32: st.enter_context(nc.sbuf_tensor(uname(name), shape, dt))
    psal = lambda name, shape: st.enter_context(nc.psum_tensor(uname(name), shape, F32))

    wv = sb("wv", [128, 16 * 512], BF16)
    ksb = sb("ksb", [128, 4 * 4096], BF16)
    vsb = sb("vsb", [128, 32 * 512], BF16)
    qsb = sb("qsb", [128, 16 * 512], BF16)
    hsb = sb("hsb", [128, 16 * 512], BF16)
    atn = sb("atn", [128, 16 * 512], BF16)
    x1s = sb("x1s", [128, 16 * 512], F32)
    Rsb = sb("Rsb", [128, 128], F32)
    cossb = sb("cossb", [128, 512], F32)
    sinsb = sb("sinsb", [128, 512], F32)
    qbs = sb("qbs", [128, 16], F32)
    kbs = sb("kbs", [128, 4], F32)
    vbs = sb("vbs", [1, 512], BF16)
    lnws = sb("lnws", [128, 16], F32)
    maskvs = sb("maskvs", [128, 32], F32)
    onesc = sb("onesc", [128, 1], BF16)
    onescol = sb("onescol", [128, 1], F32)
    onescf = sb("onescf", [1, 128], F32)
    zerocol = sb("zerocol", [1, 128], BF16)
    zrow = sb("zrow", [1, 128], BF16)
    onesrb = sb("onesrb", [1, 128], BF16)
    k2m = sb("k2m", [1, 4], F32)
    rowbuf = sb("rowbuf", [1, 512], F32)
    rowbuf2 = sb("rowbuf2", [1, 512], F32)
    negM = sb("negM", [1, 512], BF16)
    rbsb = sb("rbsb", [128, 512], F32)
    rbsb2 = rbsb
    ctxt = [sb(f"ctxt{i}", [128, 512]) for i in range(2)]
    sqt = sb("sqt", [128, 512], BF16)
    sqf = [sb(f"sqf{i}", [128, 512]) for i in range(2)]
    kf = sb("kf", [128, 512], F32)
    s1b = sb("s1b", [128, 512], F32)
    t2f = sb("t2f", [128, 512], F32)
    probs = [sb(f"probs{i}", [128, 512], BF16) for i in range(2)]
    wqt = [sb(f"wqt{i}", [128, 16 * 128], BF16) for i in range(2)]
    wot = [sb(f"wot{i}", [128, 16 * 128], BF16) for i in range(2)]

    ps_ss = psal("ps_ss", [1, 512])
    ps_ss2 = psal("ps_ss2", [1, 512])
    ps_b = psal("ps_b", [128, 512])
    ps_b2 = psal("ps_b2", [128, 512])
    ps_k = psal("ps_k", [128, 512])
    ps_r = ps_b
    ps_s = [psal(f"ps_s{i}", [128, 512]) for i in range(2)]
    ps_o = psal("ps_o", [128, 512])

    N = Node
    n_wv = N("wv")
    n_ksb, n_vsb, n_qsb, n_hsb, n_atn, n_x1s = N("ksb"), N("vsb"), N("qsb"), N("hsb"), N("atn"), N("x1s")
    n_R, n_cos, n_sin = N("R"), N("cos"), N("sin")
    n_qbs, n_kbs, n_vbs, n_lnws, n_maskv = N("qbs"), N("kbs"), N("vbs"), N("lnws"), N("maskv")
    n_onesc, n_onescol, n_onescf, n_onesrb = N("onesc"), N("onescol"), N("onescf"), N("onesrb")
    n_k2m, n_row, n_row2, n_negM = N("k2m"), N("row"), N("row2"), N("negM")
    n_rbsb = N("rbsb")
    n_rbsb2 = n_rbsb
    n_ctxt = nodes("ctxt", 2)
    n_sqt = N("sqt")
    n_sqf = nodes("sqf", 2)
    n_kf, n_s1b, n_t2f = N("kf"), N("s1b"), N("t2f")
    n_probs = nodes("probs", 2)
    n_wqt = nodes("wqt", 2)
    n_wot = nodes("wot", 2)
    n_pss, n_pss2, n_psb, n_psb2, n_psk, n_pso = (
        N("pss"), N("pss2"), N("psb"), N("psb2"), N("psk"), N("pso"))
    n_psr = n_psb
    n_pssc = nodes("pssc", 2)
    n_store = Node("stores")
    stores = []

    p.dma("gpsimd", Rsb[:], Rd, writes=[n_R])
    p.dma("gpsimd", qbs[:], qb, writes=[n_qbs])
    p.dma("gpsimd", kbs[:], kb, writes=[n_kbs])
    p.dma("gpsimd", vbs[:], vbr, writes=[n_vbs])
    p.dma("gpsimd", lnws[:], lnw, writes=[n_lnws])
    p.dma("gpsimd", maskvs[:], maskv, writes=[n_maskv])
    for i in range(CK):
        p.dma("gpsimd", wv[:, i * 512:(i + 1) * 512], WvT[i * 128:(i + 1) * 128, :], writes=[n_wv])
    p.op("vector", lambda e: e.memset(onesc[:], 1.0), writes=[n_onesc])
    p.op("vector", lambda e: e.memset(onescol[:], 1.0), writes=[n_onescol])
    p.op("vector", lambda e: e.memset(onescf[:], 1.0), writes=[n_onescf])
    p.op("vector", lambda e: e.memset(onesrb[:], 1.0), writes=[n_onesrb])
    p.op("vector", lambda e: e.memset(k2m[:], 0.0), writes=[n_k2m])

    def rms_h(t0):
        for ck in range(CK):
            b = ck % 2
            sq = ck % 2
            p.dma("gpsimd", ctxt[b][:], ctx[ck * 128:(ck + 1) * 128, t0:t0 + 512],
                  writes=[n_ctxt[b]])
            p.op("scalar", lambda e, b=b, sq=sq: e.activation(out=sqf[sq][:], in_=ctxt[b][:], func=AF.Square),
                 reads=[n_ctxt[b]], writes=[n_sqf[sq]])
            p.op("tensor", lambda e, ck=ck, sq=sq: e.matmul(
                ps_ss[:], onescol[:], sqf[sq][:], start=(ck == 0), stop=(ck == CK - 1)),
                reads=[n_onescol, n_sqf[sq]], writes=[n_pss])
        p.op("vector", lambda e: e.tensor_scalar(
            out=rowbuf[:], in0=ps_ss[:], scalar1=1.0 / 2048.0, scalar2=1e-6,
            op0=ALU.mult, op1=ALU.add), reads=[n_pss], writes=[n_row])
        p.op("scalar", lambda e: e.activation(out=rowbuf[:], in_=rowbuf[:], func=AF.Ln),
             reads=[n_row], writes=[n_row])
        p.op("scalar", lambda e: e.activation(out=rowbuf[:], in_=rowbuf[:], func=AF.Exp,
                                              scale=-0.5), reads=[n_row], writes=[n_row])
        p.op("tensor", lambda e: e.matmul(ps_b[:], onescf[:], rowbuf[:], start=True, stop=True),
             reads=[n_onescf, n_row], writes=[n_psb])
        for ck in range(CK):
            b = ck % 2
            p.dma("gpsimd", ctxt[b][:], ctx[ck * 128:(ck + 1) * 128, t0:t0 + 512],
                  writes=[n_ctxt[b]])
            p.op("vector", lambda e, ck=ck, b=b: e.scalar_tensor_tensor(
                out=hsb[:, ck * 512:(ck + 1) * 512], in0=ctxt[b][:],
                scalar=lnws[:, ck:ck + 1], in1=ps_b[:],
                op0=ALU.mult, op1=ALU.mult),
                reads=[n_ctxt[b], n_lnws, n_psb], writes=[n_hsb])

    def rope_tile(ps_src, n_ps_src, bias_col, n_bias, out_ap, n_out, t0):
        p.dma("gpsimd", cossb[:], cosd[:, t0:t0 + 512], writes=[n_cos])
        p.dma("gpsimd", sinsb[:], sind[:, t0:t0 + 512], writes=[n_sin])
        p.op("vector", lambda e: e.tensor_scalar(
            out=kf[:], in0=ps_src[:], scalar1=bias_col, scalar2=None,
            op0=ALU.add, op1=ALU.bypass), reads=[n_ps_src, n_bias], writes=[n_kf])
        p.op("vector", lambda e: e.tensor_mul(s1b[:], kf[:], sinsb[:]),
             reads=[n_kf, n_sin], writes=[n_s1b])
        p.op("tensor", lambda e: e.matmul(ps_r[:], Rsb[:], s1b[:], start=True, stop=True),
             reads=[n_R, n_s1b], writes=[n_psr])
        p.op("vector", lambda e: e.tensor_mul(t2f[:], kf[:], cossb[:]),
             reads=[n_kf, n_cos], writes=[n_t2f])
        p.op("vector", lambda e: e.tensor_add(out_ap, t2f[:], ps_r[:]),
             reads=[n_t2f, n_psr], writes=[n_out])

    for tb in range(8):
        t0 = tb * 512
        rms_h(t0)
        for ak in range(4):
            wbi = ak % 2
            p.dma("gpsimd", wqt[wbi][:].rearrange("p (k a) -> p k a", k=16),
                  WkT[:, ak * 128:(ak + 1) * 128].rearrange("(k p) a -> p k a", p=128),
                  writes=[n_wqt[wbi]])

            def kproj(e, wbi=wbi):
                last = None
                for ck in range(CK):
                    last = e.matmul(
                        ps_k[:], wqt[wbi][:, ck * 128:(ck + 1) * 128],
                        hsb[:, ck * 512:(ck + 1) * 512],
                        start=(ck == 0), stop=(ck == CK - 1))
                return last
            p.op("tensor", kproj, reads=[n_wqt[wbi], n_hsb], writes=[n_psk])
            rope_tile(ps_k, n_psk, kbs[:, ak:ak + 1], n_kbs,
                      ksb[:, ak * 4096 + t0: ak * 4096 + t0 + 512], n_ksb, t0)
            p.op("scalar", lambda e, ak=ak, t0=t0: e.activation(
                out=sqt[:], in_=ksb[:, ak * 4096 + t0: ak * 4096 + t0 + 512], func=AF.Square),
                reads=[n_ksb], writes=[n_sqt])
            p.op("tensor", lambda e: e.matmul(ps_ss2[:], onesc[:], sqt[:], start=True, stop=True),
                 reads=[n_onesc, n_sqt], writes=[n_pss2])
            p.op("vector", lambda e: e.reduce_max(rowbuf2[:, 0:1], ps_ss2[:], axis=AX.X),
                 reads=[n_pss2], writes=[n_row2])
            p.op("vector", lambda e, ak=ak: e.tensor_tensor(
                out=k2m[:, ak:ak + 1], in0=k2m[:, ak:ak + 1], in1=rowbuf2[:, 0:1],
                op=ALU.max), reads=[n_row2, n_k2m], writes=[n_k2m])
        for tt in range(4):
            tg = tb * 4 + tt

            def vproj(e, tt=tt):
                for ck in range(CK):
                    e.matmul(ps_k[:], hsb[:, ck * 512 + tt * 128: ck * 512 + (tt + 1) * 128],
                             wv[:, ck * 512:(ck + 1) * 512],
                             start=(ck == 0), stop=False)
                return e.matmul(ps_k[:], onesrb[:], vbs[:], start=False, stop=True)
            p.op("tensor", vproj, reads=[n_hsb, n_wv, n_onesrb, n_vbs], writes=[n_psk])
            p.op("vector", lambda e, tg=tg: e.tensor_copy(
                out=vsb[:, tg * 512:(tg + 1) * 512], in_=ps_k[:]),
                reads=[n_psk], writes=[n_vsb])

    if hook is not None:
        hook()
    SCALE = float(1.0 / np.sqrt(128.0))
    for tqg in range(tq_lo, tq_hi):
        if os.environ.get("TQBAR"):
            p.barrier_all()
        t0 = 2048 + tqg * 512
        rms_h(t0)
        for ak in range(NHQ):
            wbi = ak % 2
            p.dma("gpsimd", wqt[wbi][:].rearrange("p (k a) -> p k a", k=16),
                  WqT[:, ak * 128:(ak + 1) * 128].rearrange("(k p) a -> p k a", p=128),
                  writes=[n_wqt[wbi]])

            def qproj(e, wbi=wbi):
                last = None
                for ck in range(CK):
                    last = e.matmul(ps_k[:], wqt[wbi][:, ck * 128:(ck + 1) * 128],
                                    hsb[:, ck * 512:(ck + 1) * 512],
                                    start=(ck == 0), stop=(ck == CK - 1))
                return last
            p.op("tensor", qproj, reads=[n_wqt[wbi], n_hsb], writes=[n_psk])
            rope_tile(ps_k, n_psk, qbs[:, ak:ak + 1], n_qbs,
                      qsb[:, ak * 512:(ak + 1) * 512], n_qsb, t0)

        n_kv = (2048 + (tqg + 1) * 512) // 128
        for hq in range(NHQ):
            g = hq // 4
            p.op("scalar", lambda e, hq=hq: e.activation(
                out=sqt[:], in_=qsb[:, hq * 512:(hq + 1) * 512], func=AF.Square),
                reads=[n_qsb], writes=[n_sqt])
            p.op("tensor", lambda e: e.matmul(ps_ss[:], onesc[:], sqt[:], start=True, stop=True),
                 reads=[n_onesc, n_sqt], writes=[n_pss])
            p.op("vector", lambda e, g=g: e.tensor_scalar(
                out=negM[:], in0=ps_ss[:], scalar1=k2m[:, g:g + 1], scalar2=-0.5,
                op0=ALU.add, op1=ALU.mult), reads=[n_pss, n_k2m], writes=[n_negM])
            for j in range(n_kv):
                jb = j % 2

                def scores(e, g=g, j=j, jb=jb, hq=hq):
                    e.matmul(ps_s[jb][:],
                             ksb[:, g * 4096 + j * 128: g * 4096 + (j + 1) * 128],
                             qsb[:, hq * 512:(hq + 1) * 512], start=True, stop=False)
                    return e.matmul(ps_s[jb][:], onesrb[:], negM[:], start=False, stop=True)
                p.op("tensor", scores, reads=[n_ksb, n_qsb, n_onesrb, n_negM],
                     writes=[n_pssc[jb]])
                p.op("scalar", lambda e, j=j, jb=jb: e.activation(
                    out=probs[jb][:], in_=ps_s[jb][:], func=AF.Exp,
                    bias=maskvs[:, j:j + 1], scale=SCALE),
                    reads=[n_pssc[jb], n_maskv], writes=[n_probs[jb]])
                if (j + 1) * 128 > t0:
                    p.op("gpsimd", lambda e, j=j, jb=jb, t0=t0: e.affine_select(
                        out=probs[jb][:], in_=probs[jb][:],
                        pattern=[[1, 512]], compare_op=ALU.is_ge, fill=0.0,
                        base=t0 - j * 128, channel_multiplier=-1),
                        reads=[n_probs[jb]], writes=[n_probs[jb]])

                def pv(e, j=j, g=g, jb=jb, n_kv=n_kv):
                    e.matmul(ps_o[:], vsb[:, j * 512 + g * 128: j * 512 + (g + 1) * 128],
                             probs[jb][:], start=(j == 0), stop=(j == n_kv - 1))
                    return e.matmul(ps_ss2[:], onesc[:], probs[jb][:],
                                    start=(j == 0), stop=(j == n_kv - 1))
                p.op("tensor", pv, reads=[n_vsb, n_probs[jb], n_onesc],
                     writes=[n_pso, n_pss2])
            p.op("vector", lambda e: e.tensor_scalar(
                out=rowbuf[:], in0=ps_ss2[:], scalar1=1e-30, scalar2=None,
                op0=ALU.add, op1=ALU.bypass), reads=[n_pss2], writes=[n_row])
            p.op("vector", lambda e: e.reciprocal(out=rowbuf[:], in_=rowbuf[:]),
                 reads=[n_row], writes=[n_row])
            p.op("tensor", lambda e: e.matmul(ps_b2[:], onescf[:], rowbuf[:], start=True, stop=True),
                 reads=[n_onescf, n_row], writes=[n_psb2])
            p.op("vector", lambda e: e.tensor_copy(out=rbsb[:], in_=ps_b2[:]),
                 reads=[n_psb2], writes=[n_rbsb])
            p.op("vector", lambda e, hq=hq: e.tensor_mul(
                atn[:, hq * 512:(hq + 1) * 512], ps_o[:], rbsb[:]),
                reads=[n_pso, n_rbsb], writes=[n_atn])

        for ck in range(CK):
            wbi = ck % 2
            p.dma("gpsimd", wot[wbi][:].rearrange("p (k c) -> p k c", k=16),
                  WoT[:, ck * 128:(ck + 1) * 128].rearrange("(k p) c -> p k c", p=128),
                  writes=[n_wot[wbi]])

            def oproj(e, wbi=wbi):
                last = None
                for ak in range(NHQ):
                    last = e.matmul(ps_k[:], wot[wbi][:, ak * 128:(ak + 1) * 128],
                                    atn[:, ak * 512:(ak + 1) * 512],
                                    start=(ak == 0), stop=(ak == NHQ - 1))
                return last
            p.op("tensor", oproj, reads=[n_wot[wbi], n_atn], writes=[n_psk])
            b = ck % 2
            p.dma("gpsimd", ctxt[b][:], ctx[ck * 128:(ck + 1) * 128, t0:t0 + 512],
                  writes=[n_ctxt[b]])
            p.op("vector", lambda e, ck=ck, b=b: e.tensor_add(
                x1s[:, ck * 512:(ck + 1) * 512], ps_k[:], ctxt[b][:]),
                reads=[n_psk, n_ctxt[b]], writes=[n_x1s])

        for ck in range(CK):
            sq = ck % 2
            p.op("tensor", lambda e, ck=ck: e.matmul(
                ps_ss[:], onescol[:], x1s[:, ck * 512:(ck + 1) * 512],
                start=(ck == 0), stop=(ck == CK - 1)),
                reads=[n_onescol, n_x1s], writes=[n_pss])
            p.op("scalar", lambda e, ck=ck, sq=sq: e.activation(
                out=sqf[sq][:], in_=x1s[:, ck * 512:(ck + 1) * 512], func=AF.Square),
                reads=[n_x1s], writes=[n_sqf[sq]])
            p.op("tensor", lambda e, ck=ck, sq=sq: e.matmul(
                ps_ss2[:], onescol[:], sqf[sq][:],
                start=(ck == 0), stop=(ck == CK - 1)),
                reads=[n_onescol, n_sqf[sq]], writes=[n_pss2])
        p.op("vector", lambda e: e.tensor_scalar(
            out=rowbuf[:], in0=ps_ss[:], scalar1=1.0 / 2048.0, scalar2=None,
            op0=ALU.mult, op1=ALU.bypass), reads=[n_pss], writes=[n_row])
        p.op("vector", lambda e: e.tensor_mul(rowbuf2[:], rowbuf[:], rowbuf[:]),
             reads=[n_row], writes=[n_row2])
        p.op("vector", lambda e: e.tensor_scalar(
            out=rowbuf2[:], in0=rowbuf2[:], scalar1=1e-5, scalar2=None,
            op0=ALU.subtract, op1=ALU.bypass), reads=[n_row2], writes=[n_row2])
        p.op("vector", lambda e: e.scalar_tensor_tensor(
            out=rowbuf2[:], in0=ps_ss2[:], scalar=1.0 / 2048.0, in1=rowbuf2[:],
            op0=ALU.mult, op1=ALU.subtract), reads=[n_pss2, n_row2], writes=[n_row2])
        p.op("scalar", lambda e: e.activation(out=rowbuf2[:], in_=rowbuf2[:], func=AF.Ln),
             reads=[n_row2], writes=[n_row2])
        p.op("scalar", lambda e: e.activation(out=rowbuf2[:], in_=rowbuf2[:], func=AF.Exp,
                                              scale=-0.5), reads=[n_row2], writes=[n_row2])
        p.op("tensor", lambda e: e.matmul(ps_b[:], onescf[:], rowbuf[:], start=True, stop=True),
             reads=[n_onescf, n_row], writes=[n_psb])
        p.op("tensor", lambda e: e.matmul(ps_b2[:], onescf[:], rowbuf2[:], start=True, stop=True),
             reads=[n_onescf, n_row2], writes=[n_psb2])
        p.op("vector", lambda e: e.tensor_copy(out=rbsb2[:], in_=ps_b2[:]),
             reads=[n_psb2], writes=[n_rbsb2])
        for ck in range(CK):
            p.op("vector", lambda e, ck=ck: e.tensor_sub(
                x1s[:, ck * 512:(ck + 1) * 512], x1s[:, ck * 512:(ck + 1) * 512], ps_b[:]),
                reads=[n_psb], writes=[n_x1s])
            p.op("vector", lambda e, ck=ck: e.tensor_mul(
                x1s[:, ck * 512:(ck + 1) * 512], x1s[:, ck * 512:(ck + 1) * 512], rbsb2[:]),
                reads=[n_rbsb2], writes=[n_x1s])
            stores.append(p.dma(
                "sync", x1o[ck * 128:(ck + 1) * 128, tqg * 512:(tqg + 1) * 512],
                x1s[:, ck * 512:(ck + 1) * 512], reads=[n_x1s], store_node=n_store))

    return stores


def build_l0_attn(tq_lo=0, tq_hi=4):
    nc = bass.Bass("TRN2", target_bir_lowering=False, debug=False, detect_race_conditions=bool(os.environ.get("RACEDET")))
    ctx = nc.dram_tensor("ctx", [C, TCTX], F32, kind="ExternalInput").ap()
    WqT = nc.dram_tensor("WqT", [C, 2048], F32, kind="ExternalInput").ap()
    WkT = nc.dram_tensor("WkT", [C, 512], F32, kind="ExternalInput").ap()
    WvT = nc.dram_tensor("WvT", [C, 512], F32, kind="ExternalInput").ap()
    WoT = nc.dram_tensor("WoT", [2048, C], F32, kind="ExternalInput").ap()
    qb = nc.dram_tensor("qb", [128, 16], F32, kind="ExternalInput").ap()
    kb = nc.dram_tensor("kb", [128, 4], F32, kind="ExternalInput").ap()
    vbr = nc.dram_tensor("vbr", [1, 512], F32, kind="ExternalInput").ap()
    lnw = nc.dram_tensor("lnw", [128, 16], F32, kind="ExternalInput").ap()
    maskv = nc.dram_tensor("maskv", [128, 32], F32, kind="ExternalInput").ap()
    x1o = nc.dram_tensor("x1", [C, TOWN], F32, kind="ExternalOutput").ap()

    with ExitStack() as st:
        p = Prog(nc, st)
        stores = emit_l0_attn(p, nc, st, ctx, WqT, WkT, WvT, WoT, qb, kb, vbr, lnw, maskv, x1o, tq_lo, tq_hi)
        p.fence("sync", stores)
        p.finish()
    return nc

def run_l0_attn(inputs, x_parts):
    nc = build_l0_attn()
    qb = np.ascontiguousarray(inputs["L0_attn_q_bias"].reshape(16, 128).T)
    kb = np.ascontiguousarray(inputs["L0_attn_k_bias"].reshape(4, 128).T)
    vb = np.ascontiguousarray(inputs["L0_attn_v_bias"].reshape(1, 512))
    lnw = np.ascontiguousarray(inputs["L0_attn_ln_weight"].reshape(16, 128).T)
    WqT = np.ascontiguousarray(np.asarray(inputs["L0_attn_q_project"]).T)
    WkT = np.ascontiguousarray(np.asarray(inputs["L0_attn_k_project"]).T)
    WvT = np.ascontiguousarray(np.asarray(inputs["L0_attn_v_project"]).T)
    WoT = np.ascontiguousarray(np.asarray(inputs["L0_attn_o_project"]).T)
    mask_pad = np.zeros((4096,), np.float32)
    mask_pad[:2048] = -1e30
    mask_pad = np.ascontiguousarray(mask_pad.reshape(32, 128).T)
    mask_zero = np.zeros((128, 32), np.float32)
    in_maps = []
    for c in range(8):
        half = c % 2
        if half == 0:
            ctxc = np.concatenate([np.zeros_like(x_parts[c]), x_parts[c]], axis=1)
            mv = mask_pad
        else:
            ctxc = np.concatenate([x_parts[c - 1], x_parts[c]], axis=1)
            mv = mask_zero
        in_maps.append(dict(ctx=np.ascontiguousarray(ctxc), WqT=WqT, WkT=WkT, WvT=WvT,
                            WoT=WoT, qb=qb, kb=kb, vbr=vb, lnw=lnw, maskv=mv))
    res = run_bass_kernel_spmd(nc, in_maps, core_ids=list(range(8)))
    return [r["x1"] for r in res.results]


def split_x_parts(x):
    xt = np.ascontiguousarray(np.asarray(x).transpose(0, 2, 1))
    parts = []
    for c in range(8):
        b, half = c // 2, c % 2
        parts.append(np.ascontiguousarray(xt[b, :, half * 2048:(half + 1) * 2048]))
    return parts


def kernel(**inputs):
    raise NotImplementedError("phase A under construction")



def emit_ffn(p, nc, st, x1, wg_g, wu_g, wd_g, x2o, I, rms_ep, SHW):
    IK = I // 128
    nsh = cdiv(IK * 128, SHW)
    def gu_ap(g, i):
        o = i * 128
        r, ol = o // SHW, o % SHW
        return g[r * C:(r + 1) * C, ol:ol + 128].rearrange("(k p) a -> p k a", p=128)

    def dn_ap(ck):
        r, cl = ck // 2, (ck % 2) * 128
        return wd_g[r * I:(r + 1) * I, cl:cl + 128].rearrange("(k p) a -> p k a", p=128)

    sb = lambda name, shape, dt=F32: st.enter_context(nc.sbuf_tensor(uname(name), shape, dt))
    psal = lambda name, shape: st.enter_context(nc.psum_tensor(uname(name), shape, F32))

    xb = sb("xb", [128, 16 * 512], BF16)
    hsb = sb("hsb", [128, IK * 512], BF16)
    gt = sb("gt", [128, 512], BF16)
    ysb = sb("ysb", [128, 16 * 512], F32)
    x2s = sb("x2s", [128, 16 * 512], F32)
    ctxt = [sb(f"ctxt{i}", [128, 512]) for i in range(2)]
    sqf = [sb(f"sqf{i}", [128, 512]) for i in range(2)]
    wgt = [sb(f"wgt{i}", [128, 16 * 128], BF16) for i in range(2)]
    wut = [sb(f"wut{i}", [128, 16 * 128], BF16) for i in range(2)]
    wdt = [sb(f"wdt{i}", [128, IK * 128], BF16) for i in range(2)]
    onescol = sb("onescol", [128, 1], F32)
    onescf = sb("onescf", [1, 128], F32)
    zerocol = sb("zerocol", [1, 128], BF16)
    zrow = sb("zrow", [1, 128], BF16)
    rowbuf = sb("rowbuf", [1, 512], F32)
    rowbuf2 = sb("rowbuf2", [1, 512], F32)
    rbsb = sb("rbsb", [128, 512], F32)

    ps_g = psal("ps_g", [128, 512])
    ps_u = psal("ps_u", [128, 512])
    ps_d = psal("ps_d", [128, 512])
    ps_ss = psal("ps_ss", [1, 512])
    ps_ss2 = psal("ps_ss2", [1, 512])
    ps_b = psal("ps_b", [128, 512])
    ps_b2 = psal("ps_b2", [128, 512])

    N = Node
    n_xb, n_hsb, n_gt, n_ysb, n_x2s = N("xb"), N("hsb"), N("gt"), N("ysb"), N("x2s")
    n_ctxt = nodes("ctxt", 2)
    n_sqf = nodes("sqf", 2)
    n_wgt = nodes("wgt", 2)
    n_wut = nodes("wut", 2)
    n_wdt = nodes("wdt", 2)
    n_ones, n_onescf = N("ones"), N("onescf")
    n_row, n_row2, n_rbsb = N("row"), N("row2"), N("rbsb")
    n_psg, n_psu, n_psd = N("psg"), N("psu"), N("psd")
    n_pss, n_pss2, n_psb, n_psb2 = N("pss"), N("pss2"), N("psb"), N("psb2")
    n_store = Node("store")
    stores = []

    p.op("vector", lambda e: e.memset(onescol[:], 1.0), writes=[n_ones])
    p.op("vector", lambda e: e.memset(onescf[:], 1.0), writes=[n_onescf])

    for tg in range(4):
        t0 = tg * 512
        for ck in range(CK):
            p.dma("gpsimd", xb[:, ck * 512:(ck + 1) * 512],
                  x1[ck * 128:(ck + 1) * 128, t0:t0 + 512], writes=[n_xb])
        for ii in range(IK):
            wb = ii % 2
            p.dma("sync", wgt[wb][:].rearrange("p (k a) -> p k a", k=16),
                  gu_ap(wg_g, ii), writes=[n_wgt[wb]])
            p.dma("sync", wut[wb][:].rearrange("p (k a) -> p k a", k=16),
                  gu_ap(wu_g, ii), writes=[n_wut[wb]])

            def gmm(e, wb=wb, w=wgt, ps=ps_g):
                last = None
                for ck in range(CK):
                    last = e.matmul(ps[:], w[wb][:, ck * 128:(ck + 1) * 128],
                                    xb[:, ck * 512:(ck + 1) * 512],
                                    start=(ck == 0), stop=(ck == CK - 1))
                return last
            p.op("tensor", gmm, reads=[n_wgt[wb], n_xb], writes=[n_psg])
            p.op("scalar", lambda e: e.activation(out=gt[:], in_=ps_g[:], func=AF.Silu),
                 reads=[n_psg], writes=[n_gt])
            p.op("tensor", lambda e, wb=wb: gmm(e, wb=wb, w=wut, ps=ps_u),
                 reads=[n_wut[wb], n_xb], writes=[n_psu])
            p.op("vector", lambda e, ii=ii: e.tensor_mul(
                hsb[:, ii * 512:(ii + 1) * 512], ps_u[:], gt[:]),
                reads=[n_psu, n_gt], writes=[n_hsb])
        for ck in range(CK):
            wb = ck % 2
            p.dma("sync", wdt[wb][:].rearrange("p (k a) -> p k a", k=IK),
                  dn_ap(ck), writes=[n_wdt[wb]])

            def dmm(e, wb=wb):
                last = None
                for ii in range(IK):
                    last = e.matmul(ps_d[:], wdt[wb][:, ii * 128:(ii + 1) * 128],
                                    hsb[:, ii * 512:(ii + 1) * 512],
                                    start=(ii == 0), stop=(ii == IK - 1))
                return last
            p.op("tensor", dmm, reads=[n_wdt[wb], n_hsb], writes=[n_psd])
            p.op("vector", lambda e, ck=ck: e.tensor_copy(
                out=ysb[:, ck * 512:(ck + 1) * 512], in_=ps_d[:]),
                reads=[n_psd], writes=[n_ysb])

        if rms_ep:
            for ck in range(CK):
                sq = ck % 2
                p.op("scalar", lambda e, ck=ck, sq=sq: e.activation(
                    out=sqf[sq][:], in_=ysb[:, ck * 512:(ck + 1) * 512], func=AF.Square),
                    reads=[n_ysb], writes=[n_sqf[sq]])
                p.op("tensor", lambda e, ck=ck, sq=sq: e.matmul(
                    ps_ss[:], onescol[:], sqf[sq][:],
                    start=(ck == 0), stop=(ck == CK - 1)),
                    reads=[n_ones, n_sqf[sq]], writes=[n_pss])
            p.op("vector", lambda e: e.tensor_scalar(
                out=rowbuf[:], in0=ps_ss[:], scalar1=1.0 / 2048.0, scalar2=1e-6,
                op0=ALU.mult, op1=ALU.add), reads=[n_pss], writes=[n_row])
            p.op("scalar", lambda e: e.activation(out=rowbuf[:], in_=rowbuf[:], func=AF.Ln),
                 reads=[n_row], writes=[n_row])
            p.op("scalar", lambda e: e.activation(out=rowbuf[:], in_=rowbuf[:], func=AF.Exp,
                                                  scale=-0.5), reads=[n_row], writes=[n_row])
            p.op("tensor", lambda e: e.matmul(ps_b[:], onescf[:], rowbuf[:], start=True, stop=True),
                 reads=[n_onescf, n_row], writes=[n_psb])
        for ck in range(CK):
            b = ck % 2
            p.dma("gpsimd", ctxt[b][:], x1[ck * 128:(ck + 1) * 128, t0:t0 + 512],
                  writes=[n_ctxt[b]])
            if rms_ep:
                p.op("vector", lambda e, ck=ck, b=b: e.tensor_mul(
                    x2s[:, ck * 512:(ck + 1) * 512], ps_b[:], ysb[:, ck * 512:(ck + 1) * 512]),
                    reads=[n_psb, n_ysb], writes=[n_x2s])
                p.op("vector", lambda e, ck=ck, b=b: e.tensor_add(
                    x2s[:, ck * 512:(ck + 1) * 512], x2s[:, ck * 512:(ck + 1) * 512],
                    ctxt[b][:]), reads=[n_ctxt[b]], writes=[n_x2s])
            else:
                p.op("vector", lambda e, ck=ck, b=b: e.tensor_add(
                    x2s[:, ck * 512:(ck + 1) * 512], ysb[:, ck * 512:(ck + 1) * 512],
                    ctxt[b][:]), reads=[n_ysb, n_ctxt[b]], writes=[n_x2s])
        for ck in range(CK):
            sq = ck % 2
            p.op("tensor", lambda e, ck=ck: e.matmul(
                ps_ss[:], onescol[:], x2s[:, ck * 512:(ck + 1) * 512],
                start=(ck == 0), stop=(ck == CK - 1)),
                reads=[n_ones, n_x2s], writes=[n_pss])
            p.op("scalar", lambda e, ck=ck, sq=sq: e.activation(
                out=sqf[sq][:], in_=x2s[:, ck * 512:(ck + 1) * 512], func=AF.Square),
                reads=[n_x2s], writes=[n_sqf[sq]])
            p.op("tensor", lambda e, ck=ck, sq=sq: e.matmul(
                ps_ss2[:], onescol[:], sqf[sq][:],
                start=(ck == 0), stop=(ck == CK - 1)),
                reads=[n_ones, n_sqf[sq]], writes=[n_pss2])
        p.op("vector", lambda e: e.tensor_scalar(
            out=rowbuf[:], in0=ps_ss[:], scalar1=1.0 / 2048.0, scalar2=None,
            op0=ALU.mult, op1=ALU.bypass), reads=[n_pss], writes=[n_row])
        p.op("vector", lambda e: e.tensor_mul(rowbuf2[:], rowbuf[:], rowbuf[:]),
             reads=[n_row], writes=[n_row2])
        p.op("vector", lambda e: e.tensor_scalar(
            out=rowbuf2[:], in0=rowbuf2[:], scalar1=1e-5, scalar2=None,
            op0=ALU.subtract, op1=ALU.bypass), reads=[n_row2], writes=[n_row2])
        p.op("vector", lambda e: e.scalar_tensor_tensor(
            out=rowbuf2[:], in0=ps_ss2[:], scalar=1.0 / 2048.0, in1=rowbuf2[:],
            op0=ALU.mult, op1=ALU.subtract), reads=[n_pss2, n_row2], writes=[n_row2])
        p.op("scalar", lambda e: e.activation(out=rowbuf2[:], in_=rowbuf2[:], func=AF.Ln),
             reads=[n_row2], writes=[n_row2])
        p.op("scalar", lambda e: e.activation(out=rowbuf2[:], in_=rowbuf2[:], func=AF.Exp,
                                              scale=-0.5), reads=[n_row2], writes=[n_row2])
        p.op("tensor", lambda e: e.matmul(ps_b[:], onescf[:], rowbuf[:], start=True, stop=True),
             reads=[n_onescf, n_row], writes=[n_psb])
        p.op("tensor", lambda e: e.matmul(ps_b2[:], onescf[:], rowbuf2[:], start=True, stop=True),
             reads=[n_onescf, n_row2], writes=[n_psb2])
        p.op("vector", lambda e: e.tensor_copy(out=rbsb[:], in_=ps_b2[:]),
             reads=[n_psb2], writes=[n_rbsb])
        for ck in range(CK):
            p.op("vector", lambda e, ck=ck: e.tensor_sub(
                x2s[:, ck * 512:(ck + 1) * 512], x2s[:, ck * 512:(ck + 1) * 512], ps_b[:]),
                reads=[n_psb], writes=[n_x2s])
            p.op("vector", lambda e, ck=ck: e.tensor_mul(
                x2s[:, ck * 512:(ck + 1) * 512], x2s[:, ck * 512:(ck + 1) * 512], rbsb[:]),
                reads=[n_rbsb], writes=[n_x2s])
            stores.append(p.dma(
                "sync", x2o[ck * 128:(ck + 1) * 128, t0:t0 + 512],
                x2s[:, ck * 512:(ck + 1) * 512], reads=[n_x2s], store_node=n_store))

    return stores


def build_ffn(I, rms_ep, SHW):
    IK = I // 128
    nc = bass.Bass("TRN2", target_bir_lowering=False, debug=False,
                   detect_race_conditions=bool(os.environ.get("RACEDET")))
    nsh = cdiv(IK * 128, SHW)
    x1 = nc.dram_tensor("x1", [C, TOWN], F32, kind="ExternalInput").ap()
    wg_g = nc.dram_tensor("wg_g", [nsh * C, SHW], BF16, kind="ExternalInput").ap()
    wu_g = nc.dram_tensor("wu_g", [nsh * C, SHW], BF16, kind="ExternalInput").ap()
    wd_g = nc.dram_tensor("wd_g", [8 * I, 256], BF16, kind="ExternalInput").ap()
    x2o = nc.dram_tensor("x2", [C, TOWN], F32, kind="ExternalOutput").ap()

    with ExitStack() as st:
        p = Prog(nc, st)
        stores = emit_ffn(p, nc, st, x1, wg_g, wu_g, wd_g, x2o, I, rms_ep, SHW)
        p.fence("sync", stores)
        p.finish()
    return nc

def run_ffn(x_parts, wg_g, wu_g, wd_g, I, rms_ep, SHW):
    nc = build_ffn(I, rms_ep, SHW)
    in_maps = [dict(x1=np.ascontiguousarray(x_parts[c]), wg_g=wg_g, wu_g=wu_g, wd_g=wd_g)
               for c in range(8)]
    res = run_bass_kernel_spmd(nc, in_maps, core_ids=list(range(8)))
    return [r["x2"] for r in res.results]



RT = 4096
TTILE = 256
NTT = RT // TTILE
NCH = TTILE // 32
AC = 8
CLAMP = 4.8


def _masks_lt():
    s = np.arange(128)
    m_loc = ((s[:, None] // 16 == s[None, :] // 16) & (s[None, :] > s[:, None]))
    m_bnd = ((s[:, None] // 32 == s[None, :] // 32) & (s[:, None] % 32 < 16) & (s[None, :] % 32 >= 16))
    return m_loc.astype(np.float32).astype(BF), m_bnd.astype(np.float32).astype(BF)


def emit_l1_attn(p, nc, st, x2ap, maa, w1, w2, dw1, dw2, tdcol, WrT, WkT, WvT, WgT, WoT, ucol, gamr, betr, ypo, ntt):
    m_loc_np, m_bnd_np = _masks_lt()
    mlocd = nc.inline_tensor(m_loc_np, name=uname("mloc")).ap()
    mbndd = nc.inline_tensor(m_bnd_np, name=uname("mbnd")).ap()
    idbf = nc.inline_tensor(np.eye(128, dtype=np.float32).astype(BF), name=uname("idbf")).ap()
    idf32 = nc.inline_tensor(np.eye(128, dtype=np.float32), name=uname("idf32")).ap()

    sb = lambda name, shape, dt=F32: st.enter_context(nc.sbuf_tensor(uname(name), shape, dt))
    psal = lambda name, shape, dt=F32: st.enter_context(nc.psum_tensor(uname(name), shape, dt))

    xt = sb("xt", [128, 16 * TTILE])
    xpv = sb("xpv", [128, 16 * TTILE])
    mix = sb("mix", [128, 16 * TTILE], BF16)
    zb = [sb(f"zb{f}", [32, TTILE], BF16) for f in range(5)]
    zd = sb("zd", [64, TTILE], BF16)
    logw = sb("logw", [128, AC * TTILE])
    Bp = logw
    Bs = sb("Bs", [128, AC * TTILE])
    n15 = sb("n15", [128, AC * 2 * NCH])
    dcol = sb("dcol", [128, AC * NCH])
    kk31 = sb("kk31", [128, AC * 2 * NCH])
    refq = sb("refq", [128, AC * 2 * NCH])
    refk = sb("refk", [128, AC * 2 * NCH])
    targ = [sb(f"targ{i}", [128, TTILE]) for i in range(2)]
    q_loc = sb("q_loc", [128, AC * TTILE], BF16)
    k_loc = sb("k_loc", [128, AC * TTILE], BF16)
    q_bnd = sb("q_bnd", [128, AC * TTILE], BF16)
    k_bnd = sb("k_bnd", [128, AC * TTILE], BF16)
    qpp = sb("qpp", [128, AC * TTILE])
    kpp = sb("kpp", [128, AC * TTILE])
    rpl = sb("rpl", [128, AC * TTILE])
    kpl = sb("kpl", [128, AC * TTILE])
    v_tok = sb("v_tok", [128, (TTILE // 128) * 1024], BF16)
    g_tok = sb("g_tok", [128, (TTILE // 128) * 1024], BF16)
    vpl = sb("vpl", [128, AC * TTILE])
    kp_tok = sb("kp_tok", [128, (TTILE // 128) * 1024], BF16)
    og_f = sb("og_f", [128, AC * TTILE], BF16)
    out_tok = sb("out_tok", [128, (TTILE // 128) * 1024])
    Ssb = sb("Ssb", [128, AC * 64])
    erk = sb("erk", [128, TTILE], BF16)
    drows = sb("drows", [16, TTILE])
    d_tok = sb("d_tok", [128, (TTILE // 128) * 16])
    gsum = sb("gsum", [128, (TTILE // 128) * 16])
    gsq = sb("gsq", [128, (TTILE // 128) * 16])
    gsqp = sb("gsqp", [128, 1024])
    A1 = sb("A1", [128, 128], BF16)
    A1b = sb("A1b", [128, 128], BF16)
    A2 = sb("A2", [128, 128], BF16)
    A2b = sb("A2b", [128, 128], BF16)
    gamb = sb("gamb", [128, 1024])
    betb = sb("betb", [128, 1024])
    mlocs = sb("mlocs", [128, 128], BF16)
    mbnds = sb("mbnds", [128, 128], BF16)
    idfs = sb("idfs", [128, 128])
    BCs = sb("BCs", [128, AC * 16], BF16)
    ucols = sb("ucols", [128, 8])
    tds = sb("tds", [128, 8])
    maas = {k: sb(f"maas_{k}", [128, 16]) for k in ("x", "w", "k", "v", "r", "g")}
    w1s = sb("w1s", [128, 16 * 160], BF16)
    w2r = [sb(f"w2r{i}", [32, 2048], BF16) for i in range(2)]
    dw1s = sb("dw1s", [128, 16 * 64], BF16)
    dw2s = sb("dw2s", [64, 8 * 128], BF16)
    wring = [sb(f"wring{i}", [128, 16 * 128], BF16) for i in range(2)]
    onescf = sb("onescf", [1, 128], F32)
    zerocol = sb("zerocol", [1, 128], BF16)
    zrow = sb("zrow", [1, 128], BF16)
    gamrs = sb("gamrs", [1, 1024])
    betrs = sb("betrs", [1, 1024])
    tmp1 = [sb(f"tmp1{i}", [128, TTILE]) for i in range(2)]
    tmp2 = tmp1
    gn1 = sb("gn1", [128, 16])
    gn2 = sb("gn2", [128, 16])

    ps1 = psal("ps1", [128, 512])
    psA0 = psal("psA0", [128, 512])
    psA1 = psal("psA1", [128, 512])
    ps_out = psal("ps_out", [128, 512])
    ps_st = [psal(f"ps_st{i}", [128, 512]) for i in range(4)]
    psP = [ps1, psA0]

    N = Node
    n_xt, n_xpv, n_mix = N("xt"), N("xpv"), N("mix")
    n_zb = nodes("zb", 5)
    n_zd = N("zd")
    n_logw, n_B, n_Bs = N("logw"), N("B"), N("Bs")
    n_cols = N("cols")
    n_targ = nodes("targ", 2)
    n_qloc, n_kloc, n_qbnd, n_kbnd = N("qloc"), N("kloc"), N("qbnd"), N("kbnd")
    n_qpp, n_kpp, n_rpl, n_kpl = N("qpp"), N("kpp"), N("rpl"), N("kpl")
    n_vtok, n_gtok, n_ogf, n_outtok = N("vtok"), N("gtok"), N("ogf"), N("outtok")
    n_vpl, n_kptok = N("vpl"), N("kptok")
    n_S = nodes("S", AC)
    n_erk, n_drows, n_dtok = N("erk"), N("drows"), N("dtok")
    n_gsum, n_gsq, n_gsqp = N("gsum"), N("gsq"), N("gsqp")
    n_A1, n_A1b, n_A2, n_A2b = N("A1"), N("A1b"), N("A2"), N("A2b")
    n_gamb, n_betb = N("gamb"), N("betb")
    n_consts = N("consts")
    n_w1, n_dw1, n_dw2 = N("w1"), N("dw1"), N("dw2")
    n_w2r = nodes("w2r", 2)
    n_wring = nodes("wring", 2)
    n_ps1, n_psA0, n_psA1, n_psout = N("ps1"), N("psA0"), N("psA1"), N("psout")
    nP = [n_ps1, n_psA0]
    n_psst = nodes("psst", 4)

    n_tmp1 = nodes("tmp1", 2)
    n_tmp2 = n_tmp1
    n_gn1, n_gn2 = N("gn1"), N("gn2")
    n_store = N("store")
    stores = []

    p.dma("gpsimd", mlocs[:], mlocd, writes=[n_consts])
    p.dma("gpsimd", mbnds[:], mbndd, writes=[n_consts])
    p.dma("gpsimd", idfs[:], idf32, writes=[n_consts])
    p.dma("gpsimd", ucols[:], ucol, writes=[n_consts])
    p.dma("gpsimd", tds[:], tdcol, writes=[n_consts])
    for k in maas:
        p.dma("gpsimd", maas[k][:], maa[k], writes=[n_consts])
    p.dma("gpsimd", gamrs[:], gamr, writes=[n_consts])
    p.dma("gpsimd", betrs[:], betr, writes=[n_consts])
    for i in range(CK):
        p.dma("gpsimd", w1s[:, i * 160:(i + 1) * 160], w1[i * 128:(i + 1) * 128, :], writes=[n_w1])
        p.dma("gpsimd", dw1s[:, i * 64:(i + 1) * 64], dw1[i * 128:(i + 1) * 128, :], writes=[n_dw1])
    for j in range(AC):
        p.dma("gpsimd", dw2s[:, j * 128:(j + 1) * 128], dw2[:, j * 128:(j + 1) * 128], writes=[n_dw2])
    p.op("vector", lambda e: e.memset(onescf[:], 1.0), writes=[n_consts])
    p.op("vector", lambda e: e.memset(zerocol[:], 0.0), writes=[n_consts])
    p.op("vector", lambda e: e.memset(zrow[:], 0.0), writes=[n_consts])
    for h2 in range(2):
        p.op("tensor", lambda e, h2=h2: e.matmul(
            ps1[:], onescf[:], gamrs[:, h2 * 512:(h2 + 1) * 512], start=True, stop=True),
            reads=[n_consts], writes=[n_ps1])
        p.op("vector", lambda e, h2=h2: e.tensor_copy(
            out=gamb[:, h2 * 512:(h2 + 1) * 512], in_=ps1[:]), reads=[n_ps1], writes=[n_gamb])
        p.op("tensor", lambda e, h2=h2: e.matmul(
            ps1[:], onescf[:], betrs[:, h2 * 512:(h2 + 1) * 512], start=True, stop=True),
            reads=[n_consts], writes=[n_ps1])
        p.op("vector", lambda e, h2=h2: e.tensor_copy(
            out=betb[:, h2 * 512:(h2 + 1) * 512], in_=ps1[:]), reads=[n_ps1], writes=[n_betb])
    p.op("vector", lambda e: e.memset(BCs[:], 0.0), writes=[n_consts])
    for j in range(AC):
        p.op("vector", lambda e, j=j: e.tensor_copy(
            out=BCs[0:64, j * 16 + 2 * j: j * 16 + 2 * j + 1], in_=ucols[0:64, j:j + 1]),
            writes=[n_consts])
        p.op("vector", lambda e, j=j: e.tensor_copy(
            out=BCs[64:128, j * 16 + 2 * j + 1: j * 16 + 2 * j + 2], in_=ucols[64:128, j:j + 1]),
            writes=[n_consts])
    p.op("vector", lambda e: e.memset(Ssb[:], 0.0), writes=list(n_S))

    def proj_full(wT, mix_node, out_cb, nach=AC, wofs=0):
        for ac in range(nach):
            wb = ac % 2
            pb = ac % 2
            p.dma("gpsimd", wring[wb][:].rearrange("p (k a) -> p k a", k=16),
                  wT[:, wofs + ac * 128: wofs + (ac + 1) * 128].rearrange("(k p) a -> p k a", p=128),
                  writes=[n_wring[wb]])

            def mm(e, wb=wb, pb=pb):
                last = None
                for ck in range(CK):
                    last = e.matmul(psP[pb][:, :TTILE], wring[wb][:, ck * 128:(ck + 1) * 128],
                                    mix[:, ck * TTILE:(ck + 1) * TTILE],
                                    start=(ck == 0), stop=(ck == CK - 1))
                return last
            p.op("tensor", mm, reads=[n_wring[wb], mix_node], writes=[nP[pb]])
            out_cb(ac, psP[pb], nP[pb])

    for tt in range(ntt):
        t0 = tt * TTILE
        for ck in range(CK):
            p.dma("gpsimd", xt[:, ck * TTILE:(ck + 1) * TTILE],
                  x2ap(ck, t0, t0 + TTILE), writes=[n_xt])
            if t0 == 0:
                p.op("vector", lambda e, ck=ck: e.memset(
                    xpv[:, ck * TTILE:ck * TTILE + 1], 0.0), writes=[n_xpv])
                p.dma("gpsimd", xpv[:, ck * TTILE + 1:(ck + 1) * TTILE],
                      x2ap(ck, 0, TTILE - 1), writes=[n_xpv])
            elif (t0 - 1) // 2048 != (t0 + TTILE - 2) // 2048:
                p.dma("gpsimd", xpv[:, ck * TTILE:ck * TTILE + 1],
                      x2ap(ck, t0 - 1, t0), writes=[n_xpv],
                      allow_slow_non_contiguous=True)
                p.dma("gpsimd", xpv[:, ck * TTILE + 1:(ck + 1) * TTILE],
                      x2ap(ck, t0, t0 + TTILE - 1), writes=[n_xpv])
            else:
                p.dma("gpsimd", xpv[:, ck * TTILE:(ck + 1) * TTILE],
                      x2ap(ck, t0 - 1, t0 + TTILE - 1), writes=[n_xpv])
        for ck in range(CK):
            p.op("vector", lambda e, ck=ck: e.tensor_sub(
                xpv[:, ck * TTILE:(ck + 1) * TTILE], xpv[:, ck * TTILE:(ck + 1) * TTILE],
                xt[:, ck * TTILE:(ck + 1) * TTILE]), reads=[n_xt], writes=[n_xpv])
        for ck in range(CK):
            p.op("vector", lambda e, ck=ck: e.scalar_tensor_tensor(
                out=mix[:, ck * TTILE:(ck + 1) * TTILE], in0=xpv[:, ck * TTILE:(ck + 1) * TTILE],
                scalar=maas["x"][:, ck:ck + 1], in1=xt[:, ck * TTILE:(ck + 1) * TTILE],
                op0=ALU.mult, op1=ALU.add),
                reads=[n_xpv, n_xt, n_consts], writes=[n_mix])
        def zmm(e, lo, hi, pb):
            last = None
            for ck in range(CK):
                last = e.matmul(psP[pb][:hi - lo, :TTILE], w1s[:, ck * 160 + lo: ck * 160 + hi],
                                mix[:, ck * TTILE:(ck + 1) * TTILE],
                                start=(ck == 0), stop=(ck == CK - 1))
            return last
        for pi, (lo, hi) in enumerate(((0, 64), (64, 128), (128, 160))):
            pb = pi % 2
            p.op("tensor", lambda e, lo=lo, hi=hi, pb=pb: zmm(e, lo, hi, pb),
                 reads=[n_w1, n_mix], writes=[nP[pb]])
            for f in range(lo // 32, hi // 32):
                fb = (f * 32 - lo)
                p.op("scalar", lambda e, f=f, fb=fb, pb=pb: e.activation(
                    out=zb[f][:], in_=psP[pb][fb:fb + 32, :TTILE], func=AF.Tanh),
                    reads=[nP[pb]], writes=[n_zb[f]])

        def mk_mix(f, key):
            wf = f % 2
            p.dma("gpsimd", w2r[wf][:], w2[f], writes=[n_w2r[wf]])
            for ck in range(CK):
                pb = ck % 2
                p.op("tensor", lambda e, f=f, ck=ck, pb=pb, wf=wf: e.matmul(
                    psP[pb][:, :TTILE], w2r[wf][:, ck * 128:(ck + 1) * 128],
                    zb[f][:], start=True, stop=True),
                    reads=[n_w2r[wf], n_zb[f]], writes=[nP[pb]])
                p.op("vector", lambda e, ck=ck, key=key, pb=pb: e.scalar_tensor_tensor(
                    out=tmp1[pb][:], in0=psP[pb][:, :TTILE], scalar=maas[key][:, ck:ck + 1],
                    in1=xpv[:, ck * TTILE:(ck + 1) * TTILE], op0=ALU.add, op1=ALU.mult),
                    reads=[nP[pb], n_consts, n_xpv], writes=[n_tmp1[pb]])
                p.op("vector", lambda e, ck=ck, pb=pb: e.tensor_add(
                    mix[:, ck * TTILE:(ck + 1) * TTILE], tmp1[pb][:],
                    xt[:, ck * TTILE:(ck + 1) * TTILE]),
                    reads=[n_tmp1[pb], n_xt], writes=[n_mix])

        mk_mix(1, "k")
        def k_out(ac, ps, nps):
            p.op("vector", lambda e, ac=ac, ps=ps: e.tensor_copy(
                out=kpl[:, ac * TTILE:(ac + 1) * TTILE], in_=ps[:, :TTILE]),
                reads=[nps], writes=[n_kpl])
        proj_full(WkT, n_mix, k_out)
        mk_mix(2, "v")
        def v_out(ac, ps, nps):
            p.op("scalar", lambda e, ac=ac, ps=ps: e.activation(
                out=vpl[:, ac * TTILE:(ac + 1) * TTILE], in_=ps[:, :TTILE],
                func=AF.Copy), reads=[nps], writes=[n_vpl])
        proj_full(WvT, n_mix, v_out)
        for ac in range(AC):
            for tc in range(TTILE // 128):
                pb = (ac * 2 + tc) % 2
                p.op("tensor", lambda e, ac=ac, tc=tc, pb=pb: e.transpose(
                    psP[pb][:, 128:256], vpl[:, ac * TTILE + tc * 128: ac * TTILE + (tc + 1) * 128],
                    idfs[:]), reads=[n_vpl, n_consts], writes=[nP[pb]])
                p.op("vector", lambda e, ac=ac, tc=tc, pb=pb: e.tensor_copy(
                    out=v_tok[:, tc * 1024 + ac * 128: tc * 1024 + (ac + 1) * 128],
                    in_=psP[pb][:, 128:256]), reads=[nP[pb]], writes=[n_vtok])

        def tok_proj(wT, dst, dst_node, act_silu):
            for ac in range(AC):
                wb = ac % 2
                p.dma("gpsimd", wring[wb][:].rearrange("p (k a) -> p k a", k=16),
                      wT[:, ac * 128:(ac + 1) * 128].rearrange("(k p) a -> p k a", p=128),
                      writes=[n_wring[wb]])
                for tc in range(TTILE // 128):
                    pb = (ac * 2 + tc) % 2

                    def mm(e, wb=wb, tc=tc, pb=pb):
                        last = None
                        for ck in range(CK):
                            last = e.matmul(
                                psP[pb][:, :128],
                                mix[:, ck * TTILE + tc * 128: ck * TTILE + (tc + 1) * 128],
                                wring[wb][:, ck * 128:(ck + 1) * 128],
                                start=(ck == 0), stop=(ck == CK - 1))
                        return last
                    p.op("tensor", mm, reads=[n_wring[wb], n_mix], writes=[nP[pb]])
                    if act_silu:
                        p.op("scalar", lambda e, tc=tc, ac=ac, pb=pb: e.activation(
                            out=dst[:, tc * 1024 + ac * 128: tc * 1024 + (ac + 1) * 128],
                            in_=psP[pb][:, :128], func=AF.Silu),
                            reads=[nP[pb]], writes=[dst_node])
                    else:
                        p.op("vector", lambda e, tc=tc, ac=ac, pb=pb: e.tensor_copy(
                            out=dst[:, tc * 1024 + ac * 128: tc * 1024 + (ac + 1) * 128],
                            in_=psP[pb][:, :128]),
                            reads=[nP[pb]], writes=[dst_node])
        mk_mix(3, "r")
        def r_out(ac, ps, nps):
            p.op("vector", lambda e, ac=ac, ps=ps: e.tensor_copy(
                out=rpl[:, ac * TTILE:(ac + 1) * TTILE], in_=ps[:, :TTILE]),
                reads=[nps], writes=[n_rpl])
        proj_full(WrT, n_mix, r_out)
        mk_mix(4, "g")
        tok_proj(WgT, g_tok, n_gtok, True)
        mk_mix(0, "w")
        def ddmm(e):
            last = None
            for ck in range(CK):
                last = e.matmul(ps1[:64, :TTILE], dw1s[:, ck * 64:(ck + 1) * 64],
                                mix[:, ck * TTILE:(ck + 1) * TTILE],
                                start=(ck == 0), stop=(ck == CK - 1))
            return last
        p.op("tensor", ddmm, reads=[n_dw1, n_mix], writes=[nP[0]])
        p.op("scalar", lambda e: e.activation(out=zd[:], in_=ps1[:64, :TTILE], func=AF.Tanh),
             reads=[nP[0]], writes=[n_zd])
        for ac in range(AC):
            pb = ac % 2
            p.op("tensor", lambda e, ac=ac, pb=pb: e.matmul(
                psP[pb][:, :TTILE], dw2s[:, ac * 128:(ac + 1) * 128], zd[:],
                start=True, stop=True), reads=[n_dw2, n_zd], writes=[nP[pb]])
            p.op("scalar", lambda e, ac=ac, pb=pb: e.activation(
                out=targ[pb][:], in_=psP[pb][:, :TTILE], func=AF.Exp, bias=tds[:, ac:ac + 1]),
                reads=[nP[pb], n_consts], writes=[n_targ[pb]])
            p.op("vector", lambda e, ac=ac, pb=pb: e.tensor_scalar(
                out=logw[:, ac * TTILE:(ac + 1) * TTILE], in0=targ[pb][:],
                scalar1=CLAMP, scalar2=-1.0, op0=ALU.min, op1=ALU.mult),
                reads=[n_targ[pb]], writes=[n_logw])

        W = AC * TTILE
        r3w = lambda t: t.rearrange("p (c s) -> p c s", s=32)
        steps = [(logw, Bs, 1), (Bs, Bp, 2), (Bp, Bs, 4), (Bs, Bp, 8), (Bp, Bs, 16)]
        for si, (sbuf_, dbuf, k) in enumerate(steps):
            rd = n_logw if si == 0 else n_B
            p.op("vector", lambda e, s_=sbuf_, d=dbuf, k=k: e.tensor_tensor(
                out=r3w(d[:, :W])[:, :, k:32], in0=r3w(s_[:, :W])[:, :, k:32],
                in1=r3w(s_[:, :W])[:, :, 0:32 - k], op=ALU.add),
                reads=[rd], writes=[n_B])
            p.op("vector", lambda e, s_=sbuf_, d=dbuf, k=k: e.tensor_copy(
                out=r3w(d[:, :W])[:, :, 0:k], in_=r3w(s_[:, :W])[:, :, 0:k]),
                reads=[rd], writes=[n_B])
        p.op("vector", lambda e: e.tensor_copy(
            out=r3w(Bp[:, :W])[:, :, 1:32], in_=r3w(Bs[:, :W])[:, :, 0:31]),
            reads=[n_B], writes=[n_Bs])
        p.op("vector", lambda e: e.memset(Bp[:, 0:W:32], 0.0), writes=[n_Bs])
        Bfull, Bsh = Bs, Bp
        n_Bf, n_Bsh = n_B, n_Bs

        for hf, mi in ((0, 7), (1, 23)):
            p.op("vector", lambda e, hf=hf, mi=mi: e.tensor_scalar(
                out=n15[:, hf:AC * 2 * NCH:2], in0=Bfull[:, mi:W:32],
                scalar1=-1.0, scalar2=None, op0=ALU.mult, op1=ALU.bypass),
                reads=[n_Bf], writes=[n_cols])
        p.op("scalar", lambda e: e.activation(
            out=dcol[:], in_=Bfull[:, 31:W:32], func=AF.Exp),
            reads=[n_Bf], writes=[n_cols])
        for hf in range(2):
            p.op("vector", lambda e, hf=hf: e.tensor_tensor(
                out=kk31[:, hf:AC * 2 * NCH:2], in0=Bfull[:, 31:W:32],
                in1=n15[:, hf:AC * 2 * NCH:2], op=ALU.add),
                reads=[n_Bf, n_cols], writes=[n_cols])
        p.op("scalar", lambda e: e.activation(out=kk31[:], in_=kk31[:], func=AF.Exp),
             reads=[n_cols], writes=[n_cols])
        p.op("vector", lambda e: e.memset(refq[:, 0:AC * 2 * NCH:2], 0.0), writes=[n_cols])
        p.op("vector", lambda e: e.tensor_copy(
            out=refq[:, 1:AC * 2 * NCH:2], in_=Bfull[:, 15:W:32]), reads=[n_Bf], writes=[n_cols])
        p.op("vector", lambda e: e.tensor_copy(
            out=refk[:, 0:AC * 2 * NCH:2], in_=Bfull[:, 15:W:32]), reads=[n_Bf], writes=[n_cols])
        p.op("vector", lambda e: e.tensor_copy(
            out=refk[:, 1:AC * 2 * NCH:2], in_=Bfull[:, 31:W:32]), reads=[n_Bf], writes=[n_cols])

        r16 = lambda t: t.rearrange("p (c s) -> p c s", s=16)
        bc = lambda t: t[:, 0:AC * 2 * NCH].unsqueeze(2).broadcast_to([128, AC * 2 * NCH, 16])
        n15v, refqv, refkv, kk31v = bc(n15[:]), bc(refq[:]), bc(refk[:]), bc(kk31[:])
        for dst, n_dst, base, n_base, refv, sgn, mul_, n_mul in (
                (q_loc, n_qloc, Bsh, n_Bsh, n15v, 1.0, rpl, n_rpl),
                (k_loc, n_kloc, Bfull, n_Bf, n15v, -1.0, kpl, n_kpl),
                (q_bnd, n_qbnd, Bsh, n_Bsh, refqv, 1.0, rpl, n_rpl),
                (k_bnd, n_kbnd, Bfull, n_Bf, refkv, -1.0, kpl, n_kpl)):
            alop = ALU.add if refv is n15v else ALU.subtract
            p.op("vector", lambda e, base=base, refv=refv, alop=alop: e.tensor_tensor(
                out=r16(kpp[:, :W]), in0=r16(base[:, :W]), in1=refv, op=alop),
                reads=[n_base, n_cols], writes=[n_kpp])
            p.op("scalar", lambda e, sgn=sgn: e.activation(
                out=kpp[:, :W], in_=kpp[:, :W], func=AF.Exp, scale=sgn),
                reads=[n_kpp], writes=[n_kpp])
            p.op("vector", lambda e, dst=dst, mul_=mul_: e.tensor_mul(
                dst[:, :W], mul_[:, :W], kpp[:, :W]),
                reads=[n_mul, n_kpp], writes=[n_dst])
        p.op("scalar", lambda e: e.activation(out=qpp[:, :W], in_=Bsh[:, :W], func=AF.Exp),
             reads=[n_Bsh], writes=[n_qpp])
        p.op("vector", lambda e: e.tensor_mul(qpp[:, :W], qpp[:, :W], rpl[:, :W]),
             reads=[n_rpl, n_qpp], writes=[n_qpp])
        p.op("vector", lambda e: e.tensor_tensor(
            out=r16(kpp[:, :W]), in0=r16(k_loc[:, :W]), in1=kk31v, op=ALU.mult),
            reads=[n_kloc, n_cols], writes=[n_kpp])

        for ac in range(AC):
            for tc in range(TTILE // 128):
                pb = (ac * 2 + tc) % 2
                p.op("tensor", lambda e, ac=ac, tc=tc, pb=pb: e.transpose(
                    psP[pb][:, 128:256], kpp[:, ac * TTILE + tc * 128: ac * TTILE + (tc + 1) * 128],
                    idfs[:]), reads=[n_kpp, n_consts], writes=[nP[pb]])
                p.op("vector", lambda e, ac=ac, tc=tc, pb=pb: e.tensor_copy(
                    out=kp_tok[:, tc * 1024 + ac * 128: tc * 1024 + (ac + 1) * 128],
                    in_=psP[pb][:, 128:256]), reads=[nP[pb]], writes=[n_kptok])

        for ac in range(AC):
            p.op("vector", lambda e, ac=ac: e.tensor_mul(
                erk[:], rpl[:, ac * TTILE:(ac + 1) * TTILE], kpl[:, ac * TTILE:(ac + 1) * TTILE]),
                reads=[n_rpl, n_kpl], writes=[n_erk])
            p.op("tensor", lambda e, ac=ac: e.matmul(
                ps1[:16, :TTILE], BCs[:, ac * 16:(ac + 1) * 16], erk[:],
                start=(ac == 0), stop=(ac == AC - 1)),
                reads=[n_consts, n_erk], writes=[nP[0]])
        p.op("vector", lambda e: e.tensor_copy(out=drows[:], in_=ps1[:16, :TTILE]),
             reads=[nP[0]], writes=[n_drows])
        for tc in range(TTILE // 128):
            pb = tc % 2
            p.op("tensor", lambda e, tc=tc, pb=pb: e.transpose(
                psP[pb][:, :16], drows[:, tc * 128:(tc + 1) * 128], idfs[0:16, 0:16]),
                reads=[n_drows, n_consts], writes=[nP[pb]])
            p.op("vector", lambda e, tc=tc, pb=pb: e.tensor_copy(
                out=d_tok[:, tc * 16:(tc + 1) * 16], in_=psP[pb][:, :16]),
                reads=[nP[pb]], writes=[n_dtok])

        for grp in range(TTILE // 128):
            sp = slice(grp * 128, (grp + 1) * 128)
            for ac in range(AC):
                gsl0 = ac * TTILE + grp * 128
                gsl = slice(gsl0, gsl0 + 128)

                def amm(e, gsl=gsl, qpl=q_loc, kplx=k_loc):
                    e.matmul(psA0[:, :128], kplx[0:64, gsl], qpl[0:64, gsl], start=True, stop=True)
                    return e.matmul(psA1[:, :128], kplx[64:128, gsl], qpl[64:128, gsl],
                                    start=True, stop=True, tile_position=(64, 0))
                p.op("tensor", lambda e, gsl=gsl: amm(e, gsl, q_loc, k_loc),
                     reads=[n_qloc, n_kloc], writes=[n_psA0, n_psA1])
                p.op("vector", lambda e: e.tensor_mul(A1[:], psA0[:, :128], mlocs[:]),
                     reads=[n_psA0, n_consts], writes=[n_A1])
                p.op("vector", lambda e: e.tensor_mul(A1b[:], psA1[:, :128], mlocs[:]),
                     reads=[n_psA1, n_consts], writes=[n_A1b])
                p.op("tensor", lambda e, gsl=gsl: amm(e, gsl, q_bnd, k_bnd),
                     reads=[n_qbnd, n_kbnd], writes=[n_psA0, n_psA1])
                p.op("vector", lambda e: e.tensor_mul(A2[:], psA0[:, :128], mbnds[:]),
                     reads=[n_psA0, n_consts], writes=[n_A2])
                p.op("vector", lambda e: e.tensor_mul(A2b[:], psA1[:, :128], mbnds[:]),
                     reads=[n_psA1, n_consts], writes=[n_A2b])

                def intramm(e, ac=ac, grp=grp):
                    vA = v_tok[:, grp * 1024 + ac * 128: grp * 1024 + ac * 128 + 64]
                    vB = v_tok[:, grp * 1024 + ac * 128 + 64: grp * 1024 + (ac + 1) * 128]
                    e.matmul(ps_out[:, 0:64], A1[:], vA, start=True, stop=False)
                    e.matmul(ps_out[:, 64:128], A1b[:], vB, start=False, stop=False)
                    e.matmul(ps_out[:, 64:128], A2b[:], vB, start=False, stop=False)
                    return e.matmul(ps_out[:, 0:64], A2[:], vA, start=False, stop=False)
                p.op("tensor", intramm,
                     reads=[n_A1, n_A1b, n_A2, n_A2b, n_vtok],
                     writes=[n_psout])
                for ch in range(4):
                    cidx = grp * 4 + ch
                    co = ac * TTILE + grp * 128 + ch * 32

                    def interA(e, ac=ac, ch=ch, co=co):
                        return e.matmul(ps_out[ch * 32:(ch + 1) * 32, 0:64],
                                        qpp[0:64, co:co + 32], Ssb[0:64, ac * 64:(ac + 1) * 64],
                                        start=False, stop=False, tile_position=(0, ch * 32))
                    p.op("tensor", interA, reads=[n_qpp, n_S[ac]], writes=[n_psout])

                    def interB(e, ac=ac, ch=ch, co=co):
                        return e.matmul(psA1[ch * 32:(ch + 1) * 32, 0:64],
                                        qpp[64:128, co:co + 32], Ssb[64:128, ac * 64:(ac + 1) * 64],
                                        start=True, stop=True, tile_position=(64, ch * 32))
                    p.op("tensor", interB, reads=[n_qpp, n_S[ac]], writes=[n_psA1])

                    def stmm(e, ac=ac, grp=grp, ch=ch):
                        rows = slice(ch * 32, (ch + 1) * 32)
                        kA = kp_tok[rows, grp * 1024 + ac * 128: grp * 1024 + ac * 128 + 64]
                        kB = kp_tok[rows, grp * 1024 + ac * 128 + 64: grp * 1024 + (ac + 1) * 128]
                        vA = v_tok[rows, grp * 1024 + ac * 128: grp * 1024 + ac * 128 + 64]
                        vB = v_tok[rows, grp * 1024 + ac * 128 + 64: grp * 1024 + (ac + 1) * 128]
                        e.matmul(ps_st[ch][0:64, 0:64], kA, vA, start=True, stop=True,
                                 tile_position=(ch * 32, 0))
                        return e.matmul(ps_st[ch][64:128, 0:64], kB, vB, start=True, stop=True,
                                        tile_position=(ch * 32, 64))
                    p.op("tensor", stmm, reads=[n_kptok, n_vtok], writes=[n_psst[ch]])
                    p.op("vector", lambda e, ac=ac, cidx=cidx, ch=ch: e.scalar_tensor_tensor(
                        out=Ssb[:, ac * 64:(ac + 1) * 64], in0=Ssb[:, ac * 64:(ac + 1) * 64],
                        scalar=dcol[:, ac * NCH + cidx: ac * NCH + cidx + 1],
                        in1=ps_st[ch][:, 0:64], op0=ALU.mult, op1=ALU.add),
                        reads=[n_psst[ch], n_cols], writes=[n_S[ac]])
                def closer(e):
                    return e.matmul(ps_out[:, 0:64], zerocol[:], zrow[:, 0:64],
                                    start=False, stop=True)
                p.op("tensor", closer, reads=[n_consts], writes=[n_psout])
                for hh in range(2):
                    sl_o = slice(grp * 1024 + ac * 128 + hh * 64,
                                 grp * 1024 + ac * 128 + (hh + 1) * 64)
                    p.op("vector", lambda e, sl_o=sl_o, grp=grp, ac=ac, hh=hh: e.scalar_tensor_tensor(
                        out=out_tok[:, sl_o],
                        in0=v_tok[:, sl_o],
                        scalar=d_tok[:, grp * 16 + 2 * ac + hh: grp * 16 + 2 * ac + hh + 1],
                        in1=ps_out[:, hh * 64:(hh + 1) * 64], op0=ALU.mult, op1=ALU.add),
                        reads=[n_vtok, n_dtok, n_psout], writes=[n_outtok])
                    if hh == 1:
                        p.op("vector", lambda e, sl_o=sl_o: e.tensor_add(
                            out_tok[:, sl_o], out_tok[:, sl_o], psA1[:, 0:64]),
                            reads=[n_psA1], writes=[n_outtok])

        for tc in range(TTILE // 128) if not os.environ.get("SKIPGN") else ():
            otv = out_tok[:, tc * 1024:(tc + 1) * 1024]
            p.op("vector", lambda e, tc=tc, otv=otv: e.reduce_sum(
                gsum[:, tc * 16:(tc + 1) * 16],
                otv.rearrange("p (h m) -> p h m", m=64), axis=AX.X),
                reads=[n_outtok], writes=[n_gsum])
            p.op("scalar", lambda e, tc=tc, otv=otv: e.activation(
                out=gsqp[:], in_=otv, func=AF.Square), reads=[n_outtok], writes=[n_gsqp])
            p.op("vector", lambda e, tc=tc: e.reduce_sum(
                gsq[:, tc * 16:(tc + 1) * 16],
                gsqp[:].rearrange("p (h m) -> p h m", m=64), axis=AX.X),
                reads=[n_gsqp], writes=[n_gsq])
            p.op("vector", lambda e, tc=tc: e.tensor_scalar(
                out=gn1[:], in0=gsum[:, tc * 16:(tc + 1) * 16], scalar1=1.0 / 64, scalar2=None,
                op0=ALU.mult, op1=ALU.bypass), reads=[n_gsum], writes=[n_gn1])
            p.op("vector", lambda e: e.tensor_mul(gn2[:], gn1[:], gn1[:]),
                 reads=[n_gn1], writes=[n_gn2])
            p.op("vector", lambda e: e.tensor_scalar(
                out=gn2[:], in0=gn2[:], scalar1=6.4e-4, scalar2=None,
                op0=ALU.subtract, op1=ALU.bypass), reads=[n_gn2], writes=[n_gn2])
            p.op("vector", lambda e, tc=tc: e.scalar_tensor_tensor(
                out=gn2[:], in0=gsq[:, tc * 16:(tc + 1) * 16], scalar=1.0 / 64, in1=gn2[:],
                op0=ALU.mult, op1=ALU.subtract), reads=[n_gsq, n_gn2], writes=[n_gn2])
            p.op("scalar", lambda e: e.activation(out=gn2[:], in_=gn2[:], func=AF.Ln),
                 reads=[n_gn2], writes=[n_gn2])
            p.op("scalar", lambda e: e.activation(out=gn2[:], in_=gn2[:], func=AF.Exp,
                                                  scale=-0.5), reads=[n_gn2], writes=[n_gn2])
            m_v = gn1[:].unsqueeze(2).broadcast_to([128, 16, 64])
            r_v = gn2[:].unsqueeze(2).broadcast_to([128, 16, 64])
            o3 = otv.rearrange("p (h m) -> p h m", m=64)
            p.op("vector", lambda e, o3=o3, m_v=m_v: e.tensor_tensor(
                out=o3, in0=o3, in1=m_v, op=ALU.subtract),
                reads=[n_gn1], writes=[n_outtok])
            p.op("vector", lambda e, o3=o3, r_v=r_v: e.tensor_tensor(
                out=o3, in0=o3, in1=r_v, op=ALU.mult),
                reads=[n_gn2], writes=[n_outtok])
            p.op("vector", lambda e, otv=otv: e.tensor_mul(otv, otv, gamb[:]),
                 reads=[n_gamb], writes=[n_outtok])
            p.op("vector", lambda e, otv=otv: e.tensor_add(otv, otv, betb[:]),
                 reads=[n_betb], writes=[n_outtok])
            p.op("vector", lambda e, tc=tc, otv=otv: e.tensor_mul(
                out_tok[:, tc * 1024:(tc + 1) * 1024], otv, g_tok[:, tc * 1024:(tc + 1) * 1024]),
                reads=[n_gtok], writes=[n_outtok])
        for tc in range(TTILE // 128):
            for ac in range(AC):
                pb = (tc * AC + ac) % 2
                p.op("tensor", lambda e, tc=tc, ac=ac, pb=pb: e.transpose(
                    psP[pb][:, 128:256],
                    out_tok[:, tc * 1024 + ac * 128: tc * 1024 + (ac + 1) * 128], idfs[:]),
                    reads=[n_outtok, n_consts], writes=[nP[pb]])
                p.op("vector", lambda e, tc=tc, ac=ac, pb=pb: e.tensor_copy(
                    out=og_f[:, ac * TTILE + tc * 128: ac * TTILE + (tc + 1) * 128],
                    in_=psP[pb][:, 128:256]), reads=[nP[pb]], writes=[n_ogf])
        for ck in range(CK):
            wb = ck % 2
            p.dma("gpsimd", wring[wb][:, :8 * 128].rearrange("p (k a) -> p k a", k=8),
                  WoT[:, ck * 128:(ck + 1) * 128].rearrange("(k p) a -> p k a", p=128),
                  writes=[n_wring[wb]])

            pb = ck % 2

            def womm(e, wb=wb, pb=pb):
                last = None
                for ac in range(AC):
                    last = e.matmul(psP[pb][:, :TTILE], wring[wb][:, ac * 128:(ac + 1) * 128],
                                    og_f[:, ac * TTILE:(ac + 1) * TTILE],
                                    start=(ac == 0), stop=(ac == AC - 1))
                return last
            p.op("tensor", womm, reads=[n_wring[wb], n_ogf], writes=[nP[pb]])
            p.op("vector", lambda e, ck=ck, pb=pb: e.tensor_copy(out=tmp2[pb][:], in_=psP[pb][:, :TTILE]),
                 reads=[nP[pb]], writes=[n_tmp2[pb]])
            stores.append(p.dma("sync", ypo[ck * 128:(ck + 1) * 128, t0:t0 + TTILE],
                                tmp2[pb][:], reads=[n_tmp2[pb]], store_node=n_store))

    return stores


def build_l1_attn(ntt=NTT):
    nc = bass.Bass("TRN2", target_bir_lowering=False, debug=False,
                   detect_race_conditions=bool(os.environ.get("RACEDET")))
    x2 = nc.dram_tensor("x2", [C, RT], F32, kind="ExternalInput").ap()
    maa = {k: nc.dram_tensor(f"maa_{k}", [128, 16], F32, kind="ExternalInput").ap()
           for k in ("x", "w", "k", "v", "r", "g")}
    w1 = nc.dram_tensor("w1", [C, 160], F32, kind="ExternalInput").ap()
    w2 = nc.dram_tensor("w2", [5, 32, C], F32, kind="ExternalInput").ap()
    dw1 = nc.dram_tensor("dw1", [C, 64], F32, kind="ExternalInput").ap()
    dw2 = nc.dram_tensor("dw2", [64, 1024], F32, kind="ExternalInput").ap()
    tdcol = nc.dram_tensor("tdcol", [128, 8], F32, kind="ExternalInput").ap()
    WrT = nc.dram_tensor("WrT", [C, 1024], F32, kind="ExternalInput").ap()
    WkT = nc.dram_tensor("WkT", [C, 1024], F32, kind="ExternalInput").ap()
    WvT = nc.dram_tensor("WvT", [C, 1024], F32, kind="ExternalInput").ap()
    WgT = nc.dram_tensor("WgT", [C, 1024], F32, kind="ExternalInput").ap()
    WoT = nc.dram_tensor("WoT", [1024, C], F32, kind="ExternalInput").ap()
    ucol = nc.dram_tensor("ucol", [128, 8], F32, kind="ExternalInput").ap()
    gamr = nc.dram_tensor("gamr", [1, 1024], F32, kind="ExternalInput").ap()
    betr = nc.dram_tensor("betr", [1, 1024], F32, kind="ExternalInput").ap()
    ypo = nc.dram_tensor("ypart", [C, RT], F32, kind="ExternalOutput").ap()

    x2ap = lambda ck, a, b: x2[ck * 128:(ck + 1) * 128, a:b]
    with ExitStack() as st:
        p = Prog(nc, st)
        stores = emit_l1_attn(p, nc, st, x2ap, maa, w1, w2, dw1, dw2, tdcol, WrT, WkT, WvT, WgT, WoT, ucol, gamr, betr, ypo, ntt)
        p.fence("sync", stores)
        p.finish()
    return nc

def run_l1_attn(inputs, x2_parts):
    nc = build_l1_attn()
    _ = (inputs["L1_attn_time_maa_x"], inputs["L1_attn_time_maa_w"],
         inputs["L1_attn_time_maa_k"], inputs["L1_attn_time_maa_v"],
         inputs["L1_attn_time_maa_r"], inputs["L1_attn_time_maa_g"])
    g = lambda k: np.asarray(inputs["L1_attn_" + k], np.float32)
    in_maps = []
    for c in range(8):
        b, hh = c // 2, c % 2
        asl = slice(1024 * hh, 1024 * hh + 1024)
        x2f = np.ascontiguousarray(np.concatenate([x2_parts[2 * b], x2_parts[2 * b + 1]], axis=1))
        im = dict(
            x2=x2f,
            w1=np.ascontiguousarray(g("time_maa_w1")),
            w2=np.ascontiguousarray(g("time_maa_w2")),
            dw1=np.ascontiguousarray(g("time_decay_w1")),
            dw2=np.ascontiguousarray(g("time_decay_w2")[:, asl]),
            tdcol=np.ascontiguousarray(g("time_decay")[0, 0, asl].reshape(8, 128).T),
            WrT=np.ascontiguousarray(g("W_r").T[:, asl]),
            WkT=np.ascontiguousarray(g("W_k").T[:, asl]),
            WvT=np.ascontiguousarray(g("W_v").T[:, asl]),
            WgT=np.ascontiguousarray(g("W_g").T[:, asl]),
            WoT=np.ascontiguousarray(g("W_o").T[asl, :]),
            ucol=np.ascontiguousarray(g("time_faaaa")[16 * hh:16 * hh + 16].reshape(8, 128).T),
            gamr=np.ascontiguousarray(g("ln_gamma")[asl].reshape(1, 1024)),
            betr=np.ascontiguousarray(g("ln_beta")[asl].reshape(1, 1024)),
        )
        for k in ("x", "w", "k", "v", "r", "g"):
            im[f"maa_{k}"] = np.ascontiguousarray(g("time_maa_" + k)[0, 0].reshape(16, 128).T)
        in_maps.append(im)
    res = run_bass_kernel_spmd(nc, in_maps, core_ids=list(range(8)))
    return [r["ypart"] for r in res.results]



def emit_combine(p, nc, st, x2, ypA, ypB, x3o):
    sb = lambda name, shape, dt=F32: st.enter_context(nc.sbuf_tensor(uname(name), shape, dt))
    psal = lambda name, shape: st.enter_context(nc.psum_tensor(uname(name), shape, F32))
    x3s = sb("x3s", [128, 16 * 512], F32)
    ta = [sb(f"ta{i}", [128, 512]) for i in range(2)]
    tb = [sb(f"tb{i}", [128, 512]) for i in range(2)]
    sqf = [sb(f"sqf{i}", [128, 512]) for i in range(2)]
    onescol = sb("onescol", [128, 1], F32)
    onescf = sb("onescf", [1, 128], F32)
    rowbuf = sb("rowbuf", [1, 512], F32)
    rowbuf2 = sb("rowbuf2", [1, 512], F32)
    rbsb = sb("rbsb", [128, 512], F32)
    ps_ss = psal("ps_ss", [1, 512])
    ps_ss2 = psal("ps_ss2", [1, 512])
    ps_b = psal("ps_b", [128, 512])
    ps_b2 = psal("ps_b2", [128, 512])
    N = Node
    n_x3s, n_ones, n_onescf = N("x3s"), N("ones"), N("onescf")
    n_ta = nodes("ta", 2)
    n_tb = nodes("tb", 2)
    n_sqf = nodes("sqf", 2)
    n_row, n_row2, n_rbsb = N("row"), N("row2"), N("rbsb")
    n_pss, n_pss2, n_psb, n_psb2 = N("pss"), N("pss2"), N("psb"), N("psb2")
    n_store = N("store")
    stores = []
    p.op("vector", lambda e: e.memset(onescol[:], 1.0), writes=[n_ones])
    p.op("vector", lambda e: e.memset(onescf[:], 1.0), writes=[n_onescf])
    for tg in range(4):
        t0 = tg * 512
        for ck in range(CK):
            b = ck % 2
            p.dma("gpsimd", ta[b][:], x2[ck * 128:(ck + 1) * 128, t0:t0 + 512], writes=[n_ta[b]])
            p.dma("sync", tb[b][:], ypA[ck * 128:(ck + 1) * 128, t0:t0 + 512], writes=[n_tb[b]])
            p.op("vector", lambda e, ck=ck, b=b: e.tensor_add(
                x3s[:, ck * 512:(ck + 1) * 512], ta[b][:], tb[b][:]),
                reads=[n_ta[b], n_tb[b]], writes=[n_x3s])
            p.dma("sync", tb[b][:], ypB[ck * 128:(ck + 1) * 128, t0:t0 + 512], writes=[n_tb[b]])
            p.op("vector", lambda e, ck=ck, b=b: e.tensor_add(
                x3s[:, ck * 512:(ck + 1) * 512], x3s[:, ck * 512:(ck + 1) * 512], tb[b][:]),
                reads=[n_tb[b]], writes=[n_x3s])
        for ck in range(CK):
            sq = ck % 2
            p.op("tensor", lambda e, ck=ck: e.matmul(
                ps_ss[:], onescol[:], x3s[:, ck * 512:(ck + 1) * 512],
                start=(ck == 0), stop=(ck == CK - 1)),
                reads=[n_ones, n_x3s], writes=[n_pss])
            p.op("scalar", lambda e, ck=ck, sq=sq: e.activation(
                out=sqf[sq][:], in_=x3s[:, ck * 512:(ck + 1) * 512], func=AF.Square),
                reads=[n_x3s], writes=[n_sqf[sq]])
            p.op("tensor", lambda e, ck=ck, sq=sq: e.matmul(
                ps_ss2[:], onescol[:], sqf[sq][:],
                start=(ck == 0), stop=(ck == CK - 1)),
                reads=[n_ones, n_sqf[sq]], writes=[n_pss2])
        p.op("vector", lambda e: e.tensor_scalar(
            out=rowbuf[:], in0=ps_ss[:], scalar1=1.0 / 2048.0, scalar2=None,
            op0=ALU.mult, op1=ALU.bypass), reads=[n_pss], writes=[n_row])
        p.op("vector", lambda e: e.tensor_mul(rowbuf2[:], rowbuf[:], rowbuf[:]),
             reads=[n_row], writes=[n_row2])
        p.op("vector", lambda e: e.tensor_scalar(
            out=rowbuf2[:], in0=rowbuf2[:], scalar1=1e-5, scalar2=None,
            op0=ALU.subtract, op1=ALU.bypass), reads=[n_row2], writes=[n_row2])
        p.op("vector", lambda e: e.scalar_tensor_tensor(
            out=rowbuf2[:], in0=ps_ss2[:], scalar=1.0 / 2048.0, in1=rowbuf2[:],
            op0=ALU.mult, op1=ALU.subtract), reads=[n_pss2, n_row2], writes=[n_row2])
        p.op("scalar", lambda e: e.activation(out=rowbuf2[:], in_=rowbuf2[:], func=AF.Ln),
             reads=[n_row2], writes=[n_row2])
        p.op("scalar", lambda e: e.activation(out=rowbuf2[:], in_=rowbuf2[:], func=AF.Exp,
                                              scale=-0.5), reads=[n_row2], writes=[n_row2])
        p.op("tensor", lambda e: e.matmul(ps_b[:], onescf[:], rowbuf[:], start=True, stop=True),
             reads=[n_onescf, n_row], writes=[n_psb])
        p.op("tensor", lambda e: e.matmul(ps_b2[:], onescf[:], rowbuf2[:], start=True, stop=True),
             reads=[n_onescf, n_row2], writes=[n_psb2])
        p.op("vector", lambda e: e.tensor_copy(out=rbsb[:], in_=ps_b2[:]),
             reads=[n_psb2], writes=[n_rbsb])
        for ck in range(CK):
            p.op("vector", lambda e, ck=ck: e.tensor_sub(
                x3s[:, ck * 512:(ck + 1) * 512], x3s[:, ck * 512:(ck + 1) * 512], ps_b[:]),
                reads=[n_psb], writes=[n_x3s])
            p.op("vector", lambda e, ck=ck: e.tensor_mul(
                x3s[:, ck * 512:(ck + 1) * 512], x3s[:, ck * 512:(ck + 1) * 512], rbsb[:]),
                reads=[n_rbsb], writes=[n_x3s])
            stores.append(p.dma(
                "sync", x3o[ck * 128:(ck + 1) * 128, t0:t0 + 512],
                x3s[:, ck * 512:(ck + 1) * 512], reads=[n_x3s], store_node=n_store))
    return stores


def build_combine():
    nc = bass.Bass("TRN2", target_bir_lowering=False, debug=False,
                   detect_race_conditions=bool(os.environ.get("RACEDET")))
    x2 = nc.dram_tensor("x2", [C, TOWN], F32, kind="ExternalInput").ap()
    ypA = nc.dram_tensor("ypA", [C, TOWN], F32, kind="ExternalInput").ap()
    ypB = nc.dram_tensor("ypB", [C, TOWN], F32, kind="ExternalInput").ap()
    x3o = nc.dram_tensor("x3", [C, TOWN], F32, kind="ExternalOutput").ap()

    with ExitStack() as st:
        p = Prog(nc, st)
        stores = emit_combine(p, nc, st, x2, ypA, ypB, x3o)
        p.fence("sync", stores)
        p.finish()
    return nc

def run_combine(x2_parts, yparts):
    nc = build_combine()
    in_maps = []
    for c in range(8):
        b, half = c // 2, c % 2
        sl = slice(half * 2048, (half + 1) * 2048)
        in_maps.append(dict(
            x2=np.ascontiguousarray(x2_parts[c]),
            ypA=np.ascontiguousarray(yparts[2 * b][:, sl]),
            ypB=np.ascontiguousarray(yparts[2 * b + 1][:, sl]),
        ))
    res = run_bass_kernel_spmd(nc, in_maps, core_ids=list(range(8)))
    return [r["x3"] for r in res.results]



def emit_combine2(p, nc, st, x2, ysum, mA, mB, x3o):
    sb = lambda name, shape, dt=F32: st.enter_context(nc.sbuf_tensor(uname(name), shape, dt))
    psal = lambda name, shape: st.enter_context(nc.psum_tensor(uname(name), shape, F32))
    x3s = sb("c2_x3s", [128, 16 * 512], F32)
    ta = [sb(f"c2_ta{i}", [128, 512]) for i in range(2)]
    tb = [sb(f"c2_tb{i}", [128, 512]) for i in range(2)]
    sqf = [sb(f"c2_sqf{i}", [128, 512]) for i in range(2)]
    masks = sb("c2_masks", [128, 2], F32)
    onescol = sb("c2_onescol", [128, 1], F32)
    onescf = sb("c2_onescf", [1, 128], F32)
    rowbuf = sb("c2_rowbuf", [1, 512], F32)
    rowbuf2 = sb("c2_rowbuf2", [1, 512], F32)
    rbsb = sb("c2_rbsb", [128, 512], F32)
    ps_ss = psal("c2_ps_ss", [1, 512])
    ps_ss2 = psal("c2_ps_ss2", [1, 512])
    ps_b = psal("c2_ps_b", [128, 512])
    ps_b2 = psal("c2_ps_b2", [128, 512])
    N = Node
    n_x3s, n_ones, n_onescf, n_masks = N("x3s"), N("ones"), N("onescf"), N("masks")
    n_ta = nodes("c2ta", 2)
    n_tb = nodes("c2tb", 2)
    n_sqf = nodes("c2sqf", 2)
    n_row, n_row2, n_rbsb = N("row"), N("row2"), N("rbsb")
    n_pss, n_pss2, n_psb, n_psb2 = N("pss"), N("pss2"), N("psb"), N("psb2")
    n_store = N("store")
    stores = []
    p.op("vector", lambda e: e.memset(onescol[:], 1.0), writes=[n_ones])
    p.op("vector", lambda e: e.memset(onescf[:], 1.0), writes=[n_onescf])
    p.dma("gpsimd", masks[:, 0:1], mA, writes=[n_masks])
    p.dma("gpsimd", masks[:, 1:2], mB, writes=[n_masks])
    for tg in range(4):
        t0 = tg * 512
        for ck in range(CK):
            b = ck % 2
            p.dma("gpsimd", ta[b][:], x2[ck * 128:(ck + 1) * 128, t0:t0 + 512], writes=[n_ta[b]])
            p.dma("sync", tb[b][:], ysum[ck * 128:(ck + 1) * 128, t0:t0 + 512], writes=[n_tb[b]])
            p.op("vector", lambda e, ck=ck, b=b: e.scalar_tensor_tensor(
                out=x3s[:, ck * 512:(ck + 1) * 512], in0=tb[b][:], scalar=masks[:, 0:1],
                in1=ta[b][:], op0=ALU.mult, op1=ALU.add),
                reads=[n_ta[b], n_tb[b], n_masks], writes=[n_x3s])
            p.dma("sync", tb[b][:], ysum[ck * 128:(ck + 1) * 128, 2048 + t0:2048 + t0 + 512],
                  writes=[n_tb[b]])
            p.op("vector", lambda e, ck=ck, b=b: e.scalar_tensor_tensor(
                out=x3s[:, ck * 512:(ck + 1) * 512], in0=tb[b][:], scalar=masks[:, 1:2],
                in1=x3s[:, ck * 512:(ck + 1) * 512], op0=ALU.mult, op1=ALU.add),
                reads=[n_tb[b], n_masks], writes=[n_x3s])
        for ck in range(CK):
            sq = ck % 2
            p.op("tensor", lambda e, ck=ck: e.matmul(
                ps_ss[:], onescol[:], x3s[:, ck * 512:(ck + 1) * 512],
                start=(ck == 0), stop=(ck == CK - 1)),
                reads=[n_ones, n_x3s], writes=[n_pss])
            p.op("scalar", lambda e, ck=ck, sq=sq: e.activation(
                out=sqf[sq][:], in_=x3s[:, ck * 512:(ck + 1) * 512], func=AF.Square),
                reads=[n_x3s], writes=[n_sqf[sq]])
            p.op("tensor", lambda e, ck=ck, sq=sq: e.matmul(
                ps_ss2[:], onescol[:], sqf[sq][:],
                start=(ck == 0), stop=(ck == CK - 1)),
                reads=[n_ones, n_sqf[sq]], writes=[n_pss2])
        p.op("vector", lambda e: e.tensor_scalar(
            out=rowbuf[:], in0=ps_ss[:], scalar1=1.0 / 2048.0, scalar2=None,
            op0=ALU.mult, op1=ALU.bypass), reads=[n_pss], writes=[n_row])
        p.op("vector", lambda e: e.tensor_mul(rowbuf2[:], rowbuf[:], rowbuf[:]),
             reads=[n_row], writes=[n_row2])
        p.op("vector", lambda e: e.tensor_scalar(
            out=rowbuf2[:], in0=rowbuf2[:], scalar1=1e-5, scalar2=None,
            op0=ALU.subtract, op1=ALU.bypass), reads=[n_row2], writes=[n_row2])
        p.op("vector", lambda e: e.scalar_tensor_tensor(
            out=rowbuf2[:], in0=ps_ss2[:], scalar=1.0 / 2048.0, in1=rowbuf2[:],
            op0=ALU.mult, op1=ALU.subtract), reads=[n_pss2, n_row2], writes=[n_row2])
        p.op("scalar", lambda e: e.activation(out=rowbuf2[:], in_=rowbuf2[:], func=AF.Ln),
             reads=[n_row2], writes=[n_row2])
        p.op("scalar", lambda e: e.activation(out=rowbuf2[:], in_=rowbuf2[:], func=AF.Exp,
                                              scale=-0.5), reads=[n_row2], writes=[n_row2])
        p.op("tensor", lambda e: e.matmul(ps_b[:], onescf[:], rowbuf[:], start=True, stop=True),
             reads=[n_onescf, n_row], writes=[n_psb])
        p.op("tensor", lambda e: e.matmul(ps_b2[:], onescf[:], rowbuf2[:], start=True, stop=True),
             reads=[n_onescf, n_row2], writes=[n_psb2])
        p.op("vector", lambda e: e.tensor_copy(out=rbsb[:], in_=ps_b2[:]),
             reads=[n_psb2], writes=[n_rbsb])
        for ck in range(CK):
            p.op("vector", lambda e, ck=ck: e.tensor_sub(
                x3s[:, ck * 512:(ck + 1) * 512], x3s[:, ck * 512:(ck + 1) * 512], ps_b[:]),
                reads=[n_psb], writes=[n_x3s])
            p.op("vector", lambda e, ck=ck: e.tensor_mul(
                x3s[:, ck * 512:(ck + 1) * 512], x3s[:, ck * 512:(ck + 1) * 512], rbsb[:]),
                reads=[n_rbsb], writes=[n_x3s])
            stores.append(p.dma(
                "sync", x3o[ck * 128:(ck + 1) * 128, t0:t0 + 512],
                x3s[:, ck * 512:(ck + 1) * 512], reads=[n_x3s], store_node=n_store))
    return stores


def build_mega():
    nc = bass.Bass("TRN2", target_bir_lowering=False, debug=False,
                   detect_race_conditions=bool(os.environ.get("RACEDET")))
    xctx = nc.dram_tensor("xctx", [C, TCTX], F32, kind="ExternalInput").ap()
    maskv = nc.dram_tensor("maskv", [128, 32], F32, kind="ExternalInput").ap()
    WqT = nc.dram_tensor("WqT", [C, 2048], F32, kind="ExternalInput").ap()
    WkT0 = nc.dram_tensor("WkT0", [C, 512], F32, kind="ExternalInput").ap()
    WvT0 = nc.dram_tensor("WvT0", [C, 512], F32, kind="ExternalInput").ap()
    WoT0 = nc.dram_tensor("WoT0", [2048, C], F32, kind="ExternalInput").ap()
    qb = nc.dram_tensor("qb", [128, 16], F32, kind="ExternalInput").ap()
    kb = nc.dram_tensor("kb", [128, 4], F32, kind="ExternalInput").ap()
    vbr = nc.dram_tensor("vbr", [1, 512], F32, kind="ExternalInput").ap()
    lnw = nc.dram_tensor("lnw", [128, 16], F32, kind="ExternalInput").ap()
    gup_p = nc.dram_tensor("gup_packed", [GUP_SH, C // 8], I32, kind="ExternalInput").ap()
    gup_s = nc.dram_tensor("gup_scales", [GUP_SH, 16], F16, kind="ExternalInput").ap()
    dwn_p = nc.dram_tensor("down_packed", [DWN_SH, 5632 // 8], I32, kind="ExternalInput").ap()
    dwn_s = nc.dram_tensor("down_scales", [DWN_SH, 44], F16, kind="ExternalInput").ap()
    gidx = nc.dram_tensor("gate_idx", [G_SH2, C], I32, kind="ExternalInput").ap()
    uidx = nc.dram_tensor("up_idx", [G_SH2, C], I32, kind="ExternalInput").ap()
    didx = nc.dram_tensor("down_idx", [GD_SH, 5504], I32, kind="ExternalInput").ap()
    lut = nc.dram_tensor("lut", [1, 16], F32, kind="ExternalInput").ap()
    maa = {k: nc.dram_tensor(f"maa_{k}", [128, 16], F32, kind="ExternalInput").ap()
           for k in ("x", "w", "k", "v", "r", "g")}
    w1 = nc.dram_tensor("w1", [C, 160], F32, kind="ExternalInput").ap()
    w2 = nc.dram_tensor("w2", [5, 32, C], F32, kind="ExternalInput").ap()
    dw1 = nc.dram_tensor("dw1", [C, 64], F32, kind="ExternalInput").ap()
    dw2 = nc.dram_tensor("dw2", [64, 1024], F32, kind="ExternalInput").ap()
    tdcol = nc.dram_tensor("tdcol", [128, 8], F32, kind="ExternalInput").ap()
    WrT = nc.dram_tensor("WrT", [C, 1024], F32, kind="ExternalInput").ap()
    WkT1 = nc.dram_tensor("WkT1", [C, 1024], F32, kind="ExternalInput").ap()
    WvT1 = nc.dram_tensor("WvT1", [C, 1024], F32, kind="ExternalInput").ap()
    WgT1 = nc.dram_tensor("WgT1", [C, 1024], F32, kind="ExternalInput").ap()
    WoT1 = nc.dram_tensor("WoT1", [1024, C], F32, kind="ExternalInput").ap()
    ucol = nc.dram_tensor("ucol", [128, 8], F32, kind="ExternalInput").ap()
    gamr = nc.dram_tensor("gamr", [1, 1024], F32, kind="ExternalInput").ap()
    betr = nc.dram_tensor("betr", [1, 1024], F32, kind="ExternalInput").ap()
    mA = nc.dram_tensor("mA", [128, 1], F32, kind="ExternalInput").ap()
    mB = nc.dram_tensor("mB", [128, 1], F32, kind="ExternalInput").ap()
    x4o = nc.dram_tensor("x4", [C, TOWN], F32, kind="ExternalOutput").ap()

    sh_gu = nc.dram_tensor("sh_gu", [C, GUP_SH], BF16, kind="Internal")
    sh_dn = nc.dram_tensor("sh_dn", [5632, DWN_SH], BF16, kind="Internal")
    sh_g = nc.dram_tensor("sh_g", [C, G_SH2], BF16, kind="Internal")
    sh_u = nc.dram_tensor("sh_u", [C, G_SH2], BF16, kind="Internal")
    sh_d = nc.dram_tensor("sh_d", [5504, GD_SH], BF16, kind="Internal")
    g_gu = nc.dram_tensor("g_gu", [8 * C, GUP_SH], BF16, kind="Internal")
    g_dn = nc.dram_tensor("g_dn", [8 * 5632, DWN_SH], BF16, kind="Internal")
    g_g = nc.dram_tensor("g_g", [8 * C, G_SH2], BF16, kind="Internal")
    g_u = nc.dram_tensor("g_u", [8 * C, G_SH2], BF16, kind="Internal")
    g_d = nc.dram_tensor("g_d", [8 * 5504, GD_SH], BF16, kind="Internal")
    st1 = nc.dram_tensor("st1", [C, TOWN], F32, kind="Internal")
    st2 = nc.dram_tensor("st2", [C, TOWN], F32, kind="Internal")
    x2g = nc.dram_tensor("x2g", [2 * C, TOWN], F32, kind="Internal")
    ypart = nc.dram_tensor("ypart", [C, RT], F32, kind="Internal")
    ysum = nc.dram_tensor("ysum", [C, RT], F32, kind="Internal")
    st3 = nc.dram_tensor("st3", [C, TOWN], F32, kind="Internal")

    PAIRS = [[0, 1], [2, 3], [4, 5], [6, 7]]
    ALL8 = [list(range(8))]

    with ExitStack() as stp:
        p = Prog(nc, stp)
        n_cc = Node("ccchain")

        def coll(kind, op, groups, inap, outap, rd_nodes, wr_node):
            p.op("gpsimd", lambda e: e.collective_compute(
                kind, op, replica_groups=groups, ins=[inap.ap().opt()],
                outs=[outap.ap().opt()]),
                reads=list(rd_nodes) + [n_cc], writes=[wr_node, n_cc], sigk=1)

        n_shards = Node("shards")
        n_gath = Node("gathered")
        n_st1, n_st2, n_x2g, n_yp, n_ys, n_st3 = (
            Node("st1"), Node("st2"), Node("x2g"), Node("yp"), Node("ys"), Node("st3"))

        with ExitStack() as st:
            stores = emit_dequant(p, nc, st, gup_p, gup_s, dwn_p, dwn_s, gidx, uidx,
                                  didx, lut, sh_gu.ap(), sh_dn.ap(), sh_g.ap(),
                                  sh_u.ap(), sh_d.ap())
            for ev in stores:
                n_shards.writers[ev[0]] = max(n_shards.writers.get(ev[0], 0), ev[1])
            p.barrier_all()

        def gath_hook():
            coll("AllGather", ALU.bypass, ALL8, sh_gu, g_gu, [n_shards], n_gath)
            coll("AllGather", ALU.bypass, ALL8, sh_dn, g_dn, [n_shards], n_gath)
            coll("AllGather", ALU.bypass, ALL8, sh_g, g_g, [n_shards], n_gath)
            coll("AllGather", ALU.bypass, ALL8, sh_u, g_u, [n_shards], n_gath)
            coll("AllGather", ALU.bypass, ALL8, sh_d, g_d, [n_shards], n_gath)
        with ExitStack() as st:
            stores = emit_l0_attn(p, nc, st, xctx, WqT, WkT0, WvT0, WoT0, qb, kb, vbr,
                                  lnw, maskv, st1.ap(), 0, 4, hook=gath_hook)
            for ev in stores:
                n_st1.writers[ev[0]] = max(n_st1.writers.get(ev[0], 0), ev[1])
            p.barrier_all()
        p.fence("sync", list(n_gath.writers.items()))
        with ExitStack() as st:
            stores = emit_ffn(p, nc, st, st1.ap(), g_gu.ap()[0:4 * 2048, :],
                              g_gu.ap()[4 * 2048:8 * 2048, :], g_dn.ap(),
                              st2.ap(), 5632, True, 1408)
            for ev in stores:
                n_st2.writers[ev[0]] = max(n_st2.writers.get(ev[0], 0), ev[1])
            p.barrier_all()
        coll("AllGather", ALU.bypass, PAIRS, st2, x2g, [n_st2], n_x2g)
        p.fence("gpsimd", list(n_x2g.writers.items()))
        with ExitStack() as st:
            x2ap = lambda ck, a, b: (
                x2g.ap()[ck * 128:(ck + 1) * 128, a:b] if b <= 2048 else
                x2g.ap()[2048 + ck * 128: 2048 + (ck + 1) * 128, a - 2048:b - 2048])
            stores = emit_l1_attn(p, nc, st, x2ap, maa, w1, w2, dw1, dw2, tdcol,
                                  WrT, WkT1, WvT1, WgT1, WoT1, ucol, gamr, betr,
                                  ypart.ap(), NTT)
            for ev in stores:
                n_yp.writers[ev[0]] = max(n_yp.writers.get(ev[0], 0), ev[1])
            coll("AllReduce", ALU.add, PAIRS, ypart, ysum, [n_yp], n_ys)
            p.barrier_all()
        with ExitStack() as st:
            stores = emit_combine2(p, nc, st, st2.ap(), ysum.ap(), mA, mB, st3.ap())
            for ev in stores:
                n_st3.writers[ev[0]] = max(n_st3.writers.get(ev[0], 0), ev[1])
            p.barrier_all()
        with ExitStack() as st:
            stores = emit_ffn(p, nc, st, st3.ap(), g_g.ap(), g_u.ap(), g_d.ap(),
                              x4o, 5504, False, 768)
            p.fence("sync", stores)
        p.finish()
    return nc


def mega_in_maps(inputs):
    x_parts = split_x_parts(inputs["L0_attn_x"])
    dq_maps = dequant_in_maps(inputs)
    qbh = np.ascontiguousarray(np.asarray(inputs["L0_attn_q_bias"]).reshape(16, 128).T)
    kbh = np.ascontiguousarray(np.asarray(inputs["L0_attn_k_bias"]).reshape(4, 128).T)
    vbh = np.ascontiguousarray(np.asarray(inputs["L0_attn_v_bias"]).reshape(1, 512))
    lnwh = np.ascontiguousarray(np.asarray(inputs["L0_attn_ln_weight"]).reshape(16, 128).T)
    WqTh = np.ascontiguousarray(np.asarray(inputs["L0_attn_q_project"]).T)
    WkTh = np.ascontiguousarray(np.asarray(inputs["L0_attn_k_project"]).T)
    WvTh = np.ascontiguousarray(np.asarray(inputs["L0_attn_v_project"]).T)
    WoTh = np.ascontiguousarray(np.asarray(inputs["L0_attn_o_project"]).T)
    mask_pad = np.zeros((4096,), np.float32)
    mask_pad[:2048] = -1e30
    mask_pad = np.ascontiguousarray(mask_pad.reshape(32, 128).T)
    mask_zero = np.zeros((128, 32), np.float32)
    g = lambda k: np.asarray(inputs["L1_attn_" + k], np.float32)
    maps = []
    for c in range(8):
        b, half = c // 2, c % 2
        if half == 0:
            ctxc = np.concatenate([np.zeros_like(x_parts[c]), x_parts[c]], axis=1)
            mv = mask_pad
            mAv, mBv = 1.0, 0.0
        else:
            ctxc = np.concatenate([x_parts[c - 1], x_parts[c]], axis=1)
            mv = mask_zero
            mAv, mBv = 0.0, 1.0
        asl = slice(1024 * half, 1024 * half + 1024)
        im = dict(dq_maps[c])
        im.update(
            xctx=np.ascontiguousarray(ctxc), maskv=mv,
            WqT=WqTh, WkT0=WkTh, WvT0=WvTh, WoT0=WoTh,
            qb=qbh, kb=kbh, vbr=vbh, lnw=lnwh,
            w1=np.ascontiguousarray(g("time_maa_w1")),
            w2=np.ascontiguousarray(g("time_maa_w2")),
            dw1=np.ascontiguousarray(g("time_decay_w1")),
            dw2=np.ascontiguousarray(g("time_decay_w2")[:, asl]),
            tdcol=np.ascontiguousarray(g("time_decay")[0, 0, asl].reshape(8, 128).T),
            WrT=np.ascontiguousarray(g("W_r").T[:, asl]),
            WkT1=np.ascontiguousarray(g("W_k").T[:, asl]),
            WvT1=np.ascontiguousarray(g("W_v").T[:, asl]),
            WgT1=np.ascontiguousarray(g("W_g").T[:, asl]),
            WoT1=np.ascontiguousarray(g("W_o").T[asl, :]),
            ucol=np.ascontiguousarray(g("time_faaaa")[16 * half:16 * half + 16].reshape(8, 128).T),
            gamr=np.ascontiguousarray(g("ln_gamma")[asl].reshape(1, 1024)),
            betr=np.ascontiguousarray(g("ln_beta")[asl].reshape(1, 1024)),
            mA=np.full((128, 1), mAv, np.float32),
            mB=np.full((128, 1), mBv, np.float32),
        )
        for k in ("x", "w", "k", "v", "r", "g"):
            im[f"maa_{k}"] = np.ascontiguousarray(g("time_maa_" + k)[0, 0].reshape(16, 128).T)
        maps.append(im)
    return maps


def kernel(**inputs):
    nc = build_mega()
    res = run_bass_kernel_spmd(nc, mega_in_maps(inputs), core_ids=list(range(8)))
    out = np.empty((4, 4096, 2048), np.float32)
    for c in range(8):
        b, half = c // 2, c % 2
        out[b, half * 2048:(half + 1) * 2048, :] = res.results[c]["x4"].T
    return out


def kernel_multi(**inputs):
    dq = run_dequant(inputs)
    x_parts = split_x_parts(inputs["L0_attn_x"])
    x1_parts = run_l0_attn(inputs, x_parts)
    gug = dq["w_guT_g"]
    x2_parts = run_ffn(x1_parts,
                       np.ascontiguousarray(gug[:4 * 2048]),
                       np.ascontiguousarray(gug[4 * 2048:]),
                       dq["w_downT_g"], 5632, True, 1408)
    yparts = run_l1_attn(inputs, x2_parts)
    x3_parts = run_combine(x2_parts, yparts)
    x4_parts = run_ffn(x3_parts,
                       dq["w_gateT_g"], dq["w_upT_g"], dq["w_downgT_g"],
                       5504, False, 768)
    out = np.empty((4, 4096, 2048), np.float32)
    for c in range(8):
        b, half = c // 2, c % 2
        out[b, half * 2048:(half + 1) * 2048, :] = x4_parts[c].T
    return out
